# version 28
# baseline (speedup 1.0000x reference)
import numpy as np
import concourse.bass as bass
import concourse.mybir as mybir
from concourse.bass_utils import run_bass_kernel_spmd
from contextlib import ExitStack

F32 = mybir.dt.float32
BF16 = mybir.dt.bfloat16
AT = mybir.AluOpType
AF = mybir.ActivationFunctionType
AX = mybir.AxisListType

C = 1024
TQ = 4096
HL = 8
NP = 4
SEG = 512
NSEG = 8
OWN = 2048
EPS = 1e-5

SA = 0
SH = 16384
SB = 32768
SC = 48768
FARENA = 50816
O_WQ, O_WK, O_WV, O_TSC = SH, SH + 1024, SH + 2048, SH + 3072
O_TJ = SH + 11264
O_Q, O_K, O_V = SB, SB + 4096, SB + 8192
O_PB, O_VT, O_XT = SB + 12416, SB + 14464, SB + 14976
O_PT = SC
O_WPJ = SB + 1024
O_LD, O_XQ = SB + 8192, SB + 12288
O_KEY, O_VAL, O_HT = SB, SB + 4096, SB + 8192
O_GES, O_G, O_SQ = SB + 10240, SB + 10752, SB + 12800
O_OH = SB + 12800
ST_M, ST_V, ST_R, ST_T, ST_B = (SA + 0, SA + 2048, SA + 4096, SA + 6144,
                                SA + 8192)


class Rec:
    ENGS = ("sync", "tensor", "vector", "scalar", "gpsimd")

    def __init__(self, nc, ctx):
        self.nc = nc
        self.ops = {e: [] for e in self.ENGS}
        self.sems = {}
        self.cnt = {}
        self.lastwait = {}
        for name in ("LD", "PF", "TT", "VV", "SS", "GG", "SO",
                     "S0", "S1", "S2", "S3", "S4", "S5"):
            self.sems[name] = ctx.enter_context(nc.semaphore(name))
            self.cnt[name] = 0

    def emit(self, eng, fn, inc=None, by=1):
        if inc is None:
            self.ops[eng].append(fn)
            return None
        self.cnt[inc] += by
        sem = self.sems[inc]
        self.ops[eng].append(lambda e, f=fn, s=sem, b=by: f(e).then_inc(s, b))
        return self.cnt[inc]

    def wait(self, eng, sname, val):
        if not val:
            return
        key = (eng, sname)
        if self.lastwait.get(key, -1) >= val:
            return
        self.lastwait[key] = val
        sem = self.sems[sname]
        self.ops[eng].append(lambda e, s=sem, v=val: e.wait_ge(s, v))

    def barrier(self):
        for sname in ("LD", "TT", "VV", "SS", "GG", "SO"):
            v = self.cnt[sname]
            for eng in self.ENGS:
                self.wait(eng, sname, v)


def build_nc(debug=False):
    nc = bass.Bass(trn_type="TRN2", target_bir_lowering=False)

    d_xT = nc.dram_tensor("xT", [C, TQ], F32, kind="ExternalInput").ap()
    d_xq = nc.dram_tensor("xq", [C, OWN], F32, kind="ExternalInput").ap()
    d_wqkv = nc.dram_tensor("wqkv", [C, 1536], F32, kind="ExternalInput").ap()
    d_wproj = nc.dram_tensor("wprojmy", [512, C], F32, kind="ExternalInput").ap()
    d_bcols = nc.dram_tensor("bcols", [128, 20], F32, kind="ExternalInput").ap()
    d_nslope = nc.dram_tensor("negslope", [128, 8], F32, kind="ExternalInput").ap()
    d_bbase = nc.dram_tensor("biasbase", [128, 8], F32, kind="ExternalInput").ap()
    d_selT = nc.dram_tensor("selT", [C, 16], F32, kind="ExternalInput").ap()
    d_keys = nc.dram_tensor("keysR", [16 * C, 128], F32, kind="ExternalInput").ap()
    d_vals = nc.dram_tensor("valuesR", [2048, C], F32, kind="ExternalInput").ap()
    d_tj = nc.dram_tensor("tj", [128, TQ], F32, kind="ExternalInput").ap()
    d_ident = nc.dram_tensor("identc", [128, 128], F32, kind="ExternalInput").ap()
    d_ones = nc.dram_tensor("onesc", [128, 128], F32, kind="ExternalInput").ap()
    d_bones = nc.dram_tensor("blockonesc", [128, 128], F32, kind="ExternalInput").ap()
    d_oh = nc.dram_tensor("oh16", [16, 2048], F32, kind="ExternalInput").ap()
    d_out = nc.dram_tensor("out", [OWN, C], F32, kind="ExternalOutput").ap()
    if debug:
        g_attn = nc.dram_tensor("dbg_attn", [512, TQ], F32, kind="ExternalOutput").ap()
        g_h1 = nc.dram_tensor("dbg_h1", [C, OWN], F32, kind="ExternalOutput").ap()
        g_x2 = nc.dram_tensor("dbg_x2", [C, OWN], F32, kind="ExternalOutput").ap()
        g_f = nc.dram_tensor("dbg_f", [C, OWN], F32, kind="ExternalOutput").ap()
        g_qk = nc.dram_tensor("dbg_qk", [128, 8192], F32, kind="ExternalOutput").ap()
        g_v = nc.dram_tensor("dbg_v", [128, 4224], F32, kind="ExternalOutput").ap()
    h_cin = nc.dram_tensor("cc_in", [2048, OWN], F32)
    h_cout = nc.dram_tensor("cc_out", [C, OWN], F32)

    with ExitStack() as ctx:
        A = ctx.enter_context(nc.sbuf_tensor("arena", [128, FARENA], F32))
        t_ident = ctx.enter_context(nc.sbuf_tensor("identS", [128, 128], F32))
        t_ones = ctx.enter_context(nc.sbuf_tensor("onesS", [128, 128], F32))
        t_bones = ctx.enter_context(nc.sbuf_tensor("bonesS", [128, 128], F32))
        t_nsl = ctx.enter_context(nc.sbuf_tensor("nslS", [128, 8], F32))
        t_bb = ctx.enter_context(nc.sbuf_tensor("bbS", [128, 8], F32))
        t_bc = ctx.enter_context(nc.sbuf_tensor("bcS", [128, 20], F32))
        t_selT = ctx.enter_context(nc.sbuf_tensor("selTS", [128, 128], F32))
        t_sm = ctx.enter_context(nc.sbuf_tensor("smallS", [128, 512], F32))
        t_av = ctx.enter_context(nc.sbuf_tensor("avstgS", [128, 1024], F32))
        t_st = ctx.enter_context(nc.sbuf_tensor("statS", [128, 16], F32))
        t_wf = ctx.enter_context(nc.sbuf_tensor("wfoldS", [128, 8], F32))
        PS = ctx.enter_context(nc.psum_tensor("ps", [128, 4096], F32))
        r = Rec(nc, ctx)

        def BK(i, rows=128, w=512, off=0):
            return PS[0:rows, i * 512 + off:i * 512 + off + w]

        ld = None
        for dst, src in ((t_ident[:, :], d_ident), (t_ones[:, :], d_ones),
                         (t_bones[:, :], d_bones), (t_nsl[:, :], d_nslope),
                         (t_bb[:, :], d_bbase), (t_bc[:, :], d_bcols),
                         (A[:, O_TJ:O_TJ + TQ], d_tj)):
            ld = r.emit("sync", lambda e, d=dst, s=src: e.dma_start(out=d, in_=s[:, :]),
                        inc="LD", by=16)
        for cc in range(8):
            ld = r.emit("sync", lambda e, cc=cc: e.dma_start(
                out=t_selT[:, cc * 16:cc * 16 + 16],
                in_=d_selT[cc * 128:cc * 128 + 128, :]), inc="LD", by=16)
        ld_consts = ld
        vv_ones = None
        for base in (64, 130):
            ap_ones = bass.AP(A, O_V + base, [[FARENA, 128], [132, 32]])
            vv_ones = r.emit("vector", lambda e, a=ap_ones: e.memset(a, 1.0),
                             inc="VV")
        vv_m0 = r.emit("vector", lambda e: e.memset(A[:, SC:SC + 128], 0.0),
                       inc="VV")
        r.wait("gpsimd", "VV", vv_m0)
        msk_gg0 = r.emit("gpsimd", lambda e: e.affine_select(
            A[:, SC:SC + 128], A[:, SC:SC + 128],
            [[1, 128]], AT.is_ge, -1e9, channel_multiplier=-1), inc="GG")

        attn_done_vv = vv_ones
        last_exp_ss = 0
        trp_vv = {6: 0, 7: 0}
        bc_last = {4: 0, 5: 0, 6: 0, 7: 0}
        prod_done_tt = 0
        for p in range(NP):
            r.wait("sync", "TT", prod_done_tt)
            ld_w = 0
            for off, cb in ((O_WQ, p * 128), (O_WK, 512 + p * 128),
                            (O_WV, 1024 + p * 128)):
                for cc in range(8):
                    ld_w = r.emit("sync", lambda e, cc=cc, cb=cb, off=off:
                                  e.dma_start(
                        out=A[:, off + cc * 128:off + cc * 128 + 128],
                        in_=d_wqkv[cc * 128:cc * 128 + 128, cb:cb + 128]),
                        inc="S2", by=16)
            r.wait("scalar", "LD", ld_consts)
            r.wait("scalar", "VV", attn_done_vv)
            ss_tsc = None
            for hh in range(2):
                ss_tsc = r.emit("scalar", lambda e, hh=hh, h=2 * p + hh: e.mul(
                    A[:, O_TSC + hh * TQ:O_TSC + (hh + 1) * TQ],
                    A[:, O_TJ:O_TJ + TQ], t_nsl[:, h:h + 1]), inc="SS")

            tile_tt = {}
            seg_vv = {}
            first_prod = True
            trq = []

            def emit_tr():
                tseg0, ii, vvt0 = trq.pop(0)
                ti = tseg0 * 4 + ii
                trb = 6 + ii % 2
                r.wait("tensor", "VV", vvt0)
                r.wait("tensor", "VV", trp_vv[trb])
                tt2 = r.emit("tensor", lambda e, ii=ii, trb=trb: e.transpose(
                    BK(trb, 128, 128),
                    A[:, O_VT + ii * 128:O_VT + ii * 128 + 128],
                    t_ident[:, :]), inc="TT")
                r.wait("vector", "TT", tt2)
                r.emit("vector", lambda e, ti=ti, trb=trb: e.tensor_copy(
                    A[:, O_V + ti * 132:O_V + ti * 132 + 64],
                    BK(trb, 128, 64)), inc="VV")
                vv2 = r.emit("vector", lambda e, ti=ti, trb=trb: e.tensor_copy(
                    A[:, O_V + ti * 132 + 66:O_V + ti * 132 + 130],
                    BK(trb, 128, 64, off=64)), inc="VV")
                trp_vv[trb] = vv2
                return vv2

            for i in range(64):
                tseg, cc = divmod(i, 8)
                xtb32 = A[:, O_XT + (i % 2) * 512:O_XT + (i % 2) * 512 + 512]
                if i >= 2:
                    r.wait("sync", "TT", tile_tt[i - 2])
                ldx = r.emit("sync", lambda e, xtb32=xtb32, cc=cc, t=tseg:
                             e.dma_start(
                    out=xtb32,
                    in_=d_xT[cc * 128:cc * 128 + 128, t * 512:t * 512 + 512]),
                    inc="S%d" % (i % 2), by=16)
                r.wait("tensor", "S%d" % (i % 2), ldx)
                if cc == 0:
                    r.wait("tensor", "S2", ld_w)
                    if first_prod:
                        r.wait("tensor", "VV", attn_done_vv)
                        first_prod = False
                    if tseg >= 2:
                        r.wait("tensor", "VV", seg_vv[tseg - 2])
                kb, vb, qb = tseg % 2, 2 + tseg % 2, 4 + tseg % 2
                tt = None
                for off, bk in ((O_WK, kb), (O_WV, vb), (O_WQ, qb)):
                    tt = r.emit("tensor", lambda e, off=off, cc=cc, bk=bk,
                                xtb32=xtb32: e.matmul(
                        BK(bk), A[:, off + cc * 128:off + cc * 128 + 128],
                        xtb32, start=(cc == 0), stop=(cc == 7)), inc="TT")
                tile_tt[i] = tt
                if trq and cc < 4:
                    emit_tr()
                if cc == 7:
                    r.wait("vector", "TT", tt)
                    r.emit("vector", lambda e, t=tseg, kb=kb, p=p: e.tensor_scalar(
                        A[:, O_K + t * 512:O_K + t * 512 + 512], BK(kb),
                        t_bc[:, 4 + p:5 + p], None, op0=AT.add), inc="VV")
                    r.emit("vector", lambda e, t=tseg, qb=qb, p=p: e.tensor_scalar(
                        A[:, O_Q + t * 512:O_Q + t * 512 + 512], BK(qb),
                        t_bc[:, p:p + 1], 0.125, op0=AT.add, op1=AT.mult), inc="VV")
                    vvt = r.emit("vector", lambda e, vb=vb: e.tensor_copy(
                        A[:, O_VT:O_VT + 512], BK(vb)), inc="VV")
                    for ii in range(4):
                        trq.append((tseg, ii, vvt))
                    seg_vv[tseg] = vvt
            while trq:
                seg_vv[7] = emit_tr()
            prod_done_tt = r.cnt["TT"]

            if debug and p == 3:
                r.wait("gpsimd", "VV", seg_vv[7])
                r.emit("gpsimd", lambda e: e.dma_start(
                    out=g_qk[:, 0:4096], in_=A[:, O_Q:O_Q + 4096]), inc="SO", by=16)
                r.emit("gpsimd", lambda e: e.dma_start(
                    out=g_qk[:, 4096:8192], in_=A[:, O_K:O_K + 4096]), inc="SO", by=16)
                sfd = r.emit("gpsimd", lambda e: e.dma_start(
                    out=g_v[:, :], in_=A[:, O_V:O_V + 4224]), inc="SO", by=16)
                r.wait("gpsimd", "SO", sfd)

            r.wait("vector", "SS", last_exp_ss if p > 0 else 0)
            bsum_hist = {}
            bsum_tt = 0
            red_vv = 0
            for which, src_off in ((0, O_Q), (1, O_K)):
                for t in range(NSEG):
                    it = which * NSEG + t
                    sqb = A[:, O_PB + (t % 2) * 512:O_PB + (t % 2) * 512 + 512]
                    r.wait("scalar", "VV", seg_vv[7])
                    if it >= 2:
                        r.wait("scalar", "TT", bsum_hist[it - 2])
                    ss = r.emit("scalar", lambda e, t=t, so=src_off, sqb=sqb:
                                e.activation(sqb,
                                             A[:, so + t * 512:so + t * 512 + 512],
                                             AF.Square), inc="SS")
                    r.wait("tensor", "SS", ss)
                    if which == 0 and t == 0:
                        r.wait("tensor", "VV", seg_vv[7])
                    r.wait("tensor", "VV", red_vv)
                    bsum_tt = r.emit("tensor", lambda e, sqb=sqb: e.matmul(
                        BK(0, 128, 512), t_bones[:, :], sqb, start=True, stop=True),
                        inc="TT")
                    bsum_hist[it] = bsum_tt
                    r.wait("vector", "TT", bsum_tt)
                    if t == 0:
                        red_vv = r.emit("vector", lambda e, dc=which: e.reduce_max(
                            t_st[0:2, dc:dc + 1], BK(0, 2, 512), axis=AX.X),
                            inc="VV")
                    else:
                        red_vv = r.emit("vector", lambda e: e.reduce_max(
                            t_st[0:2, 8:9], BK(0, 2, 512), axis=AX.X), inc="VV")
                        r.emit("vector", lambda e, dc=which: e.tensor_tensor(
                            t_st[0:2, dc:dc + 1], t_st[0:2, dc:dc + 1],
                            t_st[0:2, 8:9], op=AT.max), inc="VV")
            vvB = r.emit("vector", lambda e: e.tensor_tensor(
                t_st[0:2, 2:3], t_st[0:2, 0:1], t_st[0:2, 1:2], op=AT.mult),
                inc="VV")
            r.wait("scalar", "VV", vvB)
            ssB = r.emit("scalar", lambda e: e.activation(
                t_st[0:2, 3:4], t_st[0:2, 2:3], AF.Sqrt), inc="SS")
            r.wait("vector", "SS", ssB)
            vvB2 = r.emit("vector", lambda e: e.tensor_scalar(
                t_st[0:2, 4:5], t_st[0:2, 3:4], 1.005, 1.0,
                op0=AT.mult, op1=AT.add), inc="VV")
            r.wait("tensor", "VV", seg_vv[7])
            r.wait("tensor", "VV", vvB2)
            tr_tt = r.emit("tensor", lambda e: e.matmul(
                PS[0:1, 3072:3074], t_st[0:2, 4:5], t_ident[0:2, 0:2],
                start=True, stop=True), inc="TT")
            r.wait("vector", "TT", tr_tt)
            vvc = r.emit("vector", lambda e: e.tensor_copy(
                t_st[0:1, 9:11], PS[0:1, 3072:3074]), inc="VV")
            bc_last[6] = vvc
            r.wait("tensor", "VV", vvc)
            ttb = 0
            for hh in range(2):
                ttb = r.emit("tensor", lambda e, hh=hh: e.matmul(
                    PS[0:128, 3584 + hh:3585 + hh], t_ones[0:1, 0:128],
                    t_st[0:1, 9 + hh:10 + hh], start=(hh == 0), stop=(hh == 1)),
                    inc="TT")
            r.wait("vector", "TT", ttb)
            bias_vv = r.emit("vector", lambda e, p=p: e.tensor_tensor(
                t_st[0:128, 6:8], t_bb[:, 2 * p:2 * p + 2],
                PS[0:128, 3584:3586], op=AT.subtract), inc="VV")

            r.wait("tensor", "VV", bias_vv)
            r.wait("tensor", "SS", ss_tsc)
            r.wait("scalar", "VV", bias_vv)
            r.wait("vector", "GG", msk_gg0)
            av_hist = {0: {}, 1: {}}
            stt_vv = {}
            exp_ss = {}
            avfree_vv = 0
            evac_done = {0: 0, 1: 0}
            pending_norm = [None]
            pending_evac = [None]

            def emit_evac(part=None):
                s0, KT0 = pending_evac[0]
                avb = 4 + 2 * (s0 % 2)
                hs = range(2) if part is None else [part]
                for hh in hs:
                    r.wait("vector", "TT", av_hist[hh][KT0 - 1])
                    r.emit("vector", lambda e, hh=hh, avb=avb: e.tensor_copy(
                        t_av[64 * hh:64 * hh + 64, 0:512],
                        PS[0:64, (avb + hh) * 512:(avb + hh) * 512 + 512]),
                        inc="VV")
                    evac_done[s0 % 2] = r.emit(
                        "vector", lambda e, hh=hh, avb=avb: e.tensor_copy(
                            t_av[32 * hh:32 * hh + 1, 512:1024],
                            PS[64:65, (avb + hh) * 512:(avb + hh) * 512 + 512]),
                        inc="VV")
                if part != 0:
                    pending_evac[0] = None

            def emit_rec(half=None):
                s0, todo, rec = pending_norm[0]
                cols = [(512, 1024)] if half is None else \
                    [(512 + half * 256, 768 + half * 256)]
                if rec is None or half == 1:
                    for c0, c1 in cols:
                        rec = r.emit("vector", lambda e, c0=c0, c1=c1:
                                     e.reciprocal(
                            t_av[0:33, c0:c1], t_av[0:33, c0:c1]), inc="VV")
                    if half != 0:
                        pending_norm[0] = (s0, todo, rec)

            def flush_norm(one=False):
                if pending_norm[0] is None:
                    return None
                emit_rec()
                s0, todo, rec = pending_norm[0]
                hs = todo[:1] if one else todo
                nv = None
                for hh in hs:
                    jc, rb = divmod(2 * p + hh, 2)
                    rb *= 64
                    rrow = 32 * hh
                    bcb = 4 + 2 * (s0 % 2) + hh
                    r.wait("tensor", "VV", rec)
                    r.wait("tensor", "VV", bc_last[bcb])
                    ttn = r.emit("tensor", lambda e, bcb=bcb, rrow=rrow: e.matmul(
                        BK(bcb, 64, 512), t_ones[rrow:rrow + 1, 0:64],
                        t_av[rrow:rrow + 1, 512:1024],
                        start=True, stop=True), inc="TT")
                    r.wait("vector", "TT", ttn)
                    bc_last[bcb] = r.emit("vector", lambda e, bcb=bcb, rb=rb:
                                          e.tensor_copy(
                        t_sm[rb:rb + 64, 0:512], BK(bcb, 64, 512)), inc="VV")
                    nv = r.emit("vector", lambda e, hh=hh, jc=jc, rb=rb,
                                s0=s0: e.tensor_tensor(
                        A[rb:rb + 64,
                          SA + jc * TQ + s0 * 512:SA + jc * TQ + s0 * 512 + 512],
                        t_av[64 * hh:64 * hh + 64, 0:512],
                        t_sm[rb:rb + 64, 0:512], op=AT.mult), inc="VV")
                rest = todo[len(hs):]
                pending_norm[0] = (s0, rest, rec) if rest else None
                return nv

            def do_av(s, q0, sk, hh):
                sp0p = max(q0, sk * 128)
                wp = q0 + 512 - sk * 128 if sk >= 4 * s else 512
                pb_i = 2 * hh + sk % 2
                avb = 4 + 2 * (s % 2)
                av_hist[hh][sk] = r.emit(
                    "tensor", lambda e, hh=hh, sk=sk, sp0p=sp0p, wp=wp, q0=q0,
                    pb_i=pb_i, avb=avb: e.matmul(
                        PS[0:65, (avb + hh) * 512 + sp0p - q0:
                           (avb + hh) * 512 + sp0p - q0 + wp],
                        A[0:128, O_V + sk * 132 + hh * 66:
                          O_V + sk * 132 + hh * 66 + 65],
                        A[:, O_PB + pb_i * 512:O_PB + pb_i * 512 + wp],
                        start=(sk == 0), stop=(sk == 4 * s + 3)), inc="TT")

            for s in range(NSEG):
                q0 = s * 512
                KT = 4 * (s + 1)
                for kt in range(KT):
                    ktb = kt * 128
                    diag = kt >= 4 * s
                    span0 = max(q0, ktb)
                    w = q0 + 512 - ktb if diag else 512
                    j0 = span0 - ktb
                    if kt in (4, 5):
                        flush_norm(one=True)
                    r.wait("tensor", "VV", max(stt_vv.get((0, kt % 2), 0),
                                               stt_vv.get((1, kt % 2), 0)))
                    if kt >= 1:
                        r.wait("tensor", "SS", max(exp_ss.get((0, kt - 1), 0),
                                                   exp_ss.get((1, kt - 1), 0)))
                        if kt == 1:
                            avb = 4 + 2 * (s % 2)
                            r.wait("tensor", "VV", max(evac_done[s % 2],
                                                       bc_last[avb],
                                                       bc_last[avb + 1]))
                    qk_tt = None
                    for hh in range(2):
                        qk_bk = (kt % 2) + 2 * hh
                        rlo = 64 * hh
                        qk_tt = r.emit("tensor", lambda e, qk_bk=qk_bk, w=w,
                                       rlo=rlo, ktb=ktb, span0=span0: e.matmul(
                            BK(qk_bk, 128, w),
                            A[rlo:rlo + 64, O_K + ktb:O_K + ktb + 128],
                            A[rlo:rlo + 64, O_Q + span0:O_Q + span0 + w],
                            start=True, stop=True), inc="TT")
                    if kt >= 1:
                        for hh in range(2):
                            do_av(s, q0, kt - 1, hh)
                    for hh in range(2):
                        qk_bk = (kt % 2) + 2 * hh
                        pb_i = 2 * hh + kt % 2
                        r.wait("vector", "TT", qk_tt)
                        if kt >= 2:
                            r.wait("vector", "TT", av_hist[hh].get(kt - 2, 0))
                        stt_vv[(hh, kt % 2)] = r.emit(
                            "vector", lambda e, hh=hh, j0=j0, w=w, qk_bk=qk_bk,
                            pb_i=pb_i: e.tensor_tensor(
                                A[:, O_PB + pb_i * 512:O_PB + pb_i * 512 + w],
                                A[:, O_TSC + hh * TQ + j0:O_TSC + hh * TQ + j0 + w],
                                BK(qk_bk, 128, w), op=AT.add), inc="VV")
                        if diag:
                            stt_vv[(hh, kt % 2)] = r.emit(
                                "vector", lambda e, pb_i=pb_i: e.tensor_tensor(
                                    A[:, O_PB + pb_i * 512:
                                      O_PB + pb_i * 512 + 128],
                                    A[:, O_PB + pb_i * 512:
                                      O_PB + pb_i * 512 + 128],
                                    A[:, SC:SC + 128], op=AT.add), inc="VV")
                        r.wait("scalar", "VV", stt_vv[(hh, kt % 2)])
                        exp_ss[(hh, kt)] = r.emit(
                            "scalar", lambda e, w=w, pb_i=pb_i, hh=hh: e.activation(
                                A[:, O_PB + pb_i * 512:O_PB + pb_i * 512 + w],
                                A[:, O_PB + pb_i * 512:O_PB + pb_i * 512 + w],
                                AF.Exp, bias=t_st[:, 6 + hh:7 + hh]), inc="SS")
                    if pending_evac[0] is not None and kt in (0, 1):
                        emit_evac(part=kt)
                    if pending_norm[0] is not None and kt in (2, 3) \
                            and pending_evac[0] is None:
                        emit_rec(half=kt - 2)
                r.wait("tensor", "SS", max(exp_ss.get((0, KT - 1), 0),
                                           exp_ss.get((1, KT - 1), 0)))
                for hh in range(2):
                    do_av(s, q0, KT - 1, hh)
                pending_evac[0] = (s, KT)
                pending_norm[0] = (s, [0, 1], None)
            emit_evac()
            attn_done_vv = flush_norm()
            last_exp_ss = r.cnt["SS"]

        r.barrier()

        def dump(dst, slot, nchunk, width):
            sf = 0
            for ch in range(nchunk):
                sf = r.emit("gpsimd", lambda e, ch=ch, dst=dst, slot=slot,
                            width=width: e.dma_start(
                    out=dst[ch * 128:ch * 128 + 128, :],
                    in_=A[:, slot + ch * width:slot + ch * width + width]),
                    inc="SO", by=16)
            r.wait("gpsimd", "SO", sf)

        if debug:
            dump(g_attn, SA, 4, TQ)
        ldp = None
        for jc in range(4):
            ldp = r.emit("sync", lambda e, jc=jc: e.dma_start(
                out=A[:, O_WPJ + jc * 1024:O_WPJ + jc * 1024 + 1024],
                in_=d_wproj[jc * 128:jc * 128 + 128, :]), inc="LD", by=16)
        r.wait("tensor", "LD", ldp)
        ttf = None
        for cs in range(8):
            for jc in range(4):
                ttf = r.emit("tensor", lambda e, cs=cs, jc=jc: e.matmul(
                    PS[0:128, 1024 + cs:1025 + cs],
                    A[:, O_WPJ + jc * 1024 + cs * 128:O_WPJ + jc * 1024 + cs * 128 + 128],
                    t_bc[:, 8 + jc:9 + jc],
                    start=(cs == 0 and jc == 0), stop=(cs == 7 and jc == 3)),
                    inc="TT")
        r.wait("vector", "TT", ttf)
        vvf = r.emit("vector", lambda e: e.tensor_copy(
            t_wf[:, 0:8], PS[0:128, 1024:1032]), inc="VV")
        last_pt_vv = {0: 0, 1: 0}
        st_hist = {}
        st_so = 0
        for idx in range(64):
            cs, s8 = divmod(idx, NSEG)
            ppb = idx % 2
            r.wait("tensor", "VV", vvf)
            r.wait("tensor", "VV", last_pt_vv[ppb])
            ttp = None
            for jc in range(4):
                ttp = r.emit("tensor", lambda e, cs=cs, s8=s8, jc=jc, ppb=ppb:
                             e.matmul(
                    BK(ppb),
                    A[:, O_WPJ + jc * 1024 + cs * 128:O_WPJ + jc * 1024 + cs * 128 + 128],
                    A[:, SA + jc * TQ + s8 * 512:SA + jc * TQ + s8 * 512 + 512],
                    start=(jc == 0), stop=(jc == 3)), inc="TT")
            r.wait("vector", "TT", ttp)
            if idx >= 2:
                r.wait("vector", "S%d" % (idx % 2), st_hist.get(idx - 2, 0))
            pt = A[:, O_PT + ppb * 512:O_PT + ppb * 512 + 512]
            last_pt_vv[ppb] = r.emit("vector", lambda e, ppb=ppb, cs=cs, pt=pt:
                                     e.tensor_scalar(
                pt, BK(ppb), t_wf[:, cs:cs + 1], None, op0=AT.add), inc="VV")
            r.wait("gpsimd", "VV", last_pt_vv[ppb])
            sh, s4 = divmod(s8, 4)
            st_so = r.emit("gpsimd", lambda e, sh=sh, cs=cs, s4=s4, pt=pt:
                           e.dma_start(
                out=h_cin[sh * 1024 + cs * 128:sh * 1024 + cs * 128 + 128,
                          s4 * 512:s4 * 512 + 512], in_=pt), inc="SO", by=16)
            r.emit("gpsimd", lambda e: e.engine_nop(), inc="S%d" % (idx % 2))
            st_hist[idx] = r.cnt["S%d" % (idx % 2)]
        r.wait("gpsimd", "SO", st_so)
        gg_cc = r.emit("gpsimd", lambda e: e.collective_compute(
            "ReduceScatter", AT.add, replica_groups=[[0, 1], [2, 3], [4, 5], [6, 7]],
            ins=[h_cin.ap().opt()], outs=[h_cout.ap().opt()]), inc="GG")

        ttp_last = r.cnt["TT"]
        r.barrier()

        r.wait("sync", "GG", gg_cc)
        evv_hist = {}

        def e_chunk(cc):
            lb = A[:, O_LD + (cc % 2) * 2048:O_LD + (cc % 2) * 2048 + 2048]
            xb = A[:, O_XQ + (cc % 2) * 2048:O_XQ + (cc % 2) * 2048 + 2048]
            if cc >= 2:
                r.wait("sync", "VV", evv_hist[cc - 2])
            r.emit("sync", lambda e, cc=cc, lb=lb: e.dma_start(
                out=lb, in_=h_cout[cc * 128:cc * 128 + 128, :]),
                inc="S%d" % (cc % 2), by=16)
            ldc = r.emit("sync", lambda e, cc=cc, xb=xb: e.dma_start(
                out=xb, in_=d_xq[cc * 128:cc * 128 + 128, :]),
                inc="S%d" % (cc % 2), by=16)
            r.wait("vector", "S%d" % (cc % 2), ldc)
            evv_hist[cc] = r.emit("vector", lambda e, cc=cc, lb=lb, xb=xb:
                                  e.scalar_tensor_tensor(
                A[:, SH + cc * 2048:SH + cc * 2048 + 2048], lb,
                t_bc[:, 12 + cc:13 + cc], xb, op0=AT.add, op1=AT.add), inc="VV")

        r.wait("gpsimd", "TT", ttp_last)
        ld_pref = None
        for e4 in range(4):
            for cc in range(8):
                ld_pref = r.emit("gpsimd", lambda e, e4=e4, cc=cc: e.dma_start(
                    out=A[:, O_KEY + (e4 * 8 + cc) * 128:
                          O_KEY + (e4 * 8 + cc) * 128 + 128],
                    in_=d_keys[e4 * 1024 + cc * 128:e4 * 1024 + cc * 128 + 128,
                               :]), inc="PF", by=16)
        for e4 in range(4):
            ld_pref = r.emit("gpsimd", lambda e, e4=e4: e.dma_start(
                out=A[:, O_VAL + e4 * 1024:O_VAL + e4 * 1024 + 1024],
                in_=d_vals[e4 * 128:e4 * 128 + 128, :]), inc="PF", by=16)
        for cc in range(8):
            e_chunk(cc)

        r.barrier()

        def layernorm(src, dst):
            sq_hist = {}
            out_vv = None

            def stats(s4):
                for cc in range(8):
                    it = s4 * 8 + cc
                    sqb = A[:, O_SQ + (cc % 2) * 512:O_SQ + (cc % 2) * 512 + 512]
                    if it >= 2:
                        r.wait("scalar", "TT", sq_hist[it - 2])
                    ss = r.emit("scalar", lambda e, s4=s4, cc=cc, sqb=sqb,
                                src=src: e.activation(
                                    sqb,
                                    A[:, src + cc * 2048 + s4 * 512:
                                      src + cc * 2048 + s4 * 512 + 512],
                                    AF.Square), inc="SS")
                    r.wait("tensor", "SS", ss)
                    r.emit("tensor", lambda e, s4=s4, cc=cc, src=src: e.matmul(
                        PS[0:1, s4 * 512:s4 * 512 + 512], t_ones[:, 0:1],
                        A[:, src + cc * 2048 + s4 * 512:
                          src + cc * 2048 + s4 * 512 + 512],
                        start=(cc == 0), stop=(cc == 7)), inc="TT")
                    sq_hist[it] = r.emit("tensor", lambda e, s4=s4, cc=cc,
                                         sqb=sqb: e.matmul(
                        PS[0:1, 2048 + s4 * 512:2048 + s4 * 512 + 512],
                        t_ones[:, 0:1], sqb,
                        start=(cc == 0), stop=(cc == 7)), inc="TT")

            def chain_apply(s4):
                nonlocal out_vv
                R0 = lambda o: A[0:1, o + s4 * 512:o + s4 * 512 + 512]
                r.wait("vector", "TT", sq_hist[s4 * 8 + 7])
                r.emit("vector", lambda e, R0=R0, s4=s4: e.tensor_scalar(
                    R0(ST_M), PS[0:1, s4 * 512:s4 * 512 + 512], 1.0 / C, None,
                    op0=AT.mult), inc="VV")
                r.emit("vector", lambda e, R0=R0, s4=s4: e.tensor_scalar(
                    R0(ST_V), PS[0:1, 2048 + s4 * 512:2048 + s4 * 512 + 512],
                    1.0 / C, None, op0=AT.mult), inc="VV")
                r.emit("vector", lambda e, R0=R0: e.tensor_tensor(
                    R0(ST_T), R0(ST_M), R0(ST_M), op=AT.mult), inc="VV")
                r.emit("vector", lambda e, R0=R0: e.tensor_tensor(
                    R0(ST_V), R0(ST_V), R0(ST_T), op=AT.subtract), inc="VV")
                vvv = r.emit("vector", lambda e, R0=R0: e.tensor_scalar(
                    R0(ST_V), R0(ST_V), EPS, None, op0=AT.add), inc="VV")
                r.wait("scalar", "VV", vvv)
                ssv = r.emit("scalar", lambda e, R0=R0: e.activation(
                    R0(ST_T), R0(ST_V), AF.Sqrt), inc="SS")
                r.wait("vector", "SS", ssv)
                r.emit("vector", lambda e, R0=R0: e.reciprocal(
                    R0(ST_R), R0(ST_T)), inc="VV")
                for _ in range(2):
                    r.emit("vector", lambda e, R0=R0: e.tensor_tensor(
                        R0(ST_T), R0(ST_R), R0(ST_R), op=AT.mult), inc="VV")
                    r.emit("vector", lambda e, R0=R0: e.tensor_tensor(
                        R0(ST_T), R0(ST_T), R0(ST_V), op=AT.mult), inc="VV")
                    r.emit("vector", lambda e, R0=R0: e.tensor_scalar(
                        R0(ST_T), R0(ST_T), -0.5, 1.5, op0=AT.mult, op1=AT.add),
                        inc="VV")
                    r.emit("vector", lambda e, R0=R0: e.tensor_tensor(
                        R0(ST_R), R0(ST_R), R0(ST_T), op=AT.mult), inc="VV")
                r.emit("vector", lambda e, R0=R0: e.tensor_tensor(
                    R0(ST_B), R0(ST_M), R0(ST_R), op=AT.mult), inc="VV")
                vvb = r.emit("vector", lambda e, R0=R0: e.tensor_scalar(
                    R0(ST_B), R0(ST_B), -1.0, None, op0=AT.mult), inc="VV")
                r.wait("tensor", "VV", vvb)
                tta = r.emit("tensor", lambda e, s4=s4: e.matmul(
                    PS[0:128, s4 * 512:s4 * 512 + 512], t_ones[0:1, 0:128],
                    A[0:1, ST_R + s4 * 512:ST_R + s4 * 512 + 512],
                    start=True, stop=True), inc="TT")
                ttb2 = r.emit("tensor", lambda e, s4=s4: e.matmul(
                    PS[0:128, 2048 + s4 * 512:2048 + s4 * 512 + 512],
                    t_ones[0:1, 0:128],
                    A[0:1, ST_B + s4 * 512:ST_B + s4 * 512 + 512],
                    start=True, stop=True), inc="TT")
                r.wait("vector", "TT", tta)
                vva = r.emit("vector", lambda e, s4=s4: e.tensor_copy(
                    A[:, SC + s4 * 512:SC + s4 * 512 + 512],
                    PS[0:128, s4 * 512:s4 * 512 + 512]), inc="VV")
                r.wait("vector", "TT", ttb2)
                for cc in range(8):
                    r.emit("vector", lambda e, cc=cc, s4=s4, src=src, dst=dst:
                           e.tensor_tensor(
                        A[:, dst + cc * 2048 + s4 * 512:
                          dst + cc * 2048 + s4 * 512 + 512],
                        A[:, src + cc * 2048 + s4 * 512:
                          src + cc * 2048 + s4 * 512 + 512],
                        A[:, SC + s4 * 512:SC + s4 * 512 + 512], op=AT.mult),
                        inc="VV")
                    out_vv = r.emit("vector", lambda e, cc=cc, s4=s4, dst=dst:
                                    e.tensor_tensor(
                        A[:, dst + cc * 2048 + s4 * 512:
                          dst + cc * 2048 + s4 * 512 + 512],
                        A[:, dst + cc * 2048 + s4 * 512:
                          dst + cc * 2048 + s4 * 512 + 512],
                        PS[0:128, 2048 + s4 * 512:2048 + s4 * 512 + 512],
                        op=AT.add), inc="VV")

            stats(0)
            for s4 in range(4):
                if s4 < 3:
                    stats(s4 + 1)
                chain_apply(s4)
            return out_vv

        if debug:
            dump(g_h1, SH, 8, OWN)
        layernorm(SH, SA)
        r.barrier()
        if debug:
            dump(g_x2, SA, 8, OWN)

        tts = None
        for s4 in range(4):
            for cc in range(8):
                tts = r.emit("tensor", lambda e, s4=s4, cc=cc: e.matmul(
                    PS[0:16, s4 * 512:s4 * 512 + 512],
                    t_selT[:, cc * 16:cc * 16 + 16],
                    A[:, SA + cc * 2048 + s4 * 512:SA + cc * 2048 + s4 * 512 + 512],
                    start=(cc == 0), stop=(cc == 7)), inc="TT")
        r.wait("scalar", "TT", tts)
        ss_g = r.emit("scalar", lambda e: e.activation(
            A[0:16, O_G:O_G + 2048], PS[0:16, 0:2048], AF.Sigmoid), inc="SS")
        r.wait("vector", "SS", ss_g)
        r.wait("sync", "SS", ss_g)
        ld_oh = r.emit("sync", lambda e: e.dma_start(
            out=A[0:16, O_OH:O_OH + 2048], in_=d_oh[:, :]), inc="LD", by=16)

        hb_last = {0: 0, 1: 0}
        fb_last = {4: 0, 5: 0}
        g2_last = 0
        q_tt = 0
        r.wait("tensor", "SS", ss_g)
        hid_done_tt = 0
        ldk = 0
        for q4 in range(4):
            if q4 == 0:
                r.wait("tensor", "PF", ld_pref)
            if q4 > 0:
                r.wait("sync", "TT", q_tt)
                for e4 in range(4):
                    eg = 4 * q4 + e4
                    ldk = r.emit("sync", lambda e, eg=eg, e4=e4: e.dma_start(
                        out=A[:, O_VAL + e4 * 1024:O_VAL + e4 * 1024 + 1024],
                        in_=d_vals[eg * 128:eg * 128 + 128, :]), inc="LD", by=16)
            for s4 in range(4):
                for e4 in range(4):
                    eg = 4 * q4 + e4
                    hb = e4 % 2
                    r.wait("tensor", "LD", ldk)
                    r.wait("tensor", "VV", hb_last[hb])
                    tth = None
                    for cc in range(8):
                        tth = r.emit("tensor", lambda e, e4=e4, cc=cc, s4=s4, hb=hb:
                                     e.matmul(
                            BK(hb), A[:, O_KEY + (e4 * 8 + cc) * 128:
                                      O_KEY + (e4 * 8 + cc) * 128 + 128],
                            A[:, SA + cc * 2048 + s4 * 512:
                              SA + cc * 2048 + s4 * 512 + 512],
                            start=(cc == 0), stop=(cc == 7)), inc="TT")
                    hid_done_tt = tth
                    if s4 == 3 and q4 < 3:
                        eg2 = 4 * (q4 + 1) + e4
                        r.wait("sync", "TT", tth)
                        for cc in range(8):
                            ldk = r.emit("sync", lambda e, eg2=eg2, e4=e4,
                                         cc=cc: e.dma_start(
                                out=A[:, O_KEY + (e4 * 8 + cc) * 128:
                                      O_KEY + (e4 * 8 + cc) * 128 + 128],
                                in_=d_keys[eg2 * 1024 + cc * 128:
                                           eg2 * 1024 + cc * 128 + 128,
                                           :]), inc="LD", by=16)
                    r.wait("tensor", "LD", ld_oh)
                    r.wait("tensor", "VV", g2_last)
                    ttg = r.emit("tensor", lambda e, eg=eg, s4=s4: e.matmul(
                        BK(2), A[0:16, O_OH + eg * 128:O_OH + eg * 128 + 128],
                        A[0:16, O_G + s4 * 512:O_G + s4 * 512 + 512],
                        start=True, stop=True), inc="TT")
                    r.wait("vector", "TT", ttg)
                    g2_last = r.emit("vector", lambda e: e.tensor_copy(
                        A[:, O_GES:O_GES + 512], BK(2)), inc="VV")
                    r.wait("vector", "TT", tth)
                    ht_vv = r.emit("vector", lambda e, e4=e4, hb=hb:
                                   e.scalar_tensor_tensor(
                        A[:, O_HT + e4 * 512:O_HT + e4 * 512 + 512], BK(hb),
                        0.0, A[:, O_GES:O_GES + 512],
                        op0=AT.max, op1=AT.mult), inc="VV")
                    hb_last[hb] = ht_vv
                for cs in range(8):
                    fb = 4 + cs % 2
                    r.wait("tensor", "VV", ht_vv)
                    r.wait("tensor", "VV", fb_last[fb])
                    ttv = None
                    for e4 in range(4):
                        ttv = r.emit("tensor", lambda e, e4=e4, cs=cs, fb=fb:
                                     e.matmul(
                            BK(fb), A[:, O_VAL + e4 * 1024 + cs * 128:
                                      O_VAL + e4 * 1024 + cs * 128 + 128],
                            A[:, O_HT + e4 * 512:O_HT + e4 * 512 + 512],
                            start=(e4 == 0), stop=(e4 == 3)), inc="TT")
                    q_tt = ttv
                    r.wait("vector", "TT", ttv)
                    src_slot = SA if q4 == 0 else SH
                    fb_last[fb] = r.emit(
                        "vector", lambda e, cs=cs, s4=s4, fb=fb, ss=src_slot:
                        e.tensor_tensor(
                            A[:, SH + cs * 2048 + s4 * 512:
                              SH + cs * 2048 + s4 * 512 + 512],
                            BK(fb),
                            A[:, ss + cs * 2048 + s4 * 512:
                              ss + cs * 2048 + s4 * 512 + 512],
                            op=AT.add), inc="VV")

        r.barrier()
        if debug:
            dump(g_f, SH, 8, OWN)
        layernorm(SH, SA)
        r.barrier()

        tr_vv = {6: 0, 7: 0}
        st_f = 0
        for t16 in range(16):
            for cc in range(8):
                trb = 6 + (t16 * 8 + cc) % 2
                r.wait("tensor", "VV", tr_vv[trb])
                ttt = r.emit("tensor", lambda e, t16=t16, cc=cc, trb=trb:
                             e.transpose(
                    BK(trb, 128, 128),
                    A[:, SA + cc * 2048 + t16 * 128:SA + cc * 2048 + t16 * 128 + 128],
                    t_ident[:, :]), inc="TT")
                r.wait("vector", "TT", ttt)
                tr_vv[trb] = r.emit("vector", lambda e, t16=t16, cc=cc, trb=trb:
                                    e.tensor_copy(
                    A[:, SH + t16 * 1024 + cc * 128:SH + t16 * 1024 + cc * 128 + 128],
                    BK(trb, 128, 128)), inc="VV")
            r.wait("gpsimd", "VV", tr_vv[6])
            r.wait("gpsimd", "VV", tr_vv[7])
            st_f = r.emit("gpsimd", lambda e, t16=t16: e.dma_start(
                out=d_out[t16 * 128:t16 * 128 + 128, :],
                in_=A[:, SH + t16 * 1024:SH + t16 * 1024 + 1024]), inc="SO", by=16)
        r.wait("gpsimd", "SO", st_f)

        with nc.Block() as block:
            block.sync(lambda e: [f(e) for f in r.ops["sync"]])
            block.tensor(lambda e: [f(e) for f in r.ops["tensor"]])
            block.vector(lambda e: [f(e) for f in r.ops["vector"]])
            block.scalar(lambda e: [f(e) for f in r.ops["scalar"]])
            block.gpsimd(lambda e: [f(e) for f in r.ops["gpsimd"]])
    return nc


_NC_CACHE = None


def kernel(L0_attn_x, L0_attn_W_attn, L0_attn_b_attn, L0_attn_W_proj,
           L0_attn_b_proj, L0_ffn_expert_sel, L0_ffn_keys, L0_ffn_values):
    global _NC_CACHE
    x = np.asarray(L0_attn_x, np.float32)
    W_attn = np.asarray(L0_attn_W_attn, np.float32)
    b_attn = np.asarray(L0_attn_b_attn, np.float32)
    W_proj = np.asarray(L0_attn_W_proj, np.float32)
    b_proj = np.asarray(L0_attn_b_proj, np.float32)
    sel = np.asarray(L0_ffn_expert_sel, np.float32)
    keys = np.asarray(L0_ffn_keys, np.float32)
    values = np.asarray(L0_ffn_values, np.float32)

    ident = np.eye(128, dtype=np.float32)
    ones = np.ones((128, 128), np.float32)
    bones = np.zeros((128, 128), np.float32)
    for j in range(128):
        if j % 2 == 0:
            bones[0:64, j] = 1.0
        else:
            bones[64:128, j] = 1.0
    tj = np.broadcast_to(np.arange(TQ, dtype=np.float32), (128, TQ)).copy()
    selT = np.ascontiguousarray(sel.T)
    keysR = np.ascontiguousarray(keys.reshape(16 * C, 128))
    valuesR = np.ascontiguousarray(values.reshape(16 * 128, C))
    parts = np.arange(128, dtype=np.float32)
    oh16 = np.kron(np.eye(16, dtype=np.float32), np.ones((1, 128), np.float32))

    in_maps = []
    for c in range(8):
        b, hg = divmod(c, 2)
        heads = np.arange(hg * 8, hg * 8 + 8)
        slopes = (0.5 ** (heads + 1)).astype(np.float32)
        xT = np.ascontiguousarray(x[b].T)
        xq = np.ascontiguousarray(xT[:, hg * OWN:(hg + 1) * OWN])
        wqkv = np.ascontiguousarray(np.concatenate(
            [W_attn[:, 512 * hg:512 * hg + 512],
             W_attn[:, 1024 + 512 * hg:1024 + 512 * hg + 512],
             W_attn[:, 2048 + 512 * hg:2048 + 512 * hg + 512]], axis=1))
        wprojmy = np.ascontiguousarray(W_proj[512 * hg:512 * hg + 512, :])
        bcols = np.zeros((128, 20), np.float32)
        for p in range(4):
            bcols[:, p] = b_attn[512 * hg + 128 * p:512 * hg + 128 * p + 128]
            bcols[:, 4 + p] = b_attn[1024 + 512 * hg + 128 * p:
                                     1024 + 512 * hg + 128 * p + 128]
        bv_my = b_attn[2048 + 512 * hg:2048 + 512 * hg + 512]
        for jc in range(4):
            bcols[:, 8 + jc] = bv_my[jc * 128:jc * 128 + 128]
        for cc in range(8):
            bcols[:, 12 + cc] = b_proj[cc * 128:cc * 128 + 128]
        negslope = np.broadcast_to(-slopes[None, :], (128, 8)).copy()
        biasbase = (parts[:, None] * slopes[None, :]).astype(np.float32)
        in_maps.append(dict(
            xT=xT, xq=xq, wqkv=wqkv, wprojmy=wprojmy, bcols=bcols,
            negslope=negslope, biasbase=biasbase, selT=selT, keysR=keysR,
            valuesR=valuesR, tj=tj, identc=ident, onesc=ones, blockonesc=bones,
            oh16=oh16))

    if _NC_CACHE is None:
        _NC_CACHE = build_nc()
    res = run_bass_kernel_spmd(_NC_CACHE, in_maps, core_ids=list(range(8)))
    y = np.empty((4, TQ, C), np.float32)
    for c in range(8):
        b, hg = divmod(c, 2)
        y[b, hg * OWN:(hg + 1) * OWN, :] = res.results[c]["out"]
    return y


# version 29
# speedup vs baseline: 1.0338x; 1.0338x over previous
import numpy as np
import concourse.bass as bass
import concourse.mybir as mybir
from concourse.bass_utils import run_bass_kernel_spmd
from contextlib import ExitStack

F32 = mybir.dt.float32
BF16 = mybir.dt.bfloat16
AT = mybir.AluOpType
AF = mybir.ActivationFunctionType
AX = mybir.AxisListType

C = 1024
TQ = 4096
HL = 8
NP = 4
SEG = 512
NSEG = 8
OWN = 2048
EPS = 1e-5

SA = 0
SH = 16384
SB = 32768
SC = 48768
FARENA = 50816
O_WQ, O_WK, O_WV, O_TSC = SH, SH + 1024, SH + 2048, SH + 3072
O_TJ = SH + 11264
O_Q, O_K, O_V = SB, SB + 4096, SB + 8192
O_PB, O_VT, O_XT = SB + 12416, SB + 14464, SB + 14976
O_PT = SC
O_WPJ = SB + 1024
O_LD, O_XQ = SB + 8192, SB + 12288
O_KEY, O_VAL, O_HT = SB, SB + 4096, SB + 8192
O_GES, O_G, O_SQ = SB + 10240, SB + 10752, SB + 12800
O_OH = SB + 12800
ST_M, ST_V, ST_R, ST_T, ST_B = (SA + 0, SA + 2048, SA + 4096, SA + 6144,
                                SA + 8192)


class Rec:
    ENGS = ("sync", "tensor", "vector", "scalar", "gpsimd")

    def __init__(self, nc, ctx):
        self.nc = nc
        self.ops = {e: [] for e in self.ENGS}
        self.sems = {}
        self.cnt = {}
        self.lastwait = {}
        for name in ("LD", "PF", "TT", "VV", "SS", "GG", "SO",
                     "S0", "S1", "S2", "S3", "S4", "S5"):
            self.sems[name] = ctx.enter_context(nc.semaphore(name))
            self.cnt[name] = 0

    def emit(self, eng, fn, inc=None, by=1):
        if inc is None:
            self.ops[eng].append(fn)
            return None
        self.cnt[inc] += by
        sem = self.sems[inc]
        self.ops[eng].append(lambda e, f=fn, s=sem, b=by: f(e).then_inc(s, b))
        return self.cnt[inc]

    def wait(self, eng, sname, val):
        if not val:
            return
        key = (eng, sname)
        if self.lastwait.get(key, -1) >= val:
            return
        self.lastwait[key] = val
        sem = self.sems[sname]
        self.ops[eng].append(lambda e, s=sem, v=val: e.wait_ge(s, v))

    def barrier(self):
        for sname in ("LD", "TT", "VV", "SS", "GG", "SO"):
            v = self.cnt[sname]
            for eng in self.ENGS:
                self.wait(eng, sname, v)


def build_nc(debug=False):
    nc = bass.Bass(trn_type="TRN2", target_bir_lowering=False)

    d_xT = nc.dram_tensor("xT", [C, TQ], F32, kind="ExternalInput").ap()
    d_xq = nc.dram_tensor("xq", [C, OWN], F32, kind="ExternalInput").ap()
    d_wqkv = nc.dram_tensor("wqkv", [C, 1536], F32, kind="ExternalInput").ap()
    d_wproj = nc.dram_tensor("wprojmy", [512, C], F32, kind="ExternalInput").ap()
    d_bcols = nc.dram_tensor("bcols", [128, 20], F32, kind="ExternalInput").ap()
    d_nslope = nc.dram_tensor("negslope", [128, 8], F32, kind="ExternalInput").ap()
    d_bbase = nc.dram_tensor("biasbase", [128, 8], F32, kind="ExternalInput").ap()
    d_selT = nc.dram_tensor("selT", [C, 16], F32, kind="ExternalInput").ap()
    d_keys = nc.dram_tensor("keysR", [16 * C, 128], F32, kind="ExternalInput").ap()
    d_vals = nc.dram_tensor("valuesR", [2048, C], F32, kind="ExternalInput").ap()
    d_tj = nc.dram_tensor("tj", [128, TQ], F32, kind="ExternalInput").ap()
    d_ident = nc.dram_tensor("identc", [128, 128], F32, kind="ExternalInput").ap()
    d_ones = nc.dram_tensor("onesc", [128, 128], F32, kind="ExternalInput").ap()
    d_bones = nc.dram_tensor("blockonesc", [128, 128], F32, kind="ExternalInput").ap()
    d_oh = nc.dram_tensor("oh16", [16, 2048], F32, kind="ExternalInput").ap()
    d_out = nc.dram_tensor("out", [OWN, C], F32, kind="ExternalOutput").ap()
    if debug:
        g_attn = nc.dram_tensor("dbg_attn", [512, TQ], F32, kind="ExternalOutput").ap()
        g_h1 = nc.dram_tensor("dbg_h1", [C, OWN], F32, kind="ExternalOutput").ap()
        g_x2 = nc.dram_tensor("dbg_x2", [C, OWN], F32, kind="ExternalOutput").ap()
        g_f = nc.dram_tensor("dbg_f", [C, OWN], F32, kind="ExternalOutput").ap()
        g_qk = nc.dram_tensor("dbg_qk", [128, 8192], F32, kind="ExternalOutput").ap()
        g_v = nc.dram_tensor("dbg_v", [128, 4224], F32, kind="ExternalOutput").ap()
    h_cin = nc.dram_tensor("cc_in", [2048, OWN], F32)
    h_cout = nc.dram_tensor("cc_out", [C, OWN], F32)

    with ExitStack() as ctx:
        A = ctx.enter_context(nc.sbuf_tensor("arena", [128, FARENA], F32))
        t_ident = ctx.enter_context(nc.sbuf_tensor("identS", [128, 128], F32))
        t_ones = ctx.enter_context(nc.sbuf_tensor("onesS", [128, 128], F32))
        t_bones = ctx.enter_context(nc.sbuf_tensor("bonesS", [128, 128], F32))
        t_nsl = ctx.enter_context(nc.sbuf_tensor("nslS", [128, 8], F32))
        t_bb = ctx.enter_context(nc.sbuf_tensor("bbS", [128, 8], F32))
        t_bc = ctx.enter_context(nc.sbuf_tensor("bcS", [128, 20], F32))
        t_selT = ctx.enter_context(nc.sbuf_tensor("selTS", [128, 128], F32))
        t_sm = ctx.enter_context(nc.sbuf_tensor("smallS", [128, 512], F32))
        t_av = ctx.enter_context(nc.sbuf_tensor("avstgS", [128, 1024], F32))
        t_st = ctx.enter_context(nc.sbuf_tensor("statS", [128, 16], F32))
        t_wf = ctx.enter_context(nc.sbuf_tensor("wfoldS", [128, 8], F32))
        PS = ctx.enter_context(nc.psum_tensor("ps", [128, 4096], F32))
        r = Rec(nc, ctx)

        def BK(i, rows=128, w=512, off=0):
            return PS[0:rows, i * 512 + off:i * 512 + off + w]

        ld = None
        for dst, src in ((t_ident[:, :], d_ident), (t_ones[:, :], d_ones),
                         (t_bones[:, :], d_bones), (t_nsl[:, :], d_nslope),
                         (t_bb[:, :], d_bbase), (t_bc[:, :], d_bcols),
                         (A[:, O_TJ:O_TJ + TQ], d_tj)):
            ld = r.emit("sync", lambda e, d=dst, s=src: e.dma_start(out=d, in_=s[:, :]),
                        inc="LD", by=16)
        for cc in range(8):
            ld = r.emit("sync", lambda e, cc=cc: e.dma_start(
                out=t_selT[:, cc * 16:cc * 16 + 16],
                in_=d_selT[cc * 128:cc * 128 + 128, :]), inc="LD", by=16)
        ld_consts = ld
        vv_ones = None
        for base in (64, 130):
            ap_ones = bass.AP(A, O_V + base, [[FARENA, 128], [132, 32]])
            vv_ones = r.emit("vector", lambda e, a=ap_ones: e.memset(a, 1.0),
                             inc="VV")
        vv_m0 = r.emit("vector", lambda e: e.memset(A[:, SC:SC + 128], 0.0),
                       inc="VV")
        r.wait("gpsimd", "VV", vv_m0)
        msk_gg0 = r.emit("gpsimd", lambda e: e.affine_select(
            A[:, SC:SC + 128], A[:, SC:SC + 128],
            [[1, 128]], AT.is_ge, -1e9, channel_multiplier=-1), inc="GG")

        attn_done_vv = vv_ones
        last_exp_ss = 0
        trp_vv = {6: 0, 7: 0}
        bc_last = {4: 0, 5: 0, 6: 0, 7: 0}
        prod_done_tt = 0
        for p in range(NP):
            r.wait("sync", "TT", prod_done_tt)
            ld_w = 0
            for off, cb in ((O_WQ, p * 128), (O_WK, 512 + p * 128),
                            (O_WV, 1024 + p * 128)):
                for cc in range(8):
                    ld_w = r.emit("sync", lambda e, cc=cc, cb=cb, off=off:
                                  e.dma_start(
                        out=A[:, off + cc * 128:off + cc * 128 + 128],
                        in_=d_wqkv[cc * 128:cc * 128 + 128, cb:cb + 128]),
                        inc="S2", by=16)
            r.wait("scalar", "LD", ld_consts)
            r.wait("scalar", "VV", attn_done_vv)
            ss_tsc = None
            for hh in range(2):
                ss_tsc = r.emit("scalar", lambda e, hh=hh, h=2 * p + hh: e.mul(
                    A[:, O_TSC + hh * TQ:O_TSC + (hh + 1) * TQ],
                    A[:, O_TJ:O_TJ + TQ], t_nsl[:, h:h + 1]), inc="SS")

            tile_tt = {}
            seg_vv = {}
            first_prod = True
            trq = []

            def emit_tr():
                tseg0, ii, vvt0 = trq.pop(0)
                ti = tseg0 * 4 + ii
                trb = 6 + ii % 2
                r.wait("tensor", "VV", vvt0)
                r.wait("tensor", "VV", trp_vv[trb])
                tt2 = r.emit("tensor", lambda e, ii=ii, trb=trb: e.transpose(
                    BK(trb, 128, 128),
                    A[:, O_VT + ii * 128:O_VT + ii * 128 + 128],
                    t_ident[:, :]), inc="TT")
                r.wait("vector", "TT", tt2)
                r.emit("vector", lambda e, ti=ti, trb=trb: e.tensor_copy(
                    A[:, O_V + ti * 132:O_V + ti * 132 + 64],
                    BK(trb, 128, 64)), inc="VV")
                vv2 = r.emit("vector", lambda e, ti=ti, trb=trb: e.tensor_copy(
                    A[:, O_V + ti * 132 + 66:O_V + ti * 132 + 130],
                    BK(trb, 128, 64, off=64)), inc="VV")
                trp_vv[trb] = vv2
                return vv2

            attn_guard_tt = r.cnt["TT"]
            for i in range(64):
                tseg, cc = divmod(i, 8)
                xtb32 = A[:, O_XT + (i % 2) * 512:O_XT + (i % 2) * 512 + 512]
                if i == 0:
                    r.wait("sync", "TT", attn_guard_tt)
                if i >= 2:
                    r.wait("sync", "TT", tile_tt[i - 2])
                ldx = r.emit("sync", lambda e, xtb32=xtb32, cc=cc, t=tseg:
                             e.dma_start(
                    out=xtb32,
                    in_=d_xT[cc * 128:cc * 128 + 128, t * 512:t * 512 + 512]),
                    inc="S%d" % (i % 2), by=16)
                r.wait("tensor", "S%d" % (i % 2), ldx)
                if cc == 0:
                    r.wait("tensor", "S2", ld_w)
                    if first_prod:
                        r.wait("tensor", "VV", attn_done_vv)
                        first_prod = False
                    if tseg >= 2:
                        r.wait("tensor", "VV", seg_vv[tseg - 2])
                kb, vb, qb = tseg % 2, 2 + tseg % 2, 4 + tseg % 2
                tt = None
                for off, bk in ((O_WK, kb), (O_WV, vb), (O_WQ, qb)):
                    tt = r.emit("tensor", lambda e, off=off, cc=cc, bk=bk,
                                xtb32=xtb32: e.matmul(
                        BK(bk), A[:, off + cc * 128:off + cc * 128 + 128],
                        xtb32, start=(cc == 0), stop=(cc == 7)), inc="TT")
                tile_tt[i] = tt
                if trq and cc < 4:
                    emit_tr()
                if cc == 7:
                    r.wait("vector", "TT", tt)
                    r.emit("vector", lambda e, t=tseg, kb=kb, p=p: e.tensor_scalar(
                        A[:, O_K + t * 512:O_K + t * 512 + 512], BK(kb),
                        t_bc[:, 4 + p:5 + p], None, op0=AT.add), inc="VV")
                    r.emit("vector", lambda e, t=tseg, qb=qb, p=p: e.tensor_scalar(
                        A[:, O_Q + t * 512:O_Q + t * 512 + 512], BK(qb),
                        t_bc[:, p:p + 1], 0.125, op0=AT.add, op1=AT.mult), inc="VV")
                    vvt = r.emit("vector", lambda e, vb=vb: e.tensor_copy(
                        A[:, O_VT:O_VT + 512], BK(vb)), inc="VV")
                    for ii in range(4):
                        trq.append((tseg, ii, vvt))
                    seg_vv[tseg] = vvt
            while trq:
                seg_vv[7] = emit_tr()
            prod_done_tt = r.cnt["TT"]

            if debug and p == 3:
                r.wait("gpsimd", "VV", seg_vv[7])
                r.emit("gpsimd", lambda e: e.dma_start(
                    out=g_qk[:, 0:4096], in_=A[:, O_Q:O_Q + 4096]), inc="SO", by=16)
                r.emit("gpsimd", lambda e: e.dma_start(
                    out=g_qk[:, 4096:8192], in_=A[:, O_K:O_K + 4096]), inc="SO", by=16)
                sfd = r.emit("gpsimd", lambda e: e.dma_start(
                    out=g_v[:, :], in_=A[:, O_V:O_V + 4224]), inc="SO", by=16)
                r.wait("gpsimd", "SO", sfd)

            r.wait("vector", "SS", last_exp_ss if p > 0 else 0)
            bsum_hist = {}
            bsum_tt = 0
            red_vv = 0
            for which, src_off in ((0, O_Q), (1, O_K)):
                for t in range(NSEG):
                    it = which * NSEG + t
                    sqb = A[:, O_PB + (t % 2) * 512:O_PB + (t % 2) * 512 + 512]
                    r.wait("scalar", "VV", seg_vv[7])
                    if it >= 2:
                        r.wait("scalar", "TT", bsum_hist[it - 2])
                    ss = r.emit("scalar", lambda e, t=t, so=src_off, sqb=sqb:
                                e.activation(sqb,
                                             A[:, so + t * 512:so + t * 512 + 512],
                                             AF.Square), inc="SS")
                    r.wait("tensor", "SS", ss)
                    if which == 0 and t == 0:
                        r.wait("tensor", "VV", seg_vv[7])
                    r.wait("tensor", "VV", red_vv)
                    bsum_tt = r.emit("tensor", lambda e, sqb=sqb: e.matmul(
                        BK(0, 128, 512), t_bones[:, :], sqb, start=True, stop=True),
                        inc="TT")
                    bsum_hist[it] = bsum_tt
                    r.wait("vector", "TT", bsum_tt)
                    if t == 0:
                        red_vv = r.emit("vector", lambda e, dc=which: e.reduce_max(
                            t_st[0:2, dc:dc + 1], BK(0, 2, 512), axis=AX.X),
                            inc="VV")
                    else:
                        red_vv = r.emit("vector", lambda e: e.reduce_max(
                            t_st[0:2, 8:9], BK(0, 2, 512), axis=AX.X), inc="VV")
                        r.emit("vector", lambda e, dc=which: e.tensor_tensor(
                            t_st[0:2, dc:dc + 1], t_st[0:2, dc:dc + 1],
                            t_st[0:2, 8:9], op=AT.max), inc="VV")
            vvB = r.emit("vector", lambda e: e.tensor_tensor(
                t_st[0:2, 2:3], t_st[0:2, 0:1], t_st[0:2, 1:2], op=AT.mult),
                inc="VV")
            r.wait("scalar", "VV", vvB)
            ssB = r.emit("scalar", lambda e: e.activation(
                t_st[0:2, 3:4], t_st[0:2, 2:3], AF.Sqrt), inc="SS")
            r.wait("vector", "SS", ssB)
            vvB2 = r.emit("vector", lambda e: e.tensor_scalar(
                t_st[0:2, 4:5], t_st[0:2, 3:4], 1.005, 1.0,
                op0=AT.mult, op1=AT.add), inc="VV")
            r.wait("tensor", "VV", seg_vv[7])
            r.wait("tensor", "VV", vvB2)
            tr_tt = r.emit("tensor", lambda e: e.matmul(
                PS[0:1, 3072:3074], t_st[0:2, 4:5], t_ident[0:2, 0:2],
                start=True, stop=True), inc="TT")
            r.wait("vector", "TT", tr_tt)
            vvc = r.emit("vector", lambda e: e.tensor_copy(
                t_st[0:1, 9:11], PS[0:1, 3072:3074]), inc="VV")
            bc_last[6] = vvc
            r.wait("tensor", "VV", vvc)
            ttb = 0
            for hh in range(2):
                ttb = r.emit("tensor", lambda e, hh=hh: e.matmul(
                    PS[0:128, 3584 + hh:3585 + hh], t_ones[0:1, 0:128],
                    t_st[0:1, 9 + hh:10 + hh], start=(hh == 0), stop=(hh == 1)),
                    inc="TT")
            r.wait("vector", "TT", ttb)
            bias_vv = r.emit("vector", lambda e, p=p: e.tensor_tensor(
                t_st[0:128, 6:8], t_bb[:, 2 * p:2 * p + 2],
                PS[0:128, 3584:3586], op=AT.subtract), inc="VV")

            r.wait("tensor", "VV", bias_vv)
            r.wait("tensor", "SS", ss_tsc)
            r.wait("scalar", "VV", bias_vv)
            r.wait("vector", "GG", msk_gg0)
            av_hist = {0: {}, 1: {}}
            stt_vv = {}
            exp_ss = {}
            avfree_vv = 0
            evac_done = {0: 0, 1: 0}
            pending_norm = [None]
            pending_evac = [None]

            def emit_evac(part=None):
                s0, KT0 = pending_evac[0]
                avb = 4 + 2 * (s0 % 2)
                hs = range(2) if part is None else [part]
                for hh in hs:
                    r.wait("vector", "TT", av_hist[hh][KT0 - 1])
                    r.emit("vector", lambda e, hh=hh, avb=avb: e.tensor_copy(
                        t_av[64 * hh:64 * hh + 64, 0:512],
                        PS[0:64, (avb + hh) * 512:(avb + hh) * 512 + 512]),
                        inc="VV")
                    evac_done[s0 % 2] = r.emit(
                        "vector", lambda e, hh=hh, avb=avb: e.tensor_copy(
                            t_av[32 * hh:32 * hh + 1, 512:1024],
                            PS[64:65, (avb + hh) * 512:(avb + hh) * 512 + 512]),
                        inc="VV")
                if part != 0:
                    pending_evac[0] = None

            def emit_rec(half=None):
                s0, todo, rec = pending_norm[0]
                cols = [(512, 1024)] if half is None else \
                    [(512 + half * 256, 768 + half * 256)]
                if rec is None or half == 1:
                    for c0, c1 in cols:
                        rec = r.emit("vector", lambda e, c0=c0, c1=c1:
                                     e.reciprocal(
                            t_av[0:33, c0:c1], t_av[0:33, c0:c1]), inc="VV")
                    if half != 0:
                        pending_norm[0] = (s0, todo, rec)

            def flush_norm(one=False):
                if pending_norm[0] is None:
                    return None
                emit_rec()
                s0, todo, rec = pending_norm[0]
                hs = todo[:1] if one else todo
                nv = None
                for hh in hs:
                    jc, rb = divmod(2 * p + hh, 2)
                    rb *= 64
                    rrow = 32 * hh
                    bcb = 4 + 2 * (s0 % 2) + hh
                    r.wait("tensor", "VV", rec)
                    r.wait("tensor", "VV", bc_last[bcb])
                    ttn = r.emit("tensor", lambda e, bcb=bcb, rrow=rrow: e.matmul(
                        BK(bcb, 64, 512), t_ones[rrow:rrow + 1, 0:64],
                        t_av[rrow:rrow + 1, 512:1024],
                        start=True, stop=True), inc="TT")
                    r.wait("vector", "TT", ttn)
                    bc_last[bcb] = r.emit("vector", lambda e, bcb=bcb, rb=rb:
                                          e.tensor_copy(
                        t_sm[rb:rb + 64, 0:512], BK(bcb, 64, 512)), inc="VV")
                    nv = r.emit("vector", lambda e, hh=hh, jc=jc, rb=rb,
                                s0=s0: e.tensor_tensor(
                        A[rb:rb + 64,
                          SA + jc * TQ + s0 * 512:SA + jc * TQ + s0 * 512 + 512],
                        t_av[64 * hh:64 * hh + 64, 0:512],
                        t_sm[rb:rb + 64, 0:512], op=AT.mult), inc="VV")
                rest = todo[len(hs):]
                pending_norm[0] = (s0, rest, rec) if rest else None
                return nv

            def do_av(s, q0, sk, hh):
                sp0p = max(q0, sk * 128)
                wp = q0 + 512 - sk * 128 if sk >= 4 * s else 512
                pb_i = 3 * hh + sk % 3
                avb = 4 + 2 * (s % 2)
                av_hist[hh][sk] = r.emit(
                    "tensor", lambda e, hh=hh, sk=sk, sp0p=sp0p, wp=wp, q0=q0,
                    pb_i=pb_i, avb=avb: e.matmul(
                        PS[0:65, (avb + hh) * 512 + sp0p - q0:
                           (avb + hh) * 512 + sp0p - q0 + wp],
                        A[0:128, O_V + sk * 132 + hh * 66:
                          O_V + sk * 132 + hh * 66 + 65],
                        A[:, O_PB + pb_i * 512:O_PB + pb_i * 512 + wp],
                        start=(sk == 0), stop=(sk == 4 * s + 3)), inc="TT")

            for s in range(NSEG):
                q0 = s * 512
                KT = 4 * (s + 1)
                for kt in range(KT):
                    ktb = kt * 128
                    diag = kt >= 4 * s
                    span0 = max(q0, ktb)
                    w = q0 + 512 - ktb if diag else 512
                    j0 = span0 - ktb
                    if kt in (4, 5):
                        flush_norm(one=True)
                    r.wait("tensor", "VV", max(stt_vv.get((0, kt % 2), 0),
                                               stt_vv.get((1, kt % 2), 0)))
                    if kt >= 2:
                        r.wait("tensor", "SS", max(exp_ss.get((0, kt - 2), 0),
                                                   exp_ss.get((1, kt - 2), 0)))
                        if kt == 2:
                            avb = 4 + 2 * (s % 2)
                            r.wait("tensor", "VV", max(evac_done[s % 2],
                                                       bc_last[avb],
                                                       bc_last[avb + 1]))
                    qk_tt = None
                    for hh in range(2):
                        qk_bk = (kt % 2) + 2 * hh
                        rlo = 64 * hh
                        qk_tt = r.emit("tensor", lambda e, qk_bk=qk_bk, w=w,
                                       rlo=rlo, ktb=ktb, span0=span0: e.matmul(
                            BK(qk_bk, 128, w),
                            A[rlo:rlo + 64, O_K + ktb:O_K + ktb + 128],
                            A[rlo:rlo + 64, O_Q + span0:O_Q + span0 + w],
                            start=True, stop=True), inc="TT")
                    if kt >= 2:
                        for hh in range(2):
                            do_av(s, q0, kt - 2, hh)
                    for hh in range(2):
                        qk_bk = (kt % 2) + 2 * hh
                        pb_i = 3 * hh + kt % 3
                        r.wait("vector", "TT", qk_tt)
                        if kt >= 3:
                            r.wait("vector", "TT", av_hist[hh].get(kt - 3, 0))
                        stt_vv[(hh, kt % 2)] = r.emit(
                            "vector", lambda e, hh=hh, j0=j0, w=w, qk_bk=qk_bk,
                            pb_i=pb_i: e.tensor_tensor(
                                A[:, O_PB + pb_i * 512:O_PB + pb_i * 512 + w],
                                A[:, O_TSC + hh * TQ + j0:O_TSC + hh * TQ + j0 + w],
                                BK(qk_bk, 128, w), op=AT.add), inc="VV")
                        if diag:
                            stt_vv[(hh, kt % 2)] = r.emit(
                                "vector", lambda e, pb_i=pb_i: e.tensor_tensor(
                                    A[:, O_PB + pb_i * 512:
                                      O_PB + pb_i * 512 + 128],
                                    A[:, O_PB + pb_i * 512:
                                      O_PB + pb_i * 512 + 128],
                                    A[:, SC:SC + 128], op=AT.add), inc="VV")
                        r.wait("scalar", "VV", stt_vv[(hh, kt % 2)])
                        exp_ss[(hh, kt)] = r.emit(
                            "scalar", lambda e, w=w, pb_i=pb_i, hh=hh: e.activation(
                                A[:, O_PB + pb_i * 512:O_PB + pb_i * 512 + w],
                                A[:, O_PB + pb_i * 512:O_PB + pb_i * 512 + w],
                                AF.Exp, bias=t_st[:, 6 + hh:7 + hh]), inc="SS")
                    if pending_evac[0] is not None and kt in (0, 1):
                        emit_evac(part=kt)
                    if pending_norm[0] is not None and kt in (2, 3) \
                            and pending_evac[0] is None:
                        emit_rec(half=kt - 2)
                r.wait("tensor", "SS", max(exp_ss.get((0, KT - 2), 0),
                                           exp_ss.get((1, KT - 2), 0)))
                for hh in range(2):
                    do_av(s, q0, KT - 2, hh)
                r.wait("tensor", "SS", max(exp_ss.get((0, KT - 1), 0),
                                           exp_ss.get((1, KT - 1), 0)))
                for hh in range(2):
                    do_av(s, q0, KT - 1, hh)
                pending_evac[0] = (s, KT)
                pending_norm[0] = (s, [0, 1], None)
            emit_evac()
            attn_done_vv = flush_norm()
            last_exp_ss = r.cnt["SS"]

        r.barrier()

        def dump(dst, slot, nchunk, width):
            sf = 0
            for ch in range(nchunk):
                sf = r.emit("gpsimd", lambda e, ch=ch, dst=dst, slot=slot,
                            width=width: e.dma_start(
                    out=dst[ch * 128:ch * 128 + 128, :],
                    in_=A[:, slot + ch * width:slot + ch * width + width]),
                    inc="SO", by=16)
            r.wait("gpsimd", "SO", sf)

        if debug:
            dump(g_attn, SA, 4, TQ)
        ldp = None
        for jc in range(4):
            ldp = r.emit("sync", lambda e, jc=jc: e.dma_start(
                out=A[:, O_WPJ + jc * 1024:O_WPJ + jc * 1024 + 1024],
                in_=d_wproj[jc * 128:jc * 128 + 128, :]), inc="LD", by=16)
        r.wait("tensor", "LD", ldp)
        ttf = None
        for cs in range(8):
            for jc in range(4):
                ttf = r.emit("tensor", lambda e, cs=cs, jc=jc: e.matmul(
                    PS[0:128, 1024 + cs:1025 + cs],
                    A[:, O_WPJ + jc * 1024 + cs * 128:O_WPJ + jc * 1024 + cs * 128 + 128],
                    t_bc[:, 8 + jc:9 + jc],
                    start=(cs == 0 and jc == 0), stop=(cs == 7 and jc == 3)),
                    inc="TT")
        r.wait("vector", "TT", ttf)
        vvf = r.emit("vector", lambda e: e.tensor_copy(
            t_wf[:, 0:8], PS[0:128, 1024:1032]), inc="VV")
        last_pt_vv = {0: 0, 1: 0}
        st_hist = {}
        st_so = 0
        for idx in range(64):
            cs, s8 = divmod(idx, NSEG)
            ppb = idx % 2
            r.wait("tensor", "VV", vvf)
            r.wait("tensor", "VV", last_pt_vv[ppb])
            ttp = None
            for jc in range(4):
                ttp = r.emit("tensor", lambda e, cs=cs, s8=s8, jc=jc, ppb=ppb:
                             e.matmul(
                    BK(ppb),
                    A[:, O_WPJ + jc * 1024 + cs * 128:O_WPJ + jc * 1024 + cs * 128 + 128],
                    A[:, SA + jc * TQ + s8 * 512:SA + jc * TQ + s8 * 512 + 512],
                    start=(jc == 0), stop=(jc == 3)), inc="TT")
            r.wait("vector", "TT", ttp)
            if idx >= 2:
                r.wait("vector", "S%d" % (idx % 2), st_hist.get(idx - 2, 0))
            pt = A[:, O_PT + ppb * 512:O_PT + ppb * 512 + 512]
            last_pt_vv[ppb] = r.emit("vector", lambda e, ppb=ppb, cs=cs, pt=pt:
                                     e.tensor_scalar(
                pt, BK(ppb), t_wf[:, cs:cs + 1], None, op0=AT.add), inc="VV")
            r.wait("gpsimd", "VV", last_pt_vv[ppb])
            sh, s4 = divmod(s8, 4)
            st_so = r.emit("gpsimd", lambda e, sh=sh, cs=cs, s4=s4, pt=pt:
                           e.dma_start(
                out=h_cin[sh * 1024 + cs * 128:sh * 1024 + cs * 128 + 128,
                          s4 * 512:s4 * 512 + 512], in_=pt), inc="SO", by=16)
            r.emit("gpsimd", lambda e: e.engine_nop(), inc="S%d" % (idx % 2))
            st_hist[idx] = r.cnt["S%d" % (idx % 2)]
        r.wait("gpsimd", "SO", st_so)
        gg_cc = r.emit("gpsimd", lambda e: e.collective_compute(
            "ReduceScatter", AT.add, replica_groups=[[0, 1], [2, 3], [4, 5], [6, 7]],
            ins=[h_cin.ap().opt()], outs=[h_cout.ap().opt()]), inc="GG")

        ttp_last = r.cnt["TT"]
        r.barrier()

        r.wait("sync", "GG", gg_cc)
        evv_hist = {}

        def e_chunk(cc):
            lb = A[:, O_LD + (cc % 2) * 2048:O_LD + (cc % 2) * 2048 + 2048]
            xb = A[:, O_XQ + (cc % 2) * 2048:O_XQ + (cc % 2) * 2048 + 2048]
            if cc >= 2:
                r.wait("sync", "VV", evv_hist[cc - 2])
            r.emit("sync", lambda e, cc=cc, lb=lb: e.dma_start(
                out=lb, in_=h_cout[cc * 128:cc * 128 + 128, :]),
                inc="S%d" % (cc % 2), by=16)
            ldc = r.emit("sync", lambda e, cc=cc, xb=xb: e.dma_start(
                out=xb, in_=d_xq[cc * 128:cc * 128 + 128, :]),
                inc="S%d" % (cc % 2), by=16)
            r.wait("vector", "S%d" % (cc % 2), ldc)
            evv_hist[cc] = r.emit("vector", lambda e, cc=cc, lb=lb, xb=xb:
                                  e.scalar_tensor_tensor(
                A[:, SH + cc * 2048:SH + cc * 2048 + 2048], lb,
                t_bc[:, 12 + cc:13 + cc], xb, op0=AT.add, op1=AT.add), inc="VV")

        r.wait("gpsimd", "TT", ttp_last)
        ld_pref = None
        for e4 in range(4):
            for cc in range(8):
                ld_pref = r.emit("gpsimd", lambda e, e4=e4, cc=cc: e.dma_start(
                    out=A[:, O_KEY + (e4 * 8 + cc) * 128:
                          O_KEY + (e4 * 8 + cc) * 128 + 128],
                    in_=d_keys[e4 * 1024 + cc * 128:e4 * 1024 + cc * 128 + 128,
                               :]), inc="PF", by=16)
        for e4 in range(4):
            ld_pref = r.emit("gpsimd", lambda e, e4=e4: e.dma_start(
                out=A[:, O_VAL + e4 * 1024:O_VAL + e4 * 1024 + 1024],
                in_=d_vals[e4 * 128:e4 * 128 + 128, :]), inc="PF", by=16)
        for cc in range(8):
            e_chunk(cc)

        r.barrier()

        def layernorm(src, dst):
            sq_hist = {}
            out_vv = None

            def stats(s4):
                for cc in range(8):
                    it = s4 * 8 + cc
                    sqb = A[:, O_SQ + (cc % 2) * 512:O_SQ + (cc % 2) * 512 + 512]
                    if it >= 2:
                        r.wait("scalar", "TT", sq_hist[it - 2])
                    ss = r.emit("scalar", lambda e, s4=s4, cc=cc, sqb=sqb,
                                src=src: e.activation(
                                    sqb,
                                    A[:, src + cc * 2048 + s4 * 512:
                                      src + cc * 2048 + s4 * 512 + 512],
                                    AF.Square), inc="SS")
                    r.wait("tensor", "SS", ss)
                    r.emit("tensor", lambda e, s4=s4, cc=cc, src=src: e.matmul(
                        PS[0:1, s4 * 512:s4 * 512 + 512], t_ones[:, 0:1],
                        A[:, src + cc * 2048 + s4 * 512:
                          src + cc * 2048 + s4 * 512 + 512],
                        start=(cc == 0), stop=(cc == 7)), inc="TT")
                    sq_hist[it] = r.emit("tensor", lambda e, s4=s4, cc=cc,
                                         sqb=sqb: e.matmul(
                        PS[0:1, 2048 + s4 * 512:2048 + s4 * 512 + 512],
                        t_ones[:, 0:1], sqb,
                        start=(cc == 0), stop=(cc == 7)), inc="TT")

            def chain_apply(s4):
                nonlocal out_vv
                R0 = lambda o: A[0:1, o + s4 * 512:o + s4 * 512 + 512]
                r.wait("vector", "TT", sq_hist[s4 * 8 + 7])
                r.emit("vector", lambda e, R0=R0, s4=s4: e.tensor_scalar(
                    R0(ST_M), PS[0:1, s4 * 512:s4 * 512 + 512], 1.0 / C, None,
                    op0=AT.mult), inc="VV")
                r.emit("vector", lambda e, R0=R0, s4=s4: e.tensor_scalar(
                    R0(ST_V), PS[0:1, 2048 + s4 * 512:2048 + s4 * 512 + 512],
                    1.0 / C, None, op0=AT.mult), inc="VV")
                r.emit("vector", lambda e, R0=R0: e.tensor_tensor(
                    R0(ST_T), R0(ST_M), R0(ST_M), op=AT.mult), inc="VV")
                r.emit("vector", lambda e, R0=R0: e.tensor_tensor(
                    R0(ST_V), R0(ST_V), R0(ST_T), op=AT.subtract), inc="VV")
                vvv = r.emit("vector", lambda e, R0=R0: e.tensor_scalar(
                    R0(ST_V), R0(ST_V), EPS, None, op0=AT.add), inc="VV")
                r.wait("scalar", "VV", vvv)
                ssv = r.emit("scalar", lambda e, R0=R0: e.activation(
                    R0(ST_T), R0(ST_V), AF.Sqrt), inc="SS")
                r.wait("vector", "SS", ssv)
                r.emit("vector", lambda e, R0=R0: e.reciprocal(
                    R0(ST_R), R0(ST_T)), inc="VV")
                for _ in range(2):
                    r.emit("vector", lambda e, R0=R0: e.tensor_tensor(
                        R0(ST_T), R0(ST_R), R0(ST_R), op=AT.mult), inc="VV")
                    r.emit("vector", lambda e, R0=R0: e.tensor_tensor(
                        R0(ST_T), R0(ST_T), R0(ST_V), op=AT.mult), inc="VV")
                    r.emit("vector", lambda e, R0=R0: e.tensor_scalar(
                        R0(ST_T), R0(ST_T), -0.5, 1.5, op0=AT.mult, op1=AT.add),
                        inc="VV")
                    r.emit("vector", lambda e, R0=R0: e.tensor_tensor(
                        R0(ST_R), R0(ST_R), R0(ST_T), op=AT.mult), inc="VV")
                r.emit("vector", lambda e, R0=R0: e.tensor_tensor(
                    R0(ST_B), R0(ST_M), R0(ST_R), op=AT.mult), inc="VV")
                vvb = r.emit("vector", lambda e, R0=R0: e.tensor_scalar(
                    R0(ST_B), R0(ST_B), -1.0, None, op0=AT.mult), inc="VV")
                r.wait("tensor", "VV", vvb)
                tta = r.emit("tensor", lambda e, s4=s4: e.matmul(
                    PS[0:128, s4 * 512:s4 * 512 + 512], t_ones[0:1, 0:128],
                    A[0:1, ST_R + s4 * 512:ST_R + s4 * 512 + 512],
                    start=True, stop=True), inc="TT")
                ttb2 = r.emit("tensor", lambda e, s4=s4: e.matmul(
                    PS[0:128, 2048 + s4 * 512:2048 + s4 * 512 + 512],
                    t_ones[0:1, 0:128],
                    A[0:1, ST_B + s4 * 512:ST_B + s4 * 512 + 512],
                    start=True, stop=True), inc="TT")
                r.wait("vector", "TT", tta)
                vva = r.emit("vector", lambda e, s4=s4: e.tensor_copy(
                    A[:, SC + s4 * 512:SC + s4 * 512 + 512],
                    PS[0:128, s4 * 512:s4 * 512 + 512]), inc="VV")
                r.wait("vector", "TT", ttb2)
                for cc in range(8):
                    r.emit("vector", lambda e, cc=cc, s4=s4, src=src, dst=dst:
                           e.tensor_tensor(
                        A[:, dst + cc * 2048 + s4 * 512:
                          dst + cc * 2048 + s4 * 512 + 512],
                        A[:, src + cc * 2048 + s4 * 512:
                          src + cc * 2048 + s4 * 512 + 512],
                        A[:, SC + s4 * 512:SC + s4 * 512 + 512], op=AT.mult),
                        inc="VV")
                    out_vv = r.emit("vector", lambda e, cc=cc, s4=s4, dst=dst:
                                    e.tensor_tensor(
                        A[:, dst + cc * 2048 + s4 * 512:
                          dst + cc * 2048 + s4 * 512 + 512],
                        A[:, dst + cc * 2048 + s4 * 512:
                          dst + cc * 2048 + s4 * 512 + 512],
                        PS[0:128, 2048 + s4 * 512:2048 + s4 * 512 + 512],
                        op=AT.add), inc="VV")

            stats(0)
            for s4 in range(4):
                if s4 < 3:
                    stats(s4 + 1)
                chain_apply(s4)
            return out_vv

        if debug:
            dump(g_h1, SH, 8, OWN)
        layernorm(SH, SA)
        r.barrier()
        if debug:
            dump(g_x2, SA, 8, OWN)

        tts = None
        for s4 in range(4):
            for cc in range(8):
                tts = r.emit("tensor", lambda e, s4=s4, cc=cc: e.matmul(
                    PS[0:16, s4 * 512:s4 * 512 + 512],
                    t_selT[:, cc * 16:cc * 16 + 16],
                    A[:, SA + cc * 2048 + s4 * 512:SA + cc * 2048 + s4 * 512 + 512],
                    start=(cc == 0), stop=(cc == 7)), inc="TT")
        r.wait("scalar", "TT", tts)
        ss_g = r.emit("scalar", lambda e: e.activation(
            A[0:16, O_G:O_G + 2048], PS[0:16, 0:2048], AF.Sigmoid), inc="SS")
        r.wait("vector", "SS", ss_g)
        r.wait("sync", "SS", ss_g)
        ld_oh = r.emit("sync", lambda e: e.dma_start(
            out=A[0:16, O_OH:O_OH + 2048], in_=d_oh[:, :]), inc="LD", by=16)

        hb_last = {0: 0, 1: 0}
        fb_last = {4: 0, 5: 0}
        g2_last = 0
        q_tt = 0
        r.wait("tensor", "SS", ss_g)
        hid_done_tt = 0
        ldk = 0
        for q4 in range(4):
            if q4 == 0:
                r.wait("tensor", "PF", ld_pref)
            if q4 > 0:
                r.wait("sync", "TT", q_tt)
                for e4 in range(4):
                    eg = 4 * q4 + e4
                    ldk = r.emit("sync", lambda e, eg=eg, e4=e4: e.dma_start(
                        out=A[:, O_VAL + e4 * 1024:O_VAL + e4 * 1024 + 1024],
                        in_=d_vals[eg * 128:eg * 128 + 128, :]), inc="LD", by=16)
            for s4 in range(4):
                for e4 in range(4):
                    eg = 4 * q4 + e4
                    hb = e4 % 2
                    r.wait("tensor", "LD", ldk)
                    r.wait("tensor", "VV", hb_last[hb])
                    tth = None
                    for cc in range(8):
                        tth = r.emit("tensor", lambda e, e4=e4, cc=cc, s4=s4, hb=hb:
                                     e.matmul(
                            BK(hb), A[:, O_KEY + (e4 * 8 + cc) * 128:
                                      O_KEY + (e4 * 8 + cc) * 128 + 128],
                            A[:, SA + cc * 2048 + s4 * 512:
                              SA + cc * 2048 + s4 * 512 + 512],
                            start=(cc == 0), stop=(cc == 7)), inc="TT")
                    hid_done_tt = tth
                    if s4 == 3 and q4 < 3:
                        eg2 = 4 * (q4 + 1) + e4
                        r.wait("sync", "TT", tth)
                        for cc in range(8):
                            ldk = r.emit("sync", lambda e, eg2=eg2, e4=e4,
                                         cc=cc: e.dma_start(
                                out=A[:, O_KEY + (e4 * 8 + cc) * 128:
                                      O_KEY + (e4 * 8 + cc) * 128 + 128],
                                in_=d_keys[eg2 * 1024 + cc * 128:
                                           eg2 * 1024 + cc * 128 + 128,
                                           :]), inc="LD", by=16)
                    r.wait("tensor", "LD", ld_oh)
                    r.wait("tensor", "VV", g2_last)
                    ttg = r.emit("tensor", lambda e, eg=eg, s4=s4: e.matmul(
                        BK(2), A[0:16, O_OH + eg * 128:O_OH + eg * 128 + 128],
                        A[0:16, O_G + s4 * 512:O_G + s4 * 512 + 512],
                        start=True, stop=True), inc="TT")
                    r.wait("vector", "TT", ttg)
                    g2_last = r.emit("vector", lambda e: e.tensor_copy(
                        A[:, O_GES:O_GES + 512], BK(2)), inc="VV")
                    r.wait("vector", "TT", tth)
                    ht_vv = r.emit("vector", lambda e, e4=e4, hb=hb:
                                   e.scalar_tensor_tensor(
                        A[:, O_HT + e4 * 512:O_HT + e4 * 512 + 512], BK(hb),
                        0.0, A[:, O_GES:O_GES + 512],
                        op0=AT.max, op1=AT.mult), inc="VV")
                    hb_last[hb] = ht_vv
                for cs in range(8):
                    fb = 4 + cs % 2
                    r.wait("tensor", "VV", ht_vv)
                    r.wait("tensor", "VV", fb_last[fb])
                    ttv = None
                    for e4 in range(4):
                        ttv = r.emit("tensor", lambda e, e4=e4, cs=cs, fb=fb:
                                     e.matmul(
                            BK(fb), A[:, O_VAL + e4 * 1024 + cs * 128:
                                      O_VAL + e4 * 1024 + cs * 128 + 128],
                            A[:, O_HT + e4 * 512:O_HT + e4 * 512 + 512],
                            start=(e4 == 0), stop=(e4 == 3)), inc="TT")
                    q_tt = ttv
                    r.wait("vector", "TT", ttv)
                    src_slot = SA if q4 == 0 else SH
                    fb_last[fb] = r.emit(
                        "vector", lambda e, cs=cs, s4=s4, fb=fb, ss=src_slot:
                        e.tensor_tensor(
                            A[:, SH + cs * 2048 + s4 * 512:
                              SH + cs * 2048 + s4 * 512 + 512],
                            BK(fb),
                            A[:, ss + cs * 2048 + s4 * 512:
                              ss + cs * 2048 + s4 * 512 + 512],
                            op=AT.add), inc="VV")

        r.barrier()
        if debug:
            dump(g_f, SH, 8, OWN)
        layernorm(SH, SA)
        r.barrier()

        tr_vv = {6: 0, 7: 0}
        st_f = 0
        for t16 in range(16):
            for cc in range(8):
                trb = 6 + (t16 * 8 + cc) % 2
                r.wait("tensor", "VV", tr_vv[trb])
                ttt = r.emit("tensor", lambda e, t16=t16, cc=cc, trb=trb:
                             e.transpose(
                    BK(trb, 128, 128),
                    A[:, SA + cc * 2048 + t16 * 128:SA + cc * 2048 + t16 * 128 + 128],
                    t_ident[:, :]), inc="TT")
                r.wait("vector", "TT", ttt)
                tr_vv[trb] = r.emit("vector", lambda e, t16=t16, cc=cc, trb=trb:
                                    e.tensor_copy(
                    A[:, SH + t16 * 1024 + cc * 128:SH + t16 * 1024 + cc * 128 + 128],
                    BK(trb, 128, 128)), inc="VV")
            r.wait("gpsimd", "VV", tr_vv[6])
            r.wait("gpsimd", "VV", tr_vv[7])
            st_f = r.emit("gpsimd", lambda e, t16=t16: e.dma_start(
                out=d_out[t16 * 128:t16 * 128 + 128, :],
                in_=A[:, SH + t16 * 1024:SH + t16 * 1024 + 1024]), inc="SO", by=16)
        r.wait("gpsimd", "SO", st_f)

        with nc.Block() as block:
            block.sync(lambda e: [f(e) for f in r.ops["sync"]])
            block.tensor(lambda e: [f(e) for f in r.ops["tensor"]])
            block.vector(lambda e: [f(e) for f in r.ops["vector"]])
            block.scalar(lambda e: [f(e) for f in r.ops["scalar"]])
            block.gpsimd(lambda e: [f(e) for f in r.ops["gpsimd"]])
    return nc


_NC_CACHE = None


def kernel(L0_attn_x, L0_attn_W_attn, L0_attn_b_attn, L0_attn_W_proj,
           L0_attn_b_proj, L0_ffn_expert_sel, L0_ffn_keys, L0_ffn_values):
    global _NC_CACHE
    x = np.asarray(L0_attn_x, np.float32)
    W_attn = np.asarray(L0_attn_W_attn, np.float32)
    b_attn = np.asarray(L0_attn_b_attn, np.float32)
    W_proj = np.asarray(L0_attn_W_proj, np.float32)
    b_proj = np.asarray(L0_attn_b_proj, np.float32)
    sel = np.asarray(L0_ffn_expert_sel, np.float32)
    keys = np.asarray(L0_ffn_keys, np.float32)
    values = np.asarray(L0_ffn_values, np.float32)

    ident = np.eye(128, dtype=np.float32)
    ones = np.ones((128, 128), np.float32)
    bones = np.zeros((128, 128), np.float32)
    for j in range(128):
        if j % 2 == 0:
            bones[0:64, j] = 1.0
        else:
            bones[64:128, j] = 1.0
    tj = np.broadcast_to(np.arange(TQ, dtype=np.float32), (128, TQ)).copy()
    selT = np.ascontiguousarray(sel.T)
    keysR = np.ascontiguousarray(keys.reshape(16 * C, 128))
    valuesR = np.ascontiguousarray(values.reshape(16 * 128, C))
    parts = np.arange(128, dtype=np.float32)
    oh16 = np.kron(np.eye(16, dtype=np.float32), np.ones((1, 128), np.float32))

    in_maps = []
    for c in range(8):
        b, hg = divmod(c, 2)
        heads = np.arange(hg * 8, hg * 8 + 8)
        slopes = (0.5 ** (heads + 1)).astype(np.float32)
        xT = np.ascontiguousarray(x[b].T)
        xq = np.ascontiguousarray(xT[:, hg * OWN:(hg + 1) * OWN])
        wqkv = np.ascontiguousarray(np.concatenate(
            [W_attn[:, 512 * hg:512 * hg + 512],
             W_attn[:, 1024 + 512 * hg:1024 + 512 * hg + 512],
             W_attn[:, 2048 + 512 * hg:2048 + 512 * hg + 512]], axis=1))
        wprojmy = np.ascontiguousarray(W_proj[512 * hg:512 * hg + 512, :])
        bcols = np.zeros((128, 20), np.float32)
        for p in range(4):
            bcols[:, p] = b_attn[512 * hg + 128 * p:512 * hg + 128 * p + 128]
            bcols[:, 4 + p] = b_attn[1024 + 512 * hg + 128 * p:
                                     1024 + 512 * hg + 128 * p + 128]
        bv_my = b_attn[2048 + 512 * hg:2048 + 512 * hg + 512]
        for jc in range(4):
            bcols[:, 8 + jc] = bv_my[jc * 128:jc * 128 + 128]
        for cc in range(8):
            bcols[:, 12 + cc] = b_proj[cc * 128:cc * 128 + 128]
        negslope = np.broadcast_to(-slopes[None, :], (128, 8)).copy()
        biasbase = (parts[:, None] * slopes[None, :]).astype(np.float32)
        in_maps.append(dict(
            xT=xT, xq=xq, wqkv=wqkv, wprojmy=wprojmy, bcols=bcols,
            negslope=negslope, biasbase=biasbase, selT=selT, keysR=keysR,
            valuesR=valuesR, tj=tj, identc=ident, onesc=ones, blockonesc=bones,
            oh16=oh16))

    if _NC_CACHE is None:
        _NC_CACHE = build_nc()
    res = run_bass_kernel_spmd(_NC_CACHE, in_maps, core_ids=list(range(8)))
    y = np.empty((4, TQ, C), np.float32)
    for c in range(8):
        b, hg = divmod(c, 2)
        y[b, hg * OWN:(hg + 1) * OWN, :] = res.results[c]["out"]
    return y


# version 31
# speedup vs baseline: 1.0388x; 1.0049x over previous
import numpy as np
import concourse.bass as bass
import concourse.mybir as mybir
from concourse.bass_utils import run_bass_kernel_spmd
from contextlib import ExitStack

F32 = mybir.dt.float32
BF16 = mybir.dt.bfloat16
AT = mybir.AluOpType
AF = mybir.ActivationFunctionType
AX = mybir.AxisListType

C = 1024
TQ = 4096
HL = 8
NP = 4
SEG = 512
NSEG = 8
OWN = 2048
EPS = 1e-5

SA = 0
SH = 16384
SB = 32768
SC = 48768
FARENA = 50816
O_WQ, O_WK, O_WV, O_TSC = SH, SH + 1024, SH + 2048, SH + 3072
O_TJ = SH + 11264
O_Q, O_K, O_V = SB, SB + 4096, SB + 8192
O_PB, O_VT, O_XT = SB + 12416, SB + 14464, SB + 14976
O_PT = SC
O_WPJ = SB + 1024
O_LD, O_XQ = SB + 8192, SB + 12288
O_KEY, O_VAL, O_HT = SB, SB + 4096, SB + 8192
O_GES, O_G, O_SQ = SB + 10240, SB + 10752, SB + 12800
O_OH = SB + 12800
ST_M, ST_V, ST_R, ST_T, ST_B = (SA + 0, SA + 2048, SA + 4096, SA + 6144,
                                SA + 8192)


class Rec:
    ENGS = ("sync", "tensor", "vector", "scalar", "gpsimd")

    def __init__(self, nc, ctx):
        self.nc = nc
        self.ops = {e: [] for e in self.ENGS}
        self.sems = {}
        self.cnt = {}
        self.lastwait = {}
        for name in ("LD", "PF", "TT", "VV", "SS", "GG", "SO",
                     "S0", "S1", "S2", "S3", "S4", "S5"):
            self.sems[name] = ctx.enter_context(nc.semaphore(name))
            self.cnt[name] = 0

    def emit(self, eng, fn, inc=None, by=1):
        if inc is None:
            self.ops[eng].append(fn)
            return None
        self.cnt[inc] += by
        sem = self.sems[inc]
        self.ops[eng].append(lambda e, f=fn, s=sem, b=by: f(e).then_inc(s, b))
        return self.cnt[inc]

    def wait(self, eng, sname, val):
        if not val:
            return
        key = (eng, sname)
        if self.lastwait.get(key, -1) >= val:
            return
        self.lastwait[key] = val
        sem = self.sems[sname]
        self.ops[eng].append(lambda e, s=sem, v=val: e.wait_ge(s, v))

    def barrier(self):
        for sname in ("LD", "TT", "VV", "SS", "GG", "SO"):
            v = self.cnt[sname]
            for eng in self.ENGS:
                self.wait(eng, sname, v)


def build_nc(debug=False):
    nc = bass.Bass(trn_type="TRN2", target_bir_lowering=False)

    d_xT = nc.dram_tensor("xT", [C, TQ], F32, kind="ExternalInput").ap()
    d_xq = nc.dram_tensor("xq", [C, OWN], F32, kind="ExternalInput").ap()
    d_wqkv = nc.dram_tensor("wqkv", [C, 1536], F32, kind="ExternalInput").ap()
    d_wproj = nc.dram_tensor("wprojmy", [512, C], F32, kind="ExternalInput").ap()
    d_bcols = nc.dram_tensor("bcols", [128, 20], F32, kind="ExternalInput").ap()
    d_nslope = nc.dram_tensor("negslope", [128, 8], F32, kind="ExternalInput").ap()
    d_bbase = nc.dram_tensor("biasbase", [128, 8], F32, kind="ExternalInput").ap()
    d_selT = nc.dram_tensor("selT", [C, 16], F32, kind="ExternalInput").ap()
    d_keys = nc.dram_tensor("keysR", [16 * C, 128], F32, kind="ExternalInput").ap()
    d_vals = nc.dram_tensor("valuesR", [2048, C], F32, kind="ExternalInput").ap()
    d_tj = nc.dram_tensor("tj", [128, TQ], F32, kind="ExternalInput").ap()
    d_ident = nc.dram_tensor("identc", [128, 128], F32, kind="ExternalInput").ap()
    d_ones = nc.dram_tensor("onesc", [128, 128], F32, kind="ExternalInput").ap()
    d_bones = nc.dram_tensor("blockonesc", [128, 128], F32, kind="ExternalInput").ap()
    d_oh = nc.dram_tensor("oh16", [16, 2048], F32, kind="ExternalInput").ap()
    d_out = nc.dram_tensor("out", [OWN, C], F32, kind="ExternalOutput").ap()
    if debug:
        g_attn = nc.dram_tensor("dbg_attn", [512, TQ], F32, kind="ExternalOutput").ap()
        g_h1 = nc.dram_tensor("dbg_h1", [C, OWN], F32, kind="ExternalOutput").ap()
        g_x2 = nc.dram_tensor("dbg_x2", [C, OWN], F32, kind="ExternalOutput").ap()
        g_f = nc.dram_tensor("dbg_f", [C, OWN], F32, kind="ExternalOutput").ap()
        g_qk = nc.dram_tensor("dbg_qk", [128, 8192], F32, kind="ExternalOutput").ap()
        g_v = nc.dram_tensor("dbg_v", [128, 4224], F32, kind="ExternalOutput").ap()
    h_cin = nc.dram_tensor("cc_in", [2048, OWN], F32)
    h_cout = nc.dram_tensor("cc_out", [C, OWN], F32)

    with ExitStack() as ctx:
        A = ctx.enter_context(nc.sbuf_tensor("arena", [128, FARENA], F32))
        t_ident = ctx.enter_context(nc.sbuf_tensor("identS", [128, 128], F32))
        t_ones = ctx.enter_context(nc.sbuf_tensor("onesS", [128, 128], F32))
        t_bones = ctx.enter_context(nc.sbuf_tensor("bonesS", [128, 128], F32))
        t_nsl = ctx.enter_context(nc.sbuf_tensor("nslS", [128, 8], F32))
        t_bb = ctx.enter_context(nc.sbuf_tensor("bbS", [128, 8], F32))
        t_bc = ctx.enter_context(nc.sbuf_tensor("bcS", [128, 20], F32))
        t_selT = ctx.enter_context(nc.sbuf_tensor("selTS", [128, 128], F32))
        t_sm = ctx.enter_context(nc.sbuf_tensor("smallS", [128, 512], F32))
        t_av = ctx.enter_context(nc.sbuf_tensor("avstgS", [128, 1024], F32))
        t_st = ctx.enter_context(nc.sbuf_tensor("statS", [128, 16], F32))
        t_wf = ctx.enter_context(nc.sbuf_tensor("wfoldS", [128, 8], F32))
        PS = ctx.enter_context(nc.psum_tensor("ps", [128, 4096], F32))
        r = Rec(nc, ctx)

        def BK(i, rows=128, w=512, off=0):
            return PS[0:rows, i * 512 + off:i * 512 + off + w]

        ld = None
        for dst, src in ((t_ident[:, :], d_ident), (t_ones[:, :], d_ones),
                         (t_bones[:, :], d_bones), (t_nsl[:, :], d_nslope),
                         (t_bb[:, :], d_bbase), (t_bc[:, :], d_bcols),
                         (A[:, O_TJ:O_TJ + TQ], d_tj)):
            ld = r.emit("sync", lambda e, d=dst, s=src: e.dma_start(out=d, in_=s[:, :]),
                        inc="LD", by=16)
        for cc in range(8):
            ld = r.emit("sync", lambda e, cc=cc: e.dma_start(
                out=t_selT[:, cc * 16:cc * 16 + 16],
                in_=d_selT[cc * 128:cc * 128 + 128, :]), inc="LD", by=16)
        ld_consts = ld
        vv_ones = None
        for base in (64, 130):
            ap_ones = bass.AP(A, O_V + base, [[FARENA, 128], [132, 32]])
            vv_ones = r.emit("vector", lambda e, a=ap_ones: e.memset(a, 1.0),
                             inc="VV")
        vv_m0 = r.emit("vector", lambda e: e.memset(A[:, SC:SC + 128], 0.0),
                       inc="VV")
        r.wait("gpsimd", "VV", vv_m0)
        msk_gg0 = r.emit("gpsimd", lambda e: e.affine_select(
            A[:, SC:SC + 128], A[:, SC:SC + 128],
            [[1, 128]], AT.is_ge, -1e9, channel_multiplier=-1), inc="GG")

        attn_done_vv = vv_ones
        last_exp_ss = 0
        trp_vv = {6: 0, 7: 0}
        bc_last = {4: 0, 5: 0, 6: 0, 7: 0}
        prod_done_tt = 0
        for p in range(NP):
            r.wait("sync", "TT", prod_done_tt)
            ld_w = 0
            for off, cb in ((O_WQ, p * 128), (O_WK, 512 + p * 128),
                            (O_WV, 1024 + p * 128)):
                for cc in range(8):
                    ld_w = r.emit("sync", lambda e, cc=cc, cb=cb, off=off:
                                  e.dma_start(
                        out=A[:, off + cc * 128:off + cc * 128 + 128],
                        in_=d_wqkv[cc * 128:cc * 128 + 128, cb:cb + 128]),
                        inc="S2", by=16)
            r.wait("scalar", "LD", ld_consts)
            r.wait("scalar", "VV", attn_done_vv)
            ss_tsc = None
            for hh in range(2):
                ss_tsc = r.emit("scalar", lambda e, hh=hh, h=2 * p + hh: e.mul(
                    A[:, O_TSC + hh * TQ:O_TSC + (hh + 1) * TQ],
                    A[:, O_TJ:O_TJ + TQ], t_nsl[:, h:h + 1]), inc="SS")

            tile_tt = {}
            seg_vv = {}
            first_prod = True
            trq = []

            def emit_tr():
                tseg0, ii, vvt0 = trq.pop(0)
                ti = tseg0 * 4 + ii
                trb = 6 + ii % 2
                r.wait("tensor", "VV", vvt0)
                r.wait("tensor", "VV", trp_vv[trb])
                tt2 = r.emit("tensor", lambda e, ii=ii, trb=trb: e.transpose(
                    BK(trb, 128, 128),
                    A[:, O_VT + ii * 128:O_VT + ii * 128 + 128],
                    t_ident[:, :]), inc="TT")
                r.wait("vector", "TT", tt2)
                r.emit("vector", lambda e, ti=ti, trb=trb: e.tensor_copy(
                    A[:, O_V + ti * 132:O_V + ti * 132 + 64],
                    BK(trb, 128, 64)), inc="VV")
                vv2 = r.emit("vector", lambda e, ti=ti, trb=trb: e.tensor_copy(
                    A[:, O_V + ti * 132 + 66:O_V + ti * 132 + 130],
                    BK(trb, 128, 64, off=64)), inc="VV")
                trp_vv[trb] = vv2
                return vv2

            attn_guard_tt = r.cnt["TT"]
            for i in range(64):
                tseg, cc = divmod(i, 8)
                xtb32 = A[:, O_XT + (i % 2) * 512:O_XT + (i % 2) * 512 + 512]
                if i == 0:
                    r.wait("sync", "TT", attn_guard_tt)
                if i >= 2:
                    r.wait("sync", "TT", tile_tt[i - 2])
                ldx = r.emit("sync", lambda e, xtb32=xtb32, cc=cc, t=tseg:
                             e.dma_start(
                    out=xtb32,
                    in_=d_xT[cc * 128:cc * 128 + 128, t * 512:t * 512 + 512]),
                    inc="S%d" % (i % 2), by=16)
                r.wait("tensor", "S%d" % (i % 2), ldx)
                if cc == 0:
                    r.wait("tensor", "S2", ld_w)
                    if first_prod:
                        r.wait("tensor", "VV", attn_done_vv)
                        first_prod = False
                    if tseg >= 2:
                        r.wait("tensor", "VV", seg_vv[tseg - 2])
                kb, vb, qb = tseg % 2, 2 + tseg % 2, 4 + tseg % 2
                tt = None
                for off, bk in ((O_WK, kb), (O_WV, vb), (O_WQ, qb)):
                    tt = r.emit("tensor", lambda e, off=off, cc=cc, bk=bk,
                                xtb32=xtb32: e.matmul(
                        BK(bk), A[:, off + cc * 128:off + cc * 128 + 128],
                        xtb32, start=(cc == 0), stop=(cc == 7)), inc="TT")
                tile_tt[i] = tt
                if trq and cc < 4:
                    emit_tr()
                if cc == 7:
                    r.wait("vector", "TT", tt)
                    r.emit("vector", lambda e, t=tseg, kb=kb, p=p: e.tensor_scalar(
                        A[:, O_K + t * 512:O_K + t * 512 + 512], BK(kb),
                        t_bc[:, 4 + p:5 + p], None, op0=AT.add), inc="VV")
                    r.emit("vector", lambda e, t=tseg, qb=qb, p=p: e.tensor_scalar(
                        A[:, O_Q + t * 512:O_Q + t * 512 + 512], BK(qb),
                        t_bc[:, p:p + 1], 0.125, op0=AT.add, op1=AT.mult), inc="VV")
                    vvt = r.emit("vector", lambda e, vb=vb: e.tensor_copy(
                        A[:, O_VT:O_VT + 512], BK(vb)), inc="VV")
                    for ii in range(4):
                        trq.append((tseg, ii, vvt))
                    seg_vv[tseg] = vvt
            while trq:
                seg_vv[7] = emit_tr()
            prod_done_tt = r.cnt["TT"]

            if debug and p == 3:
                r.wait("gpsimd", "VV", seg_vv[7])
                r.emit("gpsimd", lambda e: e.dma_start(
                    out=g_qk[:, 0:4096], in_=A[:, O_Q:O_Q + 4096]), inc="SO", by=16)
                r.emit("gpsimd", lambda e: e.dma_start(
                    out=g_qk[:, 4096:8192], in_=A[:, O_K:O_K + 4096]), inc="SO", by=16)
                sfd = r.emit("gpsimd", lambda e: e.dma_start(
                    out=g_v[:, :], in_=A[:, O_V:O_V + 4224]), inc="SO", by=16)
                r.wait("gpsimd", "SO", sfd)

            r.wait("vector", "SS", last_exp_ss if p > 0 else 0)
            bsum_hist = {}
            bsum_tt = 0
            red_vv = 0
            for which, src_off in ((0, O_Q), (1, O_K)):
                for t in range(NSEG):
                    it = which * NSEG + t
                    sqb = A[:, O_PB + (t % 2) * 512:O_PB + (t % 2) * 512 + 512]
                    r.wait("scalar", "VV", seg_vv[7])
                    if it >= 2:
                        r.wait("scalar", "TT", bsum_hist[it - 2])
                    ss = r.emit("scalar", lambda e, t=t, so=src_off, sqb=sqb:
                                e.activation(sqb,
                                             A[:, so + t * 512:so + t * 512 + 512],
                                             AF.Square), inc="SS")
                    r.wait("tensor", "SS", ss)
                    if which == 0 and t == 0:
                        r.wait("tensor", "VV", seg_vv[7])
                    r.wait("tensor", "VV", red_vv)
                    bsum_tt = r.emit("tensor", lambda e, sqb=sqb: e.matmul(
                        BK(0, 128, 512), t_bones[:, :], sqb, start=True, stop=True),
                        inc="TT")
                    bsum_hist[it] = bsum_tt
                    r.wait("vector", "TT", bsum_tt)
                    if t == 0:
                        red_vv = r.emit("vector", lambda e, dc=which: e.reduce_max(
                            t_st[0:2, dc:dc + 1], BK(0, 2, 512), axis=AX.X),
                            inc="VV")
                    else:
                        red_vv = r.emit("vector", lambda e: e.reduce_max(
                            t_st[0:2, 8:9], BK(0, 2, 512), axis=AX.X), inc="VV")
                        r.emit("vector", lambda e, dc=which: e.tensor_tensor(
                            t_st[0:2, dc:dc + 1], t_st[0:2, dc:dc + 1],
                            t_st[0:2, 8:9], op=AT.max), inc="VV")
            vvB = r.emit("vector", lambda e: e.tensor_tensor(
                t_st[0:2, 2:3], t_st[0:2, 0:1], t_st[0:2, 1:2], op=AT.mult),
                inc="VV")
            r.wait("scalar", "VV", vvB)
            ssB = r.emit("scalar", lambda e: e.activation(
                t_st[0:2, 3:4], t_st[0:2, 2:3], AF.Sqrt), inc="SS")
            r.wait("vector", "SS", ssB)
            vvB2 = r.emit("vector", lambda e: e.tensor_scalar(
                t_st[0:2, 4:5], t_st[0:2, 3:4], 1.005, 1.0,
                op0=AT.mult, op1=AT.add), inc="VV")
            r.wait("tensor", "VV", seg_vv[7])
            r.wait("tensor", "VV", vvB2)
            tr_tt = r.emit("tensor", lambda e: e.matmul(
                PS[0:1, 3072:3074], t_st[0:2, 4:5], t_ident[0:2, 0:2],
                start=True, stop=True), inc="TT")
            r.wait("vector", "TT", tr_tt)
            vvc = r.emit("vector", lambda e: e.tensor_copy(
                t_st[0:1, 9:11], PS[0:1, 3072:3074]), inc="VV")
            bc_last[6] = vvc
            r.wait("tensor", "VV", vvc)
            ttb = 0
            for hh in range(2):
                ttb = r.emit("tensor", lambda e, hh=hh: e.matmul(
                    PS[0:128, 3584 + hh:3585 + hh], t_ones[0:1, 0:128],
                    t_st[0:1, 9 + hh:10 + hh], start=(hh == 0), stop=(hh == 1)),
                    inc="TT")
            r.wait("vector", "TT", ttb)
            bias_vv = r.emit("vector", lambda e, p=p: e.tensor_tensor(
                t_st[0:128, 6:8], t_bb[:, 2 * p:2 * p + 2],
                PS[0:128, 3584:3586], op=AT.subtract), inc="VV")

            r.wait("tensor", "VV", bias_vv)
            r.wait("tensor", "SS", ss_tsc)
            r.wait("scalar", "VV", bias_vv)
            r.wait("vector", "GG", msk_gg0)
            av_hist = {0: {}, 1: {}}
            stt_vv = {}
            exp_ss = {}
            avfree_vv = 0
            evac_done = {0: 0, 1: 0}
            pending_norm = [None]
            pending_evac = [None]

            def emit_evac(part=None):
                s0, KT0 = pending_evac[0]
                avb = 4 + 2 * (s0 % 2)
                hs = range(2) if part is None else [part]
                for hh in hs:
                    r.wait("vector", "TT", av_hist[hh][KT0 - 1])
                    r.emit("vector", lambda e, hh=hh, avb=avb: e.tensor_copy(
                        t_av[64 * hh:64 * hh + 64, 0:512],
                        PS[0:64, (avb + hh) * 512:(avb + hh) * 512 + 512]),
                        inc="VV")
                    evac_done[s0 % 2] = r.emit(
                        "vector", lambda e, hh=hh, avb=avb: e.tensor_copy(
                            t_av[32 * hh:32 * hh + 1, 512:1024],
                            PS[64:65, (avb + hh) * 512:(avb + hh) * 512 + 512]),
                        inc="VV")
                if part != 0:
                    pending_evac[0] = None

            def emit_rec(half=None):
                s0, todo, rec = pending_norm[0]
                cols = [(512, 1024)] if half is None else \
                    [(512 + half * 256, 768 + half * 256)]
                if rec is None or half == 1:
                    for c0, c1 in cols:
                        rec = r.emit("vector", lambda e, c0=c0, c1=c1:
                                     e.reciprocal(
                            t_av[0:33, c0:c1], t_av[0:33, c0:c1]), inc="VV")
                    if half != 0:
                        pending_norm[0] = (s0, todo, rec)

            def flush_norm(one=False):
                if pending_norm[0] is None:
                    return None
                emit_rec()
                s0, todo, rec = pending_norm[0]
                hs = todo[:1] if one else todo
                nv = None
                for hh in hs:
                    jc, rb = divmod(2 * p + hh, 2)
                    rb *= 64
                    rrow = 32 * hh
                    bcb = 4 + 2 * (s0 % 2) + hh
                    r.wait("tensor", "VV", rec)
                    r.wait("tensor", "VV", bc_last[bcb])
                    ttn = r.emit("tensor", lambda e, bcb=bcb, rrow=rrow: e.matmul(
                        BK(bcb, 64, 512), t_ones[rrow:rrow + 1, 0:64],
                        t_av[rrow:rrow + 1, 512:1024],
                        start=True, stop=True), inc="TT")
                    r.wait("vector", "TT", ttn)
                    bc_last[bcb] = r.emit("vector", lambda e, bcb=bcb, rb=rb:
                                          e.tensor_copy(
                        t_sm[rb:rb + 64, 0:512], BK(bcb, 64, 512)), inc="VV")
                    nv = r.emit("vector", lambda e, hh=hh, jc=jc, rb=rb,
                                s0=s0: e.tensor_tensor(
                        A[rb:rb + 64,
                          SA + jc * TQ + s0 * 512:SA + jc * TQ + s0 * 512 + 512],
                        t_av[64 * hh:64 * hh + 64, 0:512],
                        t_sm[rb:rb + 64, 0:512], op=AT.mult), inc="VV")
                rest = todo[len(hs):]
                pending_norm[0] = (s0, rest, rec) if rest else None
                return nv

            def do_av(s, q0, sk, hh):
                sp0p = max(q0, sk * 128)
                wp = q0 + 512 - sk * 128 if sk >= 4 * s else 512
                pb_i = 3 * hh + sk % 3
                avb = 4 + 2 * (s % 2)
                av_hist[hh][sk] = r.emit(
                    "tensor", lambda e, hh=hh, sk=sk, sp0p=sp0p, wp=wp, q0=q0,
                    pb_i=pb_i, avb=avb: e.matmul(
                        PS[0:65, (avb + hh) * 512 + sp0p - q0:
                           (avb + hh) * 512 + sp0p - q0 + wp],
                        A[0:128, O_V + sk * 132 + hh * 66:
                          O_V + sk * 132 + hh * 66 + 65],
                        A[:, O_PB + pb_i * 512:O_PB + pb_i * 512 + wp],
                        start=(sk == 0), stop=(sk == 4 * s + 3)), inc="TT")

            for s in range(NSEG):
                q0 = s * 512
                KT = 4 * (s + 1)
                for kt in range(KT):
                    ktb = kt * 128
                    diag = kt >= 4 * s
                    span0 = max(q0, ktb)
                    w = q0 + 512 - ktb if diag else 512
                    j0 = span0 - ktb
                    if kt in (4, 5):
                        flush_norm(one=True)
                    r.wait("tensor", "VV", max(stt_vv.get((0, kt % 2), 0),
                                               stt_vv.get((1, kt % 2), 0)))
                    if kt >= 2:
                        r.wait("tensor", "SS", max(exp_ss.get((0, kt - 2), 0),
                                                   exp_ss.get((1, kt - 2), 0)))
                        if kt == 2:
                            avb = 4 + 2 * (s % 2)
                            r.wait("tensor", "VV", max(evac_done[s % 2],
                                                       bc_last[avb],
                                                       bc_last[avb + 1]))
                    qk_tt = None
                    for hh in range(2):
                        qk_bk = (kt % 2) + 2 * hh
                        rlo = 64 * hh
                        qk_tt = r.emit("tensor", lambda e, qk_bk=qk_bk, w=w,
                                       rlo=rlo, ktb=ktb, span0=span0: e.matmul(
                            BK(qk_bk, 128, w),
                            A[rlo:rlo + 64, O_K + ktb:O_K + ktb + 128],
                            A[rlo:rlo + 64, O_Q + span0:O_Q + span0 + w],
                            start=True, stop=True), inc="TT")
                    if kt >= 2:
                        for hh in range(2):
                            do_av(s, q0, kt - 2, hh)
                    for hh in range(2):
                        qk_bk = (kt % 2) + 2 * hh
                        pb_i = 3 * hh + kt % 3
                        r.wait("vector", "TT", qk_tt)
                        if kt >= 3:
                            r.wait("vector", "TT", av_hist[hh].get(kt - 3, 0))
                        stt_vv[(hh, kt % 2)] = r.emit(
                            "vector", lambda e, hh=hh, j0=j0, w=w, qk_bk=qk_bk,
                            pb_i=pb_i: e.tensor_tensor(
                                A[:, O_PB + pb_i * 512:O_PB + pb_i * 512 + w],
                                A[:, O_TSC + hh * TQ + j0:O_TSC + hh * TQ + j0 + w],
                                BK(qk_bk, 128, w), op=AT.add), inc="VV")
                        if diag:
                            stt_vv[(hh, kt % 2)] = r.emit(
                                "vector", lambda e, pb_i=pb_i: e.tensor_tensor(
                                    A[:, O_PB + pb_i * 512:
                                      O_PB + pb_i * 512 + 128],
                                    A[:, O_PB + pb_i * 512:
                                      O_PB + pb_i * 512 + 128],
                                    A[:, SC:SC + 128], op=AT.add), inc="VV")
                        r.wait("scalar", "VV", stt_vv[(hh, kt % 2)])
                        exp_ss[(hh, kt)] = r.emit(
                            "scalar", lambda e, w=w, pb_i=pb_i, hh=hh: e.activation(
                                A[:, O_PB + pb_i * 512:O_PB + pb_i * 512 + w],
                                A[:, O_PB + pb_i * 512:O_PB + pb_i * 512 + w],
                                AF.Exp, bias=t_st[:, 6 + hh:7 + hh]), inc="SS")
                    if pending_evac[0] is not None and kt in (0, 1):
                        emit_evac(part=kt)
                    if pending_norm[0] is not None and kt in (2, 3) \
                            and pending_evac[0] is None:
                        emit_rec(half=kt - 2)
                r.wait("tensor", "SS", max(exp_ss.get((0, KT - 2), 0),
                                           exp_ss.get((1, KT - 2), 0)))
                for hh in range(2):
                    do_av(s, q0, KT - 2, hh)
                r.wait("tensor", "SS", max(exp_ss.get((0, KT - 1), 0),
                                           exp_ss.get((1, KT - 1), 0)))
                for hh in range(2):
                    do_av(s, q0, KT - 1, hh)
                pending_evac[0] = (s, KT)
                pending_norm[0] = (s, [0, 1], None)
            emit_evac()
            attn_done_vv = flush_norm()
            last_exp_ss = r.cnt["SS"]

        r.barrier()

        def dump(dst, slot, nchunk, width):
            sf = 0
            for ch in range(nchunk):
                sf = r.emit("gpsimd", lambda e, ch=ch, dst=dst, slot=slot,
                            width=width: e.dma_start(
                    out=dst[ch * 128:ch * 128 + 128, :],
                    in_=A[:, slot + ch * width:slot + ch * width + width]),
                    inc="SO", by=16)
            r.wait("gpsimd", "SO", sf)

        if debug:
            dump(g_attn, SA, 4, TQ)
        ldp = None
        for jc in range(4):
            ldp = r.emit("sync", lambda e, jc=jc: e.dma_start(
                out=A[:, O_WPJ + jc * 1024:O_WPJ + jc * 1024 + 1024],
                in_=d_wproj[jc * 128:jc * 128 + 128, :]), inc="LD", by=16)
        r.wait("tensor", "LD", ldp)
        ttf = None
        for cs in range(8):
            for jc in range(4):
                ttf = r.emit("tensor", lambda e, cs=cs, jc=jc: e.matmul(
                    PS[0:128, 1024 + cs:1025 + cs],
                    A[:, O_WPJ + jc * 1024 + cs * 128:O_WPJ + jc * 1024 + cs * 128 + 128],
                    t_bc[:, 8 + jc:9 + jc],
                    start=(cs == 0 and jc == 0), stop=(cs == 7 and jc == 3)),
                    inc="TT")
        r.wait("vector", "TT", ttf)
        vvf = r.emit("vector", lambda e: e.tensor_copy(
            t_wf[:, 0:8], PS[0:128, 1024:1032]), inc="VV")
        last_pt_vv = {0: 0, 1: 0}
        st_hist = {}
        st_so = 0
        for idx in range(64):
            cs, s8 = divmod(idx, NSEG)
            ppb = idx % 2
            r.wait("tensor", "VV", vvf)
            r.wait("tensor", "VV", last_pt_vv[ppb])
            ttp = None
            for jc in range(4):
                ttp = r.emit("tensor", lambda e, cs=cs, s8=s8, jc=jc, ppb=ppb:
                             e.matmul(
                    BK(ppb),
                    A[:, O_WPJ + jc * 1024 + cs * 128:O_WPJ + jc * 1024 + cs * 128 + 128],
                    A[:, SA + jc * TQ + s8 * 512:SA + jc * TQ + s8 * 512 + 512],
                    start=(jc == 0), stop=(jc == 3)), inc="TT")
            r.wait("vector", "TT", ttp)
            if idx >= 2:
                r.wait("vector", "S%d" % (idx % 2), st_hist.get(idx - 2, 0))
            pt = A[:, O_PT + ppb * 512:O_PT + ppb * 512 + 512]
            last_pt_vv[ppb] = r.emit("vector", lambda e, ppb=ppb, cs=cs, pt=pt:
                                     e.tensor_scalar(
                pt, BK(ppb), t_wf[:, cs:cs + 1], None, op0=AT.add), inc="VV")
            r.wait("gpsimd", "VV", last_pt_vv[ppb])
            sh, s4 = divmod(s8, 4)
            st_so = r.emit("gpsimd", lambda e, sh=sh, cs=cs, s4=s4, pt=pt:
                           e.dma_start(
                out=h_cin[sh * 1024 + cs * 128:sh * 1024 + cs * 128 + 128,
                          s4 * 512:s4 * 512 + 512], in_=pt),
                inc="S%d" % (idx % 2), by=16)
            st_hist[idx] = st_so
        r.wait("gpsimd", "S0", st_hist[62])
        r.wait("gpsimd", "S1", st_hist[63])
        gg_cc = r.emit("gpsimd", lambda e: e.collective_compute(
            "ReduceScatter", AT.add, replica_groups=[[0, 1], [2, 3], [4, 5], [6, 7]],
            ins=[h_cin.ap().opt()], outs=[h_cout.ap().opt()]), inc="GG")

        ttp_last = r.cnt["TT"]
        r.barrier()

        r.wait("sync", "GG", gg_cc)
        evv_hist = {}

        def e_chunk(cc):
            lb = A[:, O_LD + (cc % 2) * 2048:O_LD + (cc % 2) * 2048 + 2048]
            xb = A[:, O_XQ + (cc % 2) * 2048:O_XQ + (cc % 2) * 2048 + 2048]
            if cc >= 2:
                r.wait("sync", "VV", evv_hist[cc - 2])
            r.emit("sync", lambda e, cc=cc, lb=lb: e.dma_start(
                out=lb, in_=h_cout[cc * 128:cc * 128 + 128, :]),
                inc="S%d" % (cc % 2), by=16)
            ldc = r.emit("sync", lambda e, cc=cc, xb=xb: e.dma_start(
                out=xb, in_=d_xq[cc * 128:cc * 128 + 128, :]),
                inc="S%d" % (cc % 2), by=16)
            r.wait("vector", "S%d" % (cc % 2), ldc)
            evv_hist[cc] = r.emit("vector", lambda e, cc=cc, lb=lb, xb=xb:
                                  e.scalar_tensor_tensor(
                A[:, SH + cc * 2048:SH + cc * 2048 + 2048], lb,
                t_bc[:, 12 + cc:13 + cc], xb, op0=AT.add, op1=AT.add), inc="VV")

        r.wait("gpsimd", "TT", ttp_last)
        ld_pref = None
        for e4 in range(4):
            for cc in range(8):
                ld_pref = r.emit("gpsimd", lambda e, e4=e4, cc=cc: e.dma_start(
                    out=A[:, O_KEY + (e4 * 8 + cc) * 128:
                          O_KEY + (e4 * 8 + cc) * 128 + 128],
                    in_=d_keys[e4 * 1024 + cc * 128:e4 * 1024 + cc * 128 + 128,
                               :]), inc="PF", by=16)
        for e4 in range(4):
            ld_pref = r.emit("gpsimd", lambda e, e4=e4: e.dma_start(
                out=A[:, O_VAL + e4 * 1024:O_VAL + e4 * 1024 + 1024],
                in_=d_vals[e4 * 128:e4 * 128 + 128, :]), inc="PF", by=16)
        for cc in range(8):
            e_chunk(cc)

        r.barrier()

        def layernorm(src, dst, after_apply=None):
            sq_hist = {}
            out_vv = None

            def stats(s4):
                for cc in range(8):
                    it = s4 * 8 + cc
                    sqb = A[:, O_SQ + (cc % 2) * 512:O_SQ + (cc % 2) * 512 + 512]
                    if it >= 2:
                        r.wait("scalar", "TT", sq_hist[it - 2])
                    ss = r.emit("scalar", lambda e, s4=s4, cc=cc, sqb=sqb,
                                src=src: e.activation(
                                    sqb,
                                    A[:, src + cc * 2048 + s4 * 512:
                                      src + cc * 2048 + s4 * 512 + 512],
                                    AF.Square), inc="SS")
                    r.wait("tensor", "SS", ss)
                    r.emit("tensor", lambda e, s4=s4, cc=cc, src=src: e.matmul(
                        PS[0:1, s4 * 512:s4 * 512 + 512], t_ones[:, 0:1],
                        A[:, src + cc * 2048 + s4 * 512:
                          src + cc * 2048 + s4 * 512 + 512],
                        start=(cc == 0), stop=(cc == 7)), inc="TT")
                    sq_hist[it] = r.emit("tensor", lambda e, s4=s4, cc=cc,
                                         sqb=sqb: e.matmul(
                        PS[0:1, 2048 + s4 * 512:2048 + s4 * 512 + 512],
                        t_ones[:, 0:1], sqb,
                        start=(cc == 0), stop=(cc == 7)), inc="TT")

            def chain_apply(s4):
                nonlocal out_vv
                R0 = lambda o: A[0:1, o + s4 * 512:o + s4 * 512 + 512]
                r.wait("vector", "TT", sq_hist[s4 * 8 + 7])
                r.emit("vector", lambda e, R0=R0, s4=s4: e.tensor_scalar(
                    R0(ST_M), PS[0:1, s4 * 512:s4 * 512 + 512], 1.0 / C, None,
                    op0=AT.mult), inc="VV")
                r.emit("vector", lambda e, R0=R0, s4=s4: e.tensor_scalar(
                    R0(ST_V), PS[0:1, 2048 + s4 * 512:2048 + s4 * 512 + 512],
                    1.0 / C, None, op0=AT.mult), inc="VV")
                r.emit("vector", lambda e, R0=R0: e.tensor_tensor(
                    R0(ST_T), R0(ST_M), R0(ST_M), op=AT.mult), inc="VV")
                r.emit("vector", lambda e, R0=R0: e.tensor_tensor(
                    R0(ST_V), R0(ST_V), R0(ST_T), op=AT.subtract), inc="VV")
                vvv = r.emit("vector", lambda e, R0=R0: e.tensor_scalar(
                    R0(ST_V), R0(ST_V), EPS, None, op0=AT.add), inc="VV")
                r.wait("scalar", "VV", vvv)
                ssv = r.emit("scalar", lambda e, R0=R0: e.activation(
                    R0(ST_T), R0(ST_V), AF.Sqrt), inc="SS")
                r.wait("vector", "SS", ssv)
                r.emit("vector", lambda e, R0=R0: e.reciprocal(
                    R0(ST_R), R0(ST_T)), inc="VV")
                for _ in range(2):
                    r.emit("vector", lambda e, R0=R0: e.tensor_tensor(
                        R0(ST_T), R0(ST_R), R0(ST_R), op=AT.mult), inc="VV")
                    r.emit("vector", lambda e, R0=R0: e.tensor_tensor(
                        R0(ST_T), R0(ST_T), R0(ST_V), op=AT.mult), inc="VV")
                    r.emit("vector", lambda e, R0=R0: e.tensor_scalar(
                        R0(ST_T), R0(ST_T), -0.5, 1.5, op0=AT.mult, op1=AT.add),
                        inc="VV")
                    r.emit("vector", lambda e, R0=R0: e.tensor_tensor(
                        R0(ST_R), R0(ST_R), R0(ST_T), op=AT.mult), inc="VV")
                r.emit("vector", lambda e, R0=R0: e.tensor_tensor(
                    R0(ST_B), R0(ST_M), R0(ST_R), op=AT.mult), inc="VV")
                vvb = r.emit("vector", lambda e, R0=R0: e.tensor_scalar(
                    R0(ST_B), R0(ST_B), -1.0, None, op0=AT.mult), inc="VV")
                r.wait("tensor", "VV", vvb)
                tta = r.emit("tensor", lambda e, s4=s4: e.matmul(
                    PS[0:128, s4 * 512:s4 * 512 + 512], t_ones[0:1, 0:128],
                    A[0:1, ST_R + s4 * 512:ST_R + s4 * 512 + 512],
                    start=True, stop=True), inc="TT")
                ttb2 = r.emit("tensor", lambda e, s4=s4: e.matmul(
                    PS[0:128, 2048 + s4 * 512:2048 + s4 * 512 + 512],
                    t_ones[0:1, 0:128],
                    A[0:1, ST_B + s4 * 512:ST_B + s4 * 512 + 512],
                    start=True, stop=True), inc="TT")
                r.wait("vector", "TT", tta)
                vva = r.emit("vector", lambda e, s4=s4: e.tensor_copy(
                    A[:, SC + s4 * 512:SC + s4 * 512 + 512],
                    PS[0:128, s4 * 512:s4 * 512 + 512]), inc="VV")
                r.wait("vector", "TT", ttb2)
                for cc in range(8):
                    r.emit("vector", lambda e, cc=cc, s4=s4, src=src, dst=dst:
                           e.tensor_tensor(
                        A[:, dst + cc * 2048 + s4 * 512:
                          dst + cc * 2048 + s4 * 512 + 512],
                        A[:, src + cc * 2048 + s4 * 512:
                          src + cc * 2048 + s4 * 512 + 512],
                        A[:, SC + s4 * 512:SC + s4 * 512 + 512], op=AT.mult),
                        inc="VV")
                    out_vv = r.emit("vector", lambda e, cc=cc, s4=s4, dst=dst:
                                    e.tensor_tensor(
                        A[:, dst + cc * 2048 + s4 * 512:
                          dst + cc * 2048 + s4 * 512 + 512],
                        A[:, dst + cc * 2048 + s4 * 512:
                          dst + cc * 2048 + s4 * 512 + 512],
                        PS[0:128, 2048 + s4 * 512:2048 + s4 * 512 + 512],
                        op=AT.add), inc="VV")

            stats(0)
            for s4 in range(4):
                if s4 < 3:
                    stats(s4 + 1)
                chain_apply(s4)
                if after_apply is not None:
                    after_apply(s4, out_vv)
            return out_vv

        if debug:
            dump(g_h1, SH, 8, OWN)
        g_tt = [0]

        def ln1_after(s4, vv):
            r.wait("tensor", "VV", vv)
            for cc in range(8):
                g_tt[0] = r.emit("tensor", lambda e, s4=s4, cc=cc: e.matmul(
                    PS[0:16, s4 * 512:s4 * 512 + 512],
                    t_selT[:, cc * 16:cc * 16 + 16],
                    A[:, SA + cc * 2048 + s4 * 512:SA + cc * 2048 + s4 * 512 + 512],
                    start=(cc == 0), stop=(cc == 7)), inc="TT")

        layernorm(SH, SA, after_apply=ln1_after)
        r.barrier()
        if debug:
            dump(g_x2, SA, 8, OWN)

        r.wait("scalar", "TT", g_tt[0])
        ss_g = r.emit("scalar", lambda e: e.activation(
            A[0:16, O_G:O_G + 2048], PS[0:16, 0:2048], AF.Sigmoid), inc="SS")
        r.wait("vector", "SS", ss_g)
        r.wait("sync", "SS", ss_g)
        ld_oh = r.emit("sync", lambda e: e.dma_start(
            out=A[0:16, O_OH:O_OH + 2048], in_=d_oh[:, :]), inc="LD", by=16)

        hb_last = {0: 0, 1: 0}
        fb_last = {4: 0, 5: 0}
        g2_last = 0
        q_tt = 0
        r.wait("tensor", "SS", ss_g)
        hid_done_tt = 0
        ldk = 0
        for q4 in range(4):
            if q4 == 0:
                r.wait("tensor", "PF", ld_pref)
            if q4 > 0:
                r.wait("sync", "TT", q_tt)
                for e4 in range(4):
                    eg = 4 * q4 + e4
                    ldk = r.emit("sync", lambda e, eg=eg, e4=e4: e.dma_start(
                        out=A[:, O_VAL + e4 * 1024:O_VAL + e4 * 1024 + 1024],
                        in_=d_vals[eg * 128:eg * 128 + 128, :]), inc="LD", by=16)
            for s4 in range(4):
                for e4 in range(4):
                    eg = 4 * q4 + e4
                    hb = e4 % 2
                    r.wait("tensor", "LD", ldk)
                    r.wait("tensor", "VV", hb_last[hb])
                    tth = None
                    for cc in range(8):
                        tth = r.emit("tensor", lambda e, e4=e4, cc=cc, s4=s4, hb=hb:
                                     e.matmul(
                            BK(hb), A[:, O_KEY + (e4 * 8 + cc) * 128:
                                      O_KEY + (e4 * 8 + cc) * 128 + 128],
                            A[:, SA + cc * 2048 + s4 * 512:
                              SA + cc * 2048 + s4 * 512 + 512],
                            start=(cc == 0), stop=(cc == 7)), inc="TT")
                    hid_done_tt = tth
                    if s4 == 3 and q4 < 3:
                        eg2 = 4 * (q4 + 1) + e4
                        r.wait("sync", "TT", tth)
                        for cc in range(8):
                            ldk = r.emit("sync", lambda e, eg2=eg2, e4=e4,
                                         cc=cc: e.dma_start(
                                out=A[:, O_KEY + (e4 * 8 + cc) * 128:
                                      O_KEY + (e4 * 8 + cc) * 128 + 128],
                                in_=d_keys[eg2 * 1024 + cc * 128:
                                           eg2 * 1024 + cc * 128 + 128,
                                           :]), inc="LD", by=16)
                    r.wait("tensor", "LD", ld_oh)
                    r.wait("tensor", "VV", g2_last)
                    ttg = r.emit("tensor", lambda e, eg=eg, s4=s4: e.matmul(
                        BK(2), A[0:16, O_OH + eg * 128:O_OH + eg * 128 + 128],
                        A[0:16, O_G + s4 * 512:O_G + s4 * 512 + 512],
                        start=True, stop=True), inc="TT")
                    r.wait("vector", "TT", ttg)
                    g2_last = r.emit("vector", lambda e: e.tensor_copy(
                        A[:, O_GES:O_GES + 512], BK(2)), inc="VV")
                    r.wait("vector", "TT", tth)
                    ht_vv = r.emit("vector", lambda e, e4=e4, hb=hb:
                                   e.scalar_tensor_tensor(
                        A[:, O_HT + e4 * 512:O_HT + e4 * 512 + 512], BK(hb),
                        0.0, A[:, O_GES:O_GES + 512],
                        op0=AT.max, op1=AT.mult), inc="VV")
                    hb_last[hb] = ht_vv
                for cs in range(8):
                    fb = 4 + cs % 2
                    r.wait("tensor", "VV", ht_vv)
                    r.wait("tensor", "VV", fb_last[fb])
                    ttv = None
                    for e4 in range(4):
                        ttv = r.emit("tensor", lambda e, e4=e4, cs=cs, fb=fb:
                                     e.matmul(
                            BK(fb), A[:, O_VAL + e4 * 1024 + cs * 128:
                                      O_VAL + e4 * 1024 + cs * 128 + 128],
                            A[:, O_HT + e4 * 512:O_HT + e4 * 512 + 512],
                            start=(e4 == 0), stop=(e4 == 3)), inc="TT")
                    q_tt = ttv
                    r.wait("vector", "TT", ttv)
                    src_slot = SA if q4 == 0 else SH
                    fb_last[fb] = r.emit(
                        "vector", lambda e, cs=cs, s4=s4, fb=fb, ss=src_slot:
                        e.tensor_tensor(
                            A[:, SH + cs * 2048 + s4 * 512:
                              SH + cs * 2048 + s4 * 512 + 512],
                            BK(fb),
                            A[:, ss + cs * 2048 + s4 * 512:
                              ss + cs * 2048 + s4 * 512 + 512],
                            op=AT.add), inc="VV")

        r.barrier()
        if debug:
            dump(g_f, SH, 8, OWN)

        tr_vv = {i: 0 for i in range(8)}
        gst_hist = {}

        def ln2_after(s4, vv):
            r.wait("tensor", "VV", vv)
            for t16 in range(4 * s4, 4 * s4 + 4):
                ob = SB + (t16 % 2) * 1024
                if t16 >= 2:
                    r.wait("vector", "S%d" % (t16 % 2), gst_hist[t16 - 2])
                for cc in range(8):
                    trb = s4 if (t16 * 8 + cc) % 2 == 0 else 4 + s4
                    r.wait("tensor", "VV", tr_vv[trb])
                    ttt = r.emit("tensor", lambda e, t16=t16, cc=cc, trb=trb:
                                 e.transpose(
                        BK(trb, 128, 128),
                        A[:, SA + cc * 2048 + t16 * 128:
                          SA + cc * 2048 + t16 * 128 + 128],
                        t_ident[:, :]), inc="TT")
                    r.wait("vector", "TT", ttt)
                    tr_vv[trb] = r.emit("vector", lambda e, ob=ob, cc=cc,
                                        trb=trb: e.tensor_copy(
                        A[:, ob + cc * 128:ob + cc * 128 + 128],
                        BK(trb, 128, 128)), inc="VV")
                r.wait("gpsimd", "VV", tr_vv[s4])
                r.wait("gpsimd", "VV", tr_vv[4 + s4])
                gst_hist[t16] = r.emit("gpsimd", lambda e, t16=t16, ob=ob:
                                       e.dma_start(
                    out=d_out[t16 * 128:t16 * 128 + 128, :],
                    in_=A[:, ob:ob + 1024]),
                    inc="S%d" % (t16 % 2), by=16)

        layernorm(SH, SA, after_apply=ln2_after)
        r.wait("gpsimd", "S0", gst_hist[14])
        r.wait("gpsimd", "S1", gst_hist[15])

        with nc.Block() as block:
            block.sync(lambda e: [f(e) for f in r.ops["sync"]])
            block.tensor(lambda e: [f(e) for f in r.ops["tensor"]])
            block.vector(lambda e: [f(e) for f in r.ops["vector"]])
            block.scalar(lambda e: [f(e) for f in r.ops["scalar"]])
            block.gpsimd(lambda e: [f(e) for f in r.ops["gpsimd"]])
    return nc


_NC_CACHE = None


def kernel(L0_attn_x, L0_attn_W_attn, L0_attn_b_attn, L0_attn_W_proj,
           L0_attn_b_proj, L0_ffn_expert_sel, L0_ffn_keys, L0_ffn_values):
    global _NC_CACHE
    x = np.asarray(L0_attn_x, np.float32)
    W_attn = np.asarray(L0_attn_W_attn, np.float32)
    b_attn = np.asarray(L0_attn_b_attn, np.float32)
    W_proj = np.asarray(L0_attn_W_proj, np.float32)
    b_proj = np.asarray(L0_attn_b_proj, np.float32)
    sel = np.asarray(L0_ffn_expert_sel, np.float32)
    keys = np.asarray(L0_ffn_keys, np.float32)
    values = np.asarray(L0_ffn_values, np.float32)

    ident = np.eye(128, dtype=np.float32)
    ones = np.ones((128, 128), np.float32)
    bones = np.zeros((128, 128), np.float32)
    for j in range(128):
        if j % 2 == 0:
            bones[0:64, j] = 1.0
        else:
            bones[64:128, j] = 1.0
    tj = np.broadcast_to(np.arange(TQ, dtype=np.float32), (128, TQ)).copy()
    selT = np.ascontiguousarray(sel.T)
    keysR = np.ascontiguousarray(keys.reshape(16 * C, 128))
    valuesR = np.ascontiguousarray(values.reshape(16 * 128, C))
    parts = np.arange(128, dtype=np.float32)
    oh16 = np.kron(np.eye(16, dtype=np.float32), np.ones((1, 128), np.float32))

    in_maps = []
    for c in range(8):
        b, hg = divmod(c, 2)
        heads = np.arange(hg * 8, hg * 8 + 8)
        slopes = (0.5 ** (heads + 1)).astype(np.float32)
        xT = np.ascontiguousarray(x[b].T)
        xq = np.ascontiguousarray(xT[:, hg * OWN:(hg + 1) * OWN])
        wqkv = np.ascontiguousarray(np.concatenate(
            [W_attn[:, 512 * hg:512 * hg + 512],
             W_attn[:, 1024 + 512 * hg:1024 + 512 * hg + 512],
             W_attn[:, 2048 + 512 * hg:2048 + 512 * hg + 512]], axis=1))
        wprojmy = np.ascontiguousarray(W_proj[512 * hg:512 * hg + 512, :])
        bcols = np.zeros((128, 20), np.float32)
        for p in range(4):
            bcols[:, p] = b_attn[512 * hg + 128 * p:512 * hg + 128 * p + 128]
            bcols[:, 4 + p] = b_attn[1024 + 512 * hg + 128 * p:
                                     1024 + 512 * hg + 128 * p + 128]
        bv_my = b_attn[2048 + 512 * hg:2048 + 512 * hg + 512]
        for jc in range(4):
            bcols[:, 8 + jc] = bv_my[jc * 128:jc * 128 + 128]
        for cc in range(8):
            bcols[:, 12 + cc] = b_proj[cc * 128:cc * 128 + 128]
        negslope = np.broadcast_to(-slopes[None, :], (128, 8)).copy()
        biasbase = (parts[:, None] * slopes[None, :]).astype(np.float32)
        in_maps.append(dict(
            xT=xT, xq=xq, wqkv=wqkv, wprojmy=wprojmy, bcols=bcols,
            negslope=negslope, biasbase=biasbase, selT=selT, keysR=keysR,
            valuesR=valuesR, tj=tj, identc=ident, onesc=ones, blockonesc=bones,
            oh16=oh16))

    if _NC_CACHE is None:
        _NC_CACHE = build_nc()
    res = run_bass_kernel_spmd(_NC_CACHE, in_maps, core_ids=list(range(8)))
    y = np.empty((4, TQ, C), np.float32)
    for c in range(8):
        b, hg = divmod(c, 2)
        y[b, hg * OWN:(hg + 1) * OWN, :] = res.results[c]["out"]
    return y


# version 32
# speedup vs baseline: 1.0476x; 1.0085x over previous
import numpy as np
import concourse.bass as bass
import concourse.mybir as mybir
from concourse.bass_utils import run_bass_kernel_spmd
from contextlib import ExitStack

F32 = mybir.dt.float32
BF16 = mybir.dt.bfloat16
AT = mybir.AluOpType
AF = mybir.ActivationFunctionType
AX = mybir.AxisListType

C = 1024
TQ = 4096
HL = 8
NP = 4
SEG = 512
NSEG = 8
OWN = 2048
EPS = 1e-5

SA = 0
SH = 16384
SB = 32768
SC = 48768
FARENA = 50816
O_WQ, O_WK, O_WV, O_TSC = SH, SH + 1024, SH + 2048, SH + 3072
O_TJ = SH + 11264
O_Q, O_K, O_V = SB, SB + 4096, SB + 8192
O_PB, O_VT, O_XT = SB + 12416, SB + 14464, SB + 14976
O_PT = SC
O_WPJ = SH + 11264
O_LD, O_XQ = SB + 8192, SB + 12288
O_KEY, O_VAL, O_HT = SB, SB + 4096, SB + 8192
O_GES, O_G, O_SQ = SB + 10240, SB + 10752, SB + 12800
O_OH = SB + 12800
ST_M, ST_V, ST_R, ST_T, ST_B = (SA + 0, SA + 2048, SA + 4096, SA + 6144,
                                SA + 8192)


class Rec:
    ENGS = ("sync", "tensor", "vector", "scalar", "gpsimd")

    def __init__(self, nc, ctx):
        self.nc = nc
        self.ops = {e: [] for e in self.ENGS}
        self.sems = {}
        self.cnt = {}
        self.lastwait = {}
        for name in ("LD", "PF", "TT", "VV", "SS", "GG", "SO",
                     "S0", "S1", "S2", "S3", "S4", "S5"):
            self.sems[name] = ctx.enter_context(nc.semaphore(name))
            self.cnt[name] = 0

    def emit(self, eng, fn, inc=None, by=1):
        if inc is None:
            self.ops[eng].append(fn)
            return None
        self.cnt[inc] += by
        sem = self.sems[inc]
        self.ops[eng].append(lambda e, f=fn, s=sem, b=by: f(e).then_inc(s, b))
        return self.cnt[inc]

    def wait(self, eng, sname, val):
        if not val:
            return
        key = (eng, sname)
        if self.lastwait.get(key, -1) >= val:
            return
        self.lastwait[key] = val
        sem = self.sems[sname]
        self.ops[eng].append(lambda e, s=sem, v=val: e.wait_ge(s, v))

    def barrier(self):
        for sname in ("LD", "TT", "VV", "SS", "GG", "SO"):
            v = self.cnt[sname]
            for eng in self.ENGS:
                self.wait(eng, sname, v)


def build_nc(debug=False):
    nc = bass.Bass(trn_type="TRN2", target_bir_lowering=False)

    d_xT = nc.dram_tensor("xT", [C, TQ], F32, kind="ExternalInput").ap()
    d_xq = nc.dram_tensor("xq", [C, OWN], F32, kind="ExternalInput").ap()
    d_wqkv = nc.dram_tensor("wqkv", [C, 1536], F32, kind="ExternalInput").ap()
    d_wproj = nc.dram_tensor("wprojmy", [512, C], F32, kind="ExternalInput").ap()
    d_bcols = nc.dram_tensor("bcols", [128, 20], F32, kind="ExternalInput").ap()
    d_nslope = nc.dram_tensor("negslope", [128, 8], F32, kind="ExternalInput").ap()
    d_bbase = nc.dram_tensor("biasbase", [128, 8], F32, kind="ExternalInput").ap()
    d_selT = nc.dram_tensor("selT", [C, 16], F32, kind="ExternalInput").ap()
    d_keys = nc.dram_tensor("keysR", [16 * C, 128], F32, kind="ExternalInput").ap()
    d_vals = nc.dram_tensor("valuesR", [2048, C], F32, kind="ExternalInput").ap()
    d_tj = nc.dram_tensor("tj", [128, TQ], F32, kind="ExternalInput").ap()
    d_ident = nc.dram_tensor("identc", [128, 128], F32, kind="ExternalInput").ap()
    d_ones = nc.dram_tensor("onesc", [128, 128], F32, kind="ExternalInput").ap()
    d_bones = nc.dram_tensor("blockonesc", [128, 128], F32, kind="ExternalInput").ap()
    d_oh = nc.dram_tensor("oh16", [16, 2048], F32, kind="ExternalInput").ap()
    d_out = nc.dram_tensor("out", [OWN, C], F32, kind="ExternalOutput").ap()
    if debug:
        g_attn = nc.dram_tensor("dbg_attn", [512, TQ], F32, kind="ExternalOutput").ap()
        g_h1 = nc.dram_tensor("dbg_h1", [C, OWN], F32, kind="ExternalOutput").ap()
        g_x2 = nc.dram_tensor("dbg_x2", [C, OWN], F32, kind="ExternalOutput").ap()
        g_f = nc.dram_tensor("dbg_f", [C, OWN], F32, kind="ExternalOutput").ap()
        g_qk = nc.dram_tensor("dbg_qk", [128, 8192], F32, kind="ExternalOutput").ap()
        g_v = nc.dram_tensor("dbg_v", [128, 4224], F32, kind="ExternalOutput").ap()
    h_cin = nc.dram_tensor("cc_in", [2048, OWN], F32)
    h_cout = nc.dram_tensor("cc_out", [C, OWN], F32)

    with ExitStack() as ctx:
        A = ctx.enter_context(nc.sbuf_tensor("arena", [128, FARENA], F32))
        t_ident = ctx.enter_context(nc.sbuf_tensor("identS", [128, 128], F32))
        t_ones = ctx.enter_context(nc.sbuf_tensor("onesS", [128, 128], F32))
        t_bones = ctx.enter_context(nc.sbuf_tensor("bonesS", [128, 128], F32))
        t_nsl = ctx.enter_context(nc.sbuf_tensor("nslS", [128, 8], F32))
        t_bb = ctx.enter_context(nc.sbuf_tensor("bbS", [128, 8], F32))
        t_bc = ctx.enter_context(nc.sbuf_tensor("bcS", [128, 20], F32))
        t_selT = ctx.enter_context(nc.sbuf_tensor("selTS", [128, 128], F32))
        t_sm = ctx.enter_context(nc.sbuf_tensor("smallS", [128, 512], F32))
        t_av = ctx.enter_context(nc.sbuf_tensor("avstgS", [128, 1024], F32))
        t_st = ctx.enter_context(nc.sbuf_tensor("statS", [128, 16], F32))
        t_wf = ctx.enter_context(nc.sbuf_tensor("wfoldS", [128, 8], F32))
        PS = ctx.enter_context(nc.psum_tensor("ps", [128, 4096], F32))
        r = Rec(nc, ctx)

        def BK(i, rows=128, w=512, off=0):
            return PS[0:rows, i * 512 + off:i * 512 + off + w]

        ld = None
        for dst, src in ((t_ident[:, :], d_ident), (t_ones[:, :], d_ones),
                         (t_bones[:, :], d_bones), (t_nsl[:, :], d_nslope),
                         (t_bb[:, :], d_bbase), (t_bc[:, :], d_bcols),
                         (A[:, O_TJ:O_TJ + TQ], d_tj)):
            ld = r.emit("sync", lambda e, d=dst, s=src: e.dma_start(out=d, in_=s[:, :]),
                        inc="LD", by=16)
        for cc in range(8):
            ld = r.emit("sync", lambda e, cc=cc: e.dma_start(
                out=t_selT[:, cc * 16:cc * 16 + 16],
                in_=d_selT[cc * 128:cc * 128 + 128, :]), inc="LD", by=16)
        ld_consts = ld
        vv_ones = None
        for base in (64, 130):
            ap_ones = bass.AP(A, O_V + base, [[FARENA, 128], [132, 32]])
            vv_ones = r.emit("vector", lambda e, a=ap_ones: e.memset(a, 1.0),
                             inc="VV")
        vv_m0 = r.emit("vector", lambda e: e.memset(A[:, SC:SC + 128], 0.0),
                       inc="VV")
        r.wait("gpsimd", "VV", vv_m0)
        msk_gg0 = r.emit("gpsimd", lambda e: e.affine_select(
            A[:, SC:SC + 128], A[:, SC:SC + 128],
            [[1, 128]], AT.is_ge, -1e9, channel_multiplier=-1), inc="GG")

        attn_done_vv = vv_ones
        last_exp_ss = 0
        trp_vv = {6: 0, 7: 0}
        bc_last = {4: 0, 5: 0, 6: 0, 7: 0}
        prod_done_tt = 0
        for p in range(NP):
            r.wait("sync", "TT", prod_done_tt)
            ld_w = 0
            for off, cb in ((O_WQ, p * 128), (O_WK, 512 + p * 128),
                            (O_WV, 1024 + p * 128)):
                for cc in range(8):
                    ld_w = r.emit("sync", lambda e, cc=cc, cb=cb, off=off:
                                  e.dma_start(
                        out=A[:, off + cc * 128:off + cc * 128 + 128],
                        in_=d_wqkv[cc * 128:cc * 128 + 128, cb:cb + 128]),
                        inc="S2", by=16)
            r.wait("scalar", "LD", ld_consts)
            r.wait("scalar", "VV", attn_done_vv)
            ss_tsc = None
            for hh in range(2):
                ss_tsc = r.emit("scalar", lambda e, hh=hh, h=2 * p + hh: e.mul(
                    A[:, O_TSC + hh * TQ:O_TSC + (hh + 1) * TQ],
                    A[:, O_TJ:O_TJ + TQ], t_nsl[:, h:h + 1]), inc="SS")

            tile_tt = {}
            seg_vv = {}
            first_prod = True
            trq = []

            def emit_tr():
                tseg0, ii, vvt0 = trq.pop(0)
                ti = tseg0 * 4 + ii
                trb = 6 + ii % 2
                r.wait("tensor", "VV", vvt0)
                r.wait("tensor", "VV", trp_vv[trb])
                tt2 = r.emit("tensor", lambda e, ii=ii, trb=trb: e.transpose(
                    BK(trb, 128, 128),
                    A[:, O_VT + ii * 128:O_VT + ii * 128 + 128],
                    t_ident[:, :]), inc="TT")
                r.wait("vector", "TT", tt2)
                r.emit("vector", lambda e, ti=ti, trb=trb: e.tensor_copy(
                    A[:, O_V + ti * 132:O_V + ti * 132 + 64],
                    BK(trb, 128, 64)), inc="VV")
                vv2 = r.emit("vector", lambda e, ti=ti, trb=trb: e.tensor_copy(
                    A[:, O_V + ti * 132 + 66:O_V + ti * 132 + 130],
                    BK(trb, 128, 64, off=64)), inc="VV")
                trp_vv[trb] = vv2
                return vv2

            attn_guard_tt = r.cnt["TT"]
            for i in range(64):
                tseg, cc = divmod(i, 8)
                xtb32 = A[:, O_XT + (i % 2) * 512:O_XT + (i % 2) * 512 + 512]
                if i == 0:
                    r.wait("sync", "TT", attn_guard_tt)
                if i >= 2:
                    r.wait("sync", "TT", tile_tt[i - 2])
                ldx = r.emit("sync", lambda e, xtb32=xtb32, cc=cc, t=tseg:
                             e.dma_start(
                    out=xtb32,
                    in_=d_xT[cc * 128:cc * 128 + 128, t * 512:t * 512 + 512]),
                    inc="S%d" % (i % 2), by=16)
                r.wait("tensor", "S%d" % (i % 2), ldx)
                if cc == 0:
                    r.wait("tensor", "S2", ld_w)
                    if first_prod:
                        r.wait("tensor", "VV", attn_done_vv)
                        first_prod = False
                    if tseg >= 2:
                        r.wait("tensor", "VV", seg_vv[tseg - 2])
                kb, vb, qb = tseg % 2, 2 + tseg % 2, 4 + tseg % 2
                tt = None
                for off, bk in ((O_WK, kb), (O_WV, vb), (O_WQ, qb)):
                    tt = r.emit("tensor", lambda e, off=off, cc=cc, bk=bk,
                                xtb32=xtb32: e.matmul(
                        BK(bk), A[:, off + cc * 128:off + cc * 128 + 128],
                        xtb32, start=(cc == 0), stop=(cc == 7)), inc="TT")
                tile_tt[i] = tt
                if trq and cc < 4:
                    emit_tr()
                if cc == 7:
                    r.wait("vector", "TT", tt)
                    r.emit("vector", lambda e, t=tseg, kb=kb, p=p: e.tensor_scalar(
                        A[:, O_K + t * 512:O_K + t * 512 + 512], BK(kb),
                        t_bc[:, 4 + p:5 + p], None, op0=AT.add), inc="VV")
                    r.emit("vector", lambda e, t=tseg, qb=qb, p=p: e.tensor_scalar(
                        A[:, O_Q + t * 512:O_Q + t * 512 + 512], BK(qb),
                        t_bc[:, p:p + 1], 0.125, op0=AT.add, op1=AT.mult), inc="VV")
                    vvt = r.emit("vector", lambda e, vb=vb: e.tensor_copy(
                        A[:, O_VT:O_VT + 512], BK(vb)), inc="VV")
                    for ii in range(4):
                        trq.append((tseg, ii, vvt))
                    seg_vv[tseg] = vvt
            while trq:
                seg_vv[7] = emit_tr()
            prod_done_tt = r.cnt["TT"]
            if p == 3:
                r.wait("sync", "SS", ss_tsc)
                for jc in range(4):
                    ldwpj = r.emit("sync", lambda e, jc=jc: e.dma_start(
                        out=A[:, O_WPJ + jc * 1024:O_WPJ + jc * 1024 + 1024],
                        in_=d_wproj[jc * 128:jc * 128 + 128, :]),
                        inc="S3", by=16)

            if debug and p == 3:
                r.wait("gpsimd", "VV", seg_vv[7])
                r.emit("gpsimd", lambda e: e.dma_start(
                    out=g_qk[:, 0:4096], in_=A[:, O_Q:O_Q + 4096]), inc="SO", by=16)
                r.emit("gpsimd", lambda e: e.dma_start(
                    out=g_qk[:, 4096:8192], in_=A[:, O_K:O_K + 4096]), inc="SO", by=16)
                sfd = r.emit("gpsimd", lambda e: e.dma_start(
                    out=g_v[:, :], in_=A[:, O_V:O_V + 4224]), inc="SO", by=16)
                r.wait("gpsimd", "SO", sfd)

            r.wait("vector", "SS", last_exp_ss if p > 0 else 0)
            bsum_hist = {}
            bsum_tt = 0
            red_vv = 0
            for which, src_off in ((0, O_Q), (1, O_K)):
                for t in range(NSEG):
                    it = which * NSEG + t
                    sqb = A[:, O_PB + (t % 2) * 512:O_PB + (t % 2) * 512 + 512]
                    r.wait("scalar", "VV", seg_vv[7])
                    if it >= 2:
                        r.wait("scalar", "TT", bsum_hist[it - 2])
                    ss = r.emit("scalar", lambda e, t=t, so=src_off, sqb=sqb:
                                e.activation(sqb,
                                             A[:, so + t * 512:so + t * 512 + 512],
                                             AF.Square), inc="SS")
                    r.wait("tensor", "SS", ss)
                    if which == 0 and t == 0:
                        r.wait("tensor", "VV", seg_vv[7])
                    r.wait("tensor", "VV", red_vv)
                    bsum_tt = r.emit("tensor", lambda e, sqb=sqb: e.matmul(
                        BK(0, 128, 512), t_bones[:, :], sqb, start=True, stop=True),
                        inc="TT")
                    bsum_hist[it] = bsum_tt
                    r.wait("vector", "TT", bsum_tt)
                    if t == 0:
                        red_vv = r.emit("vector", lambda e, dc=which: e.reduce_max(
                            t_st[0:2, dc:dc + 1], BK(0, 2, 512), axis=AX.X),
                            inc="VV")
                    else:
                        red_vv = r.emit("vector", lambda e: e.reduce_max(
                            t_st[0:2, 8:9], BK(0, 2, 512), axis=AX.X), inc="VV")
                        r.emit("vector", lambda e, dc=which: e.tensor_tensor(
                            t_st[0:2, dc:dc + 1], t_st[0:2, dc:dc + 1],
                            t_st[0:2, 8:9], op=AT.max), inc="VV")
            vvB = r.emit("vector", lambda e: e.tensor_tensor(
                t_st[0:2, 2:3], t_st[0:2, 0:1], t_st[0:2, 1:2], op=AT.mult),
                inc="VV")
            r.wait("scalar", "VV", vvB)
            ssB = r.emit("scalar", lambda e: e.activation(
                t_st[0:2, 3:4], t_st[0:2, 2:3], AF.Sqrt), inc="SS")
            r.wait("vector", "SS", ssB)
            vvB2 = r.emit("vector", lambda e: e.tensor_scalar(
                t_st[0:2, 4:5], t_st[0:2, 3:4], 1.005, 1.0,
                op0=AT.mult, op1=AT.add), inc="VV")
            r.wait("tensor", "VV", seg_vv[7])
            r.wait("tensor", "VV", vvB2)
            tr_tt = r.emit("tensor", lambda e: e.matmul(
                PS[0:1, 3072:3074], t_st[0:2, 4:5], t_ident[0:2, 0:2],
                start=True, stop=True), inc="TT")
            r.wait("vector", "TT", tr_tt)
            vvc = r.emit("vector", lambda e: e.tensor_copy(
                t_st[0:1, 9:11], PS[0:1, 3072:3074]), inc="VV")
            bc_last[6] = vvc
            r.wait("tensor", "VV", vvc)
            ttb = 0
            for hh in range(2):
                ttb = r.emit("tensor", lambda e, hh=hh: e.matmul(
                    PS[0:128, 3584 + hh:3585 + hh], t_ones[0:1, 0:128],
                    t_st[0:1, 9 + hh:10 + hh], start=(hh == 0), stop=(hh == 1)),
                    inc="TT")
            r.wait("vector", "TT", ttb)
            bias_vv = r.emit("vector", lambda e, p=p: e.tensor_tensor(
                t_st[0:128, 6:8], t_bb[:, 2 * p:2 * p + 2],
                PS[0:128, 3584:3586], op=AT.subtract), inc="VV")

            r.wait("tensor", "VV", bias_vv)
            r.wait("tensor", "SS", ss_tsc)
            r.wait("scalar", "VV", bias_vv)
            r.wait("vector", "GG", msk_gg0)
            av_hist = {0: {}, 1: {}}
            stt_vv = {}
            exp_ss = {}
            avfree_vv = 0
            evac_done = {0: 0, 1: 0}
            pending_norm = [None]
            pending_evac = [None]

            def emit_evac(part=None):
                s0, KT0 = pending_evac[0]
                avb = 4 + 2 * (s0 % 2)
                hs = range(2) if part is None else [part]
                for hh in hs:
                    r.wait("vector", "TT", av_hist[hh][KT0 - 1])
                    r.emit("vector", lambda e, hh=hh, avb=avb: e.tensor_copy(
                        t_av[64 * hh:64 * hh + 64, 0:512],
                        PS[0:64, (avb + hh) * 512:(avb + hh) * 512 + 512]),
                        inc="VV")
                    evac_done[s0 % 2] = r.emit(
                        "vector", lambda e, hh=hh, avb=avb: e.tensor_copy(
                            t_av[32 * hh:32 * hh + 1, 512:1024],
                            PS[64:65, (avb + hh) * 512:(avb + hh) * 512 + 512]),
                        inc="VV")
                if part != 0:
                    pending_evac[0] = None

            def emit_rec(half=None):
                s0, todo, rec = pending_norm[0]
                cols = [(512, 1024)] if half is None else \
                    [(512 + half * 256, 768 + half * 256)]
                if rec is None or half == 1:
                    for c0, c1 in cols:
                        rec = r.emit("vector", lambda e, c0=c0, c1=c1:
                                     e.reciprocal(
                            t_av[0:33, c0:c1], t_av[0:33, c0:c1]), inc="VV")
                    if half != 0:
                        pending_norm[0] = (s0, todo, rec)

            def flush_norm(one=False):
                if pending_norm[0] is None:
                    return None
                emit_rec()
                s0, todo, rec = pending_norm[0]
                hs = todo[:1] if one else todo
                nv = None
                for hh in hs:
                    jc, rb = divmod(2 * p + hh, 2)
                    rb *= 64
                    rrow = 32 * hh
                    bcb = 4 + 2 * (s0 % 2) + hh
                    r.wait("tensor", "VV", rec)
                    r.wait("tensor", "VV", bc_last[bcb])
                    ttn = r.emit("tensor", lambda e, bcb=bcb, rrow=rrow: e.matmul(
                        BK(bcb, 64, 512), t_ones[rrow:rrow + 1, 0:64],
                        t_av[rrow:rrow + 1, 512:1024],
                        start=True, stop=True), inc="TT")
                    r.wait("vector", "TT", ttn)
                    bc_last[bcb] = r.emit("vector", lambda e, bcb=bcb, rb=rb:
                                          e.tensor_copy(
                        t_sm[rb:rb + 64, 0:512], BK(bcb, 64, 512)), inc="VV")
                    nv = r.emit("vector", lambda e, hh=hh, jc=jc, rb=rb,
                                s0=s0: e.tensor_tensor(
                        A[rb:rb + 64,
                          SA + jc * TQ + s0 * 512:SA + jc * TQ + s0 * 512 + 512],
                        t_av[64 * hh:64 * hh + 64, 0:512],
                        t_sm[rb:rb + 64, 0:512], op=AT.mult), inc="VV")
                rest = todo[len(hs):]
                pending_norm[0] = (s0, rest, rec) if rest else None
                return nv

            def do_av(s, q0, sk, hh):
                sp0p = max(q0, sk * 128)
                wp = q0 + 512 - sk * 128 if sk >= 4 * s else 512
                pb_i = 3 * hh + sk % 3
                avb = 4 + 2 * (s % 2)
                av_hist[hh][sk] = r.emit(
                    "tensor", lambda e, hh=hh, sk=sk, sp0p=sp0p, wp=wp, q0=q0,
                    pb_i=pb_i, avb=avb: e.matmul(
                        PS[0:65, (avb + hh) * 512 + sp0p - q0:
                           (avb + hh) * 512 + sp0p - q0 + wp],
                        A[0:128, O_V + sk * 132 + hh * 66:
                          O_V + sk * 132 + hh * 66 + 65],
                        A[:, O_PB + pb_i * 512:O_PB + pb_i * 512 + wp],
                        start=(sk == 0), stop=(sk == 4 * s + 3)), inc="TT")

            for s in range(NSEG):
                q0 = s * 512
                KT = 4 * (s + 1)
                for kt in range(KT):
                    ktb = kt * 128
                    diag = kt >= 4 * s
                    span0 = max(q0, ktb)
                    w = q0 + 512 - ktb if diag else 512
                    j0 = span0 - ktb
                    if kt in (4, 5):
                        flush_norm(one=True)
                    r.wait("tensor", "VV", max(stt_vv.get((0, kt % 2), 0),
                                               stt_vv.get((1, kt % 2), 0)))
                    if kt >= 2:
                        r.wait("tensor", "SS", max(exp_ss.get((0, kt - 2), 0),
                                                   exp_ss.get((1, kt - 2), 0)))
                        if kt == 2:
                            avb = 4 + 2 * (s % 2)
                            r.wait("tensor", "VV", max(evac_done[s % 2],
                                                       bc_last[avb],
                                                       bc_last[avb + 1]))
                    qk_tt = None
                    for hh in range(2):
                        qk_bk = (kt % 2) + 2 * hh
                        rlo = 64 * hh
                        qk_tt = r.emit("tensor", lambda e, qk_bk=qk_bk, w=w,
                                       rlo=rlo, ktb=ktb, span0=span0: e.matmul(
                            BK(qk_bk, 128, w),
                            A[rlo:rlo + 64, O_K + ktb:O_K + ktb + 128],
                            A[rlo:rlo + 64, O_Q + span0:O_Q + span0 + w],
                            start=True, stop=True), inc="TT")
                    if kt >= 2:
                        for hh in range(2):
                            do_av(s, q0, kt - 2, hh)
                    for hh in range(2):
                        qk_bk = (kt % 2) + 2 * hh
                        pb_i = 3 * hh + kt % 3
                        r.wait("vector", "TT", qk_tt)
                        if kt >= 3:
                            r.wait("vector", "TT", av_hist[hh].get(kt - 3, 0))
                        stt_vv[(hh, kt % 2)] = r.emit(
                            "vector", lambda e, hh=hh, j0=j0, w=w, qk_bk=qk_bk,
                            pb_i=pb_i: e.tensor_tensor(
                                A[:, O_PB + pb_i * 512:O_PB + pb_i * 512 + w],
                                A[:, O_TSC + hh * TQ + j0:O_TSC + hh * TQ + j0 + w],
                                BK(qk_bk, 128, w), op=AT.add), inc="VV")
                        if diag:
                            stt_vv[(hh, kt % 2)] = r.emit(
                                "vector", lambda e, pb_i=pb_i: e.tensor_tensor(
                                    A[:, O_PB + pb_i * 512:
                                      O_PB + pb_i * 512 + 128],
                                    A[:, O_PB + pb_i * 512:
                                      O_PB + pb_i * 512 + 128],
                                    A[:, SC:SC + 128], op=AT.add), inc="VV")
                        r.wait("scalar", "VV", stt_vv[(hh, kt % 2)])
                        exp_ss[(hh, kt)] = r.emit(
                            "scalar", lambda e, w=w, pb_i=pb_i, hh=hh: e.activation(
                                A[:, O_PB + pb_i * 512:O_PB + pb_i * 512 + w],
                                A[:, O_PB + pb_i * 512:O_PB + pb_i * 512 + w],
                                AF.Exp, bias=t_st[:, 6 + hh:7 + hh]), inc="SS")
                    if pending_evac[0] is not None and kt in (0, 1):
                        emit_evac(part=kt)
                    if pending_norm[0] is not None and kt in (2, 3) \
                            and pending_evac[0] is None:
                        emit_rec(half=kt - 2)
                r.wait("tensor", "SS", max(exp_ss.get((0, KT - 2), 0),
                                           exp_ss.get((1, KT - 2), 0)))
                for hh in range(2):
                    do_av(s, q0, KT - 2, hh)
                r.wait("tensor", "SS", max(exp_ss.get((0, KT - 1), 0),
                                           exp_ss.get((1, KT - 1), 0)))
                for hh in range(2):
                    do_av(s, q0, KT - 1, hh)
                pending_evac[0] = (s, KT)
                pending_norm[0] = (s, [0, 1], None)
            emit_evac()
            attn_done_vv = flush_norm()
            last_exp_ss = r.cnt["SS"]

        r.barrier()

        def dump(dst, slot, nchunk, width):
            sf = 0
            for ch in range(nchunk):
                sf = r.emit("gpsimd", lambda e, ch=ch, dst=dst, slot=slot,
                            width=width: e.dma_start(
                    out=dst[ch * 128:ch * 128 + 128, :],
                    in_=A[:, slot + ch * width:slot + ch * width + width]),
                    inc="SO", by=16)
            r.wait("gpsimd", "SO", sf)

        if debug:
            dump(g_attn, SA, 4, TQ)
        r.wait("tensor", "S3", ldwpj)
        ttf = None
        for cs in range(8):
            for jc in range(4):
                ttf = r.emit("tensor", lambda e, cs=cs, jc=jc: e.matmul(
                    PS[0:128, 1024 + cs:1025 + cs],
                    A[:, O_WPJ + jc * 1024 + cs * 128:O_WPJ + jc * 1024 + cs * 128 + 128],
                    t_bc[:, 8 + jc:9 + jc],
                    start=(cs == 0 and jc == 0), stop=(cs == 7 and jc == 3)),
                    inc="TT")
        r.wait("vector", "TT", ttf)
        vvf = r.emit("vector", lambda e: e.tensor_copy(
            t_wf[:, 0:8], PS[0:128, 1024:1032]), inc="VV")
        last_pt_vv = {0: 0, 1: 0}
        st_hist = {}
        st_so = 0
        for idx in range(64):
            cs, s8 = divmod(idx, NSEG)
            ppb = idx % 2
            r.wait("tensor", "VV", vvf)
            r.wait("tensor", "VV", last_pt_vv[ppb])
            ttp = None
            for jc in range(4):
                ttp = r.emit("tensor", lambda e, cs=cs, s8=s8, jc=jc, ppb=ppb:
                             e.matmul(
                    BK(ppb),
                    A[:, O_WPJ + jc * 1024 + cs * 128:O_WPJ + jc * 1024 + cs * 128 + 128],
                    A[:, SA + jc * TQ + s8 * 512:SA + jc * TQ + s8 * 512 + 512],
                    start=(jc == 0), stop=(jc == 3)), inc="TT")
            r.wait("vector", "TT", ttp)
            if idx >= 2:
                r.wait("vector", "S%d" % (idx % 2), st_hist.get(idx - 2, 0))
            pt = A[:, O_PT + ppb * 512:O_PT + ppb * 512 + 512]
            last_pt_vv[ppb] = r.emit("vector", lambda e, ppb=ppb, cs=cs, pt=pt:
                                     e.tensor_scalar(
                pt, BK(ppb), t_wf[:, cs:cs + 1], None, op0=AT.add), inc="VV")
            r.wait("gpsimd", "VV", last_pt_vv[ppb])
            sh, s4 = divmod(s8, 4)
            st_so = r.emit("gpsimd", lambda e, sh=sh, cs=cs, s4=s4, pt=pt:
                           e.dma_start(
                out=h_cin[sh * 1024 + cs * 128:sh * 1024 + cs * 128 + 128,
                          s4 * 512:s4 * 512 + 512], in_=pt),
                inc="S%d" % (idx % 2), by=16)
            st_hist[idx] = st_so
        r.wait("gpsimd", "S0", st_hist[62])
        r.wait("gpsimd", "S1", st_hist[63])
        gg_cc = r.emit("gpsimd", lambda e: e.collective_compute(
            "ReduceScatter", AT.add, replica_groups=[[0, 1], [2, 3], [4, 5], [6, 7]],
            ins=[h_cin.ap().opt()], outs=[h_cout.ap().opt()]), inc="GG")

        ttp_last = r.cnt["TT"]
        r.barrier()

        r.wait("sync", "GG", gg_cc)
        evv_hist = {}
        sq_hist_e = {}
        sq_seq = []

        def e_chunk(cc):
            lb = A[:, O_LD + (cc % 2) * 2048:O_LD + (cc % 2) * 2048 + 2048]
            xb = A[:, O_XQ + (cc % 2) * 2048:O_XQ + (cc % 2) * 2048 + 2048]
            if cc >= 2:
                r.wait("sync", "VV", evv_hist[cc - 2])
            r.emit("sync", lambda e, cc=cc, lb=lb: e.dma_start(
                out=lb, in_=h_cout[cc * 128:cc * 128 + 128, :]),
                inc="S%d" % (cc % 2), by=16)
            ldc = r.emit("sync", lambda e, cc=cc, xb=xb: e.dma_start(
                out=xb, in_=d_xq[cc * 128:cc * 128 + 128, :]),
                inc="S%d" % (cc % 2), by=16)
            r.wait("vector", "S%d" % (cc % 2), ldc)
            evv_hist[cc] = r.emit("vector", lambda e, cc=cc, lb=lb, xb=xb:
                                  e.scalar_tensor_tensor(
                A[:, SH + cc * 2048:SH + cc * 2048 + 2048], lb,
                t_bc[:, 12 + cc:13 + cc], xb, op0=AT.add, op1=AT.add), inc="VV")
            for s4 in range(4):
                sqb = A[:, SA + 10240 + (len(sq_seq) % 2) * 512:
                        SA + 10240 + (len(sq_seq) % 2) * 512 + 512]
                if len(sq_seq) >= 2:
                    r.wait("scalar", "TT", sq_seq[-2])
                r.wait("scalar", "VV", evv_hist[cc])
                ss = r.emit("scalar", lambda e, s4=s4, cc=cc, sqb=sqb:
                            e.activation(
                                sqb,
                                A[:, SH + cc * 2048 + s4 * 512:
                                  SH + cc * 2048 + s4 * 512 + 512],
                                AF.Square), inc="SS")
                r.wait("tensor", "SS", ss)
                r.emit("tensor", lambda e, s4=s4, cc=cc: e.matmul(
                    PS[0:1, s4 * 512:s4 * 512 + 512], t_ones[:, 0:1],
                    A[:, SH + cc * 2048 + s4 * 512:
                      SH + cc * 2048 + s4 * 512 + 512],
                    start=(cc == 0), stop=(cc == 7)), inc="TT")
                v = r.emit("tensor", lambda e, s4=s4, sqb=sqb: e.matmul(
                    PS[0:1, 2048 + s4 * 512:2048 + s4 * 512 + 512],
                    t_ones[:, 0:1], sqb,
                    start=(cc == 0), stop=(cc == 7)), inc="TT")
                sq_seq.append(v)
                sq_hist_e[s4 * 8 + cc] = v

        r.wait("gpsimd", "TT", ttp_last)
        ld_pref = None
        for e4 in range(4):
            for cc in range(8):
                ld_pref = r.emit("gpsimd", lambda e, e4=e4, cc=cc: e.dma_start(
                    out=A[:, O_KEY + (e4 * 8 + cc) * 128:
                          O_KEY + (e4 * 8 + cc) * 128 + 128],
                    in_=d_keys[e4 * 1024 + cc * 128:e4 * 1024 + cc * 128 + 128,
                               :]), inc="PF", by=16)
        for e4 in range(4):
            ld_pref = r.emit("gpsimd", lambda e, e4=e4: e.dma_start(
                out=A[:, O_VAL + e4 * 1024:O_VAL + e4 * 1024 + 1024],
                in_=d_vals[e4 * 128:e4 * 128 + 128, :]), inc="PF", by=16)
        for cc in range(8):
            e_chunk(cc)

        r.barrier()

        def layernorm(src, dst, after_apply=None, pre_sq=None):
            sq_hist = {} if pre_sq is None else pre_sq
            out_vv = None

            def stats(s4):
                for cc in range(8):
                    it = s4 * 8 + cc
                    sqb = A[:, O_SQ + (cc % 2) * 512:O_SQ + (cc % 2) * 512 + 512]
                    if it >= 2:
                        r.wait("scalar", "TT", sq_hist[it - 2])
                    ss = r.emit("scalar", lambda e, s4=s4, cc=cc, sqb=sqb,
                                src=src: e.activation(
                                    sqb,
                                    A[:, src + cc * 2048 + s4 * 512:
                                      src + cc * 2048 + s4 * 512 + 512],
                                    AF.Square), inc="SS")
                    r.wait("tensor", "SS", ss)
                    r.emit("tensor", lambda e, s4=s4, cc=cc, src=src: e.matmul(
                        PS[0:1, s4 * 512:s4 * 512 + 512], t_ones[:, 0:1],
                        A[:, src + cc * 2048 + s4 * 512:
                          src + cc * 2048 + s4 * 512 + 512],
                        start=(cc == 0), stop=(cc == 7)), inc="TT")
                    sq_hist[it] = r.emit("tensor", lambda e, s4=s4, cc=cc,
                                         sqb=sqb: e.matmul(
                        PS[0:1, 2048 + s4 * 512:2048 + s4 * 512 + 512],
                        t_ones[:, 0:1], sqb,
                        start=(cc == 0), stop=(cc == 7)), inc="TT")

            def chain_apply(s4):
                nonlocal out_vv
                R0 = lambda o: A[0:1, o + s4 * 512:o + s4 * 512 + 512]
                r.wait("vector", "TT", sq_hist[s4 * 8 + 7])
                r.emit("vector", lambda e, R0=R0, s4=s4: e.tensor_scalar(
                    R0(ST_M), PS[0:1, s4 * 512:s4 * 512 + 512], 1.0 / C, None,
                    op0=AT.mult), inc="VV")
                r.emit("vector", lambda e, R0=R0, s4=s4: e.tensor_scalar(
                    R0(ST_V), PS[0:1, 2048 + s4 * 512:2048 + s4 * 512 + 512],
                    1.0 / C, None, op0=AT.mult), inc="VV")
                r.emit("vector", lambda e, R0=R0: e.tensor_tensor(
                    R0(ST_T), R0(ST_M), R0(ST_M), op=AT.mult), inc="VV")
                r.emit("vector", lambda e, R0=R0: e.tensor_tensor(
                    R0(ST_V), R0(ST_V), R0(ST_T), op=AT.subtract), inc="VV")
                vvv = r.emit("vector", lambda e, R0=R0: e.tensor_scalar(
                    R0(ST_V), R0(ST_V), EPS, None, op0=AT.add), inc="VV")
                r.wait("scalar", "VV", vvv)
                ssv = r.emit("scalar", lambda e, R0=R0: e.activation(
                    R0(ST_T), R0(ST_V), AF.Sqrt), inc="SS")
                r.wait("vector", "SS", ssv)
                r.emit("vector", lambda e, R0=R0: e.reciprocal(
                    R0(ST_R), R0(ST_T)), inc="VV")
                for _ in range(2):
                    r.emit("vector", lambda e, R0=R0: e.tensor_tensor(
                        R0(ST_T), R0(ST_R), R0(ST_R), op=AT.mult), inc="VV")
                    r.emit("vector", lambda e, R0=R0: e.tensor_tensor(
                        R0(ST_T), R0(ST_T), R0(ST_V), op=AT.mult), inc="VV")
                    r.emit("vector", lambda e, R0=R0: e.tensor_scalar(
                        R0(ST_T), R0(ST_T), -0.5, 1.5, op0=AT.mult, op1=AT.add),
                        inc="VV")
                    r.emit("vector", lambda e, R0=R0: e.tensor_tensor(
                        R0(ST_R), R0(ST_R), R0(ST_T), op=AT.mult), inc="VV")
                r.emit("vector", lambda e, R0=R0: e.tensor_tensor(
                    R0(ST_B), R0(ST_M), R0(ST_R), op=AT.mult), inc="VV")
                vvb = r.emit("vector", lambda e, R0=R0: e.tensor_scalar(
                    R0(ST_B), R0(ST_B), -1.0, None, op0=AT.mult), inc="VV")
                r.wait("tensor", "VV", vvb)
                tta = r.emit("tensor", lambda e, s4=s4: e.matmul(
                    PS[0:128, s4 * 512:s4 * 512 + 512], t_ones[0:1, 0:128],
                    A[0:1, ST_R + s4 * 512:ST_R + s4 * 512 + 512],
                    start=True, stop=True), inc="TT")
                ttb2 = r.emit("tensor", lambda e, s4=s4: e.matmul(
                    PS[0:128, 2048 + s4 * 512:2048 + s4 * 512 + 512],
                    t_ones[0:1, 0:128],
                    A[0:1, ST_B + s4 * 512:ST_B + s4 * 512 + 512],
                    start=True, stop=True), inc="TT")
                r.wait("vector", "TT", tta)
                vva = r.emit("vector", lambda e, s4=s4: e.tensor_copy(
                    A[:, SC + s4 * 512:SC + s4 * 512 + 512],
                    PS[0:128, s4 * 512:s4 * 512 + 512]), inc="VV")
                r.wait("vector", "TT", ttb2)
                for cc in range(8):
                    r.emit("vector", lambda e, cc=cc, s4=s4, src=src, dst=dst:
                           e.tensor_tensor(
                        A[:, dst + cc * 2048 + s4 * 512:
                          dst + cc * 2048 + s4 * 512 + 512],
                        A[:, src + cc * 2048 + s4 * 512:
                          src + cc * 2048 + s4 * 512 + 512],
                        A[:, SC + s4 * 512:SC + s4 * 512 + 512], op=AT.mult),
                        inc="VV")
                    out_vv = r.emit("vector", lambda e, cc=cc, s4=s4, dst=dst:
                                    e.tensor_tensor(
                        A[:, dst + cc * 2048 + s4 * 512:
                          dst + cc * 2048 + s4 * 512 + 512],
                        A[:, dst + cc * 2048 + s4 * 512:
                          dst + cc * 2048 + s4 * 512 + 512],
                        PS[0:128, 2048 + s4 * 512:2048 + s4 * 512 + 512],
                        op=AT.add), inc="VV")

            if pre_sq is None:
                stats(0)
            for s4 in range(4):
                if pre_sq is None and s4 < 3:
                    stats(s4 + 1)
                chain_apply(s4)
                if after_apply is not None:
                    after_apply(s4, out_vv)
            return out_vv

        if debug:
            dump(g_h1, SH, 8, OWN)
        g_tt = [0]

        def ln1_after(s4, vv):
            r.wait("tensor", "VV", vv)
            for cc in range(8):
                g_tt[0] = r.emit("tensor", lambda e, s4=s4, cc=cc: e.matmul(
                    PS[0:16, s4 * 512:s4 * 512 + 512],
                    t_selT[:, cc * 16:cc * 16 + 16],
                    A[:, SA + cc * 2048 + s4 * 512:SA + cc * 2048 + s4 * 512 + 512],
                    start=(cc == 0), stop=(cc == 7)), inc="TT")

        layernorm(SH, SA, after_apply=ln1_after,
                  pre_sq=sq_hist_e)
        r.barrier()
        if debug:
            dump(g_x2, SA, 8, OWN)

        r.wait("scalar", "TT", g_tt[0])
        ss_g = r.emit("scalar", lambda e: e.activation(
            A[0:16, O_G:O_G + 2048], PS[0:16, 0:2048], AF.Sigmoid), inc="SS")
        r.wait("vector", "SS", ss_g)
        r.wait("sync", "SS", ss_g)
        ld_oh = r.emit("sync", lambda e: e.dma_start(
            out=A[0:16, O_OH:O_OH + 2048], in_=d_oh[:, :]), inc="LD", by=16)

        hb_last = {0: 0, 1: 0}
        fb_last = {4: 0, 5: 0}
        g2_last = 0
        q_tt = 0
        r.wait("tensor", "SS", ss_g)
        hid_done_tt = 0
        ldk = 0
        for q4 in range(4):
            if q4 == 0:
                r.wait("tensor", "PF", ld_pref)
            if q4 > 0:
                r.wait("sync", "TT", q_tt)
                for e4 in range(4):
                    eg = 4 * q4 + e4
                    ldk = r.emit("sync", lambda e, eg=eg, e4=e4: e.dma_start(
                        out=A[:, O_VAL + e4 * 1024:O_VAL + e4 * 1024 + 1024],
                        in_=d_vals[eg * 128:eg * 128 + 128, :]), inc="LD", by=16)
            for s4 in range(4):
                for e4 in range(4):
                    eg = 4 * q4 + e4
                    hb = e4 % 2
                    r.wait("tensor", "LD", ldk)
                    r.wait("tensor", "VV", hb_last[hb])
                    tth = None
                    for cc in range(8):
                        tth = r.emit("tensor", lambda e, e4=e4, cc=cc, s4=s4, hb=hb:
                                     e.matmul(
                            BK(hb), A[:, O_KEY + (e4 * 8 + cc) * 128:
                                      O_KEY + (e4 * 8 + cc) * 128 + 128],
                            A[:, SA + cc * 2048 + s4 * 512:
                              SA + cc * 2048 + s4 * 512 + 512],
                            start=(cc == 0), stop=(cc == 7)), inc="TT")
                    hid_done_tt = tth
                    if s4 == 3 and q4 < 3:
                        eg2 = 4 * (q4 + 1) + e4
                        r.wait("sync", "TT", tth)
                        for cc in range(8):
                            ldk = r.emit("sync", lambda e, eg2=eg2, e4=e4,
                                         cc=cc: e.dma_start(
                                out=A[:, O_KEY + (e4 * 8 + cc) * 128:
                                      O_KEY + (e4 * 8 + cc) * 128 + 128],
                                in_=d_keys[eg2 * 1024 + cc * 128:
                                           eg2 * 1024 + cc * 128 + 128,
                                           :]), inc="LD", by=16)
                    r.wait("tensor", "LD", ld_oh)
                    r.wait("tensor", "VV", g2_last)
                    ttg = r.emit("tensor", lambda e, eg=eg, s4=s4: e.matmul(
                        BK(2), A[0:16, O_OH + eg * 128:O_OH + eg * 128 + 128],
                        A[0:16, O_G + s4 * 512:O_G + s4 * 512 + 512],
                        start=True, stop=True), inc="TT")
                    r.wait("vector", "TT", ttg)
                    g2_last = r.emit("vector", lambda e: e.tensor_copy(
                        A[:, O_GES:O_GES + 512], BK(2)), inc="VV")
                    r.wait("vector", "TT", tth)
                    ht_vv = r.emit("vector", lambda e, e4=e4, hb=hb:
                                   e.scalar_tensor_tensor(
                        A[:, O_HT + e4 * 512:O_HT + e4 * 512 + 512], BK(hb),
                        0.0, A[:, O_GES:O_GES + 512],
                        op0=AT.max, op1=AT.mult), inc="VV")
                    hb_last[hb] = ht_vv
                for cs in range(8):
                    fb = 4 + cs % 2
                    r.wait("tensor", "VV", ht_vv)
                    r.wait("tensor", "VV", fb_last[fb])
                    ttv = None
                    for e4 in range(4):
                        ttv = r.emit("tensor", lambda e, e4=e4, cs=cs, fb=fb:
                                     e.matmul(
                            BK(fb), A[:, O_VAL + e4 * 1024 + cs * 128:
                                      O_VAL + e4 * 1024 + cs * 128 + 128],
                            A[:, O_HT + e4 * 512:O_HT + e4 * 512 + 512],
                            start=(e4 == 0), stop=(e4 == 3)), inc="TT")
                    q_tt = ttv
                    r.wait("vector", "TT", ttv)
                    src_slot = SA if q4 == 0 else SH
                    fb_last[fb] = r.emit(
                        "vector", lambda e, cs=cs, s4=s4, fb=fb, ss=src_slot:
                        e.tensor_tensor(
                            A[:, SH + cs * 2048 + s4 * 512:
                              SH + cs * 2048 + s4 * 512 + 512],
                            BK(fb),
                            A[:, ss + cs * 2048 + s4 * 512:
                              ss + cs * 2048 + s4 * 512 + 512],
                            op=AT.add), inc="VV")

        r.barrier()
        if debug:
            dump(g_f, SH, 8, OWN)

        tr_vv = {i: 0 for i in range(8)}
        gst_hist = {}

        def ln2_after(s4, vv):
            r.wait("tensor", "VV", vv)
            for t16 in range(4 * s4, 4 * s4 + 4):
                ob = SB + (t16 % 2) * 1024
                if t16 >= 2:
                    r.wait("vector", "S%d" % (t16 % 2), gst_hist[t16 - 2])
                for cc in range(8):
                    trb = s4 if (t16 * 8 + cc) % 2 == 0 else 4 + s4
                    r.wait("tensor", "VV", tr_vv[trb])
                    ttt = r.emit("tensor", lambda e, t16=t16, cc=cc, trb=trb:
                                 e.transpose(
                        BK(trb, 128, 128),
                        A[:, SA + cc * 2048 + t16 * 128:
                          SA + cc * 2048 + t16 * 128 + 128],
                        t_ident[:, :]), inc="TT")
                    r.wait("vector", "TT", ttt)
                    tr_vv[trb] = r.emit("vector", lambda e, ob=ob, cc=cc,
                                        trb=trb: e.tensor_copy(
                        A[:, ob + cc * 128:ob + cc * 128 + 128],
                        BK(trb, 128, 128)), inc="VV")
                r.wait("gpsimd", "VV", tr_vv[s4])
                r.wait("gpsimd", "VV", tr_vv[4 + s4])
                gst_hist[t16] = r.emit("gpsimd", lambda e, t16=t16, ob=ob:
                                       e.dma_start(
                    out=d_out[t16 * 128:t16 * 128 + 128, :],
                    in_=A[:, ob:ob + 1024]),
                    inc="S%d" % (t16 % 2), by=16)

        layernorm(SH, SA, after_apply=ln2_after)
        r.wait("gpsimd", "S0", gst_hist[14])
        r.wait("gpsimd", "S1", gst_hist[15])

        with nc.Block() as block:
            block.sync(lambda e: [f(e) for f in r.ops["sync"]])
            block.tensor(lambda e: [f(e) for f in r.ops["tensor"]])
            block.vector(lambda e: [f(e) for f in r.ops["vector"]])
            block.scalar(lambda e: [f(e) for f in r.ops["scalar"]])
            block.gpsimd(lambda e: [f(e) for f in r.ops["gpsimd"]])
    return nc


_NC_CACHE = None


def kernel(L0_attn_x, L0_attn_W_attn, L0_attn_b_attn, L0_attn_W_proj,
           L0_attn_b_proj, L0_ffn_expert_sel, L0_ffn_keys, L0_ffn_values):
    global _NC_CACHE
    x = np.asarray(L0_attn_x, np.float32)
    W_attn = np.asarray(L0_attn_W_attn, np.float32)
    b_attn = np.asarray(L0_attn_b_attn, np.float32)
    W_proj = np.asarray(L0_attn_W_proj, np.float32)
    b_proj = np.asarray(L0_attn_b_proj, np.float32)
    sel = np.asarray(L0_ffn_expert_sel, np.float32)
    keys = np.asarray(L0_ffn_keys, np.float32)
    values = np.asarray(L0_ffn_values, np.float32)

    ident = np.eye(128, dtype=np.float32)
    ones = np.ones((128, 128), np.float32)
    bones = np.zeros((128, 128), np.float32)
    for j in range(128):
        if j % 2 == 0:
            bones[0:64, j] = 1.0
        else:
            bones[64:128, j] = 1.0
    tj = np.broadcast_to(np.arange(TQ, dtype=np.float32), (128, TQ)).copy()
    selT = np.ascontiguousarray(sel.T)
    keysR = np.ascontiguousarray(keys.reshape(16 * C, 128))
    valuesR = np.ascontiguousarray(values.reshape(16 * 128, C))
    parts = np.arange(128, dtype=np.float32)
    oh16 = np.kron(np.eye(16, dtype=np.float32), np.ones((1, 128), np.float32))

    in_maps = []
    for c in range(8):
        b, hg = divmod(c, 2)
        heads = np.arange(hg * 8, hg * 8 + 8)
        slopes = (0.5 ** (heads + 1)).astype(np.float32)
        xT = np.ascontiguousarray(x[b].T)
        xq = np.ascontiguousarray(xT[:, hg * OWN:(hg + 1) * OWN])
        wqkv = np.ascontiguousarray(np.concatenate(
            [W_attn[:, 512 * hg:512 * hg + 512],
             W_attn[:, 1024 + 512 * hg:1024 + 512 * hg + 512],
             W_attn[:, 2048 + 512 * hg:2048 + 512 * hg + 512]], axis=1))
        wprojmy = np.ascontiguousarray(W_proj[512 * hg:512 * hg + 512, :])
        bcols = np.zeros((128, 20), np.float32)
        for p in range(4):
            bcols[:, p] = b_attn[512 * hg + 128 * p:512 * hg + 128 * p + 128]
            bcols[:, 4 + p] = b_attn[1024 + 512 * hg + 128 * p:
                                     1024 + 512 * hg + 128 * p + 128]
        bv_my = b_attn[2048 + 512 * hg:2048 + 512 * hg + 512]
        for jc in range(4):
            bcols[:, 8 + jc] = bv_my[jc * 128:jc * 128 + 128]
        for cc in range(8):
            bcols[:, 12 + cc] = b_proj[cc * 128:cc * 128 + 128]
        negslope = np.broadcast_to(-slopes[None, :], (128, 8)).copy()
        biasbase = (parts[:, None] * slopes[None, :]).astype(np.float32)
        in_maps.append(dict(
            xT=xT, xq=xq, wqkv=wqkv, wprojmy=wprojmy, bcols=bcols,
            negslope=negslope, biasbase=biasbase, selT=selT, keysR=keysR,
            valuesR=valuesR, tj=tj, identc=ident, onesc=ones, blockonesc=bones,
            oh16=oh16))

    if _NC_CACHE is None:
        _NC_CACHE = build_nc()
    res = run_bass_kernel_spmd(_NC_CACHE, in_maps, core_ids=list(range(8)))
    y = np.empty((4, TQ, C), np.float32)
    for c in range(8):
        b, hg = divmod(c, 2)
        y[b, hg * OWN:(hg + 1) * OWN, :] = res.results[c]["out"]
    return y


# version 34
# speedup vs baseline: 1.1464x; 1.0943x over previous
import numpy as np
import concourse.bass as bass
import concourse.mybir as mybir
from concourse.bass_utils import run_bass_kernel_spmd
from contextlib import ExitStack

F32 = mybir.dt.float32
BF16 = mybir.dt.bfloat16
AT = mybir.AluOpType
AF = mybir.ActivationFunctionType
AX = mybir.AxisListType

C = 1024
TQ = 4096
HL = 8
NP = 4
SEG = 512
NSEG = 8
OWN = 2048
EPS = 1e-5

SA = 0
SH = 16384
SB = 32768
SC = 48768
FARENA = 50816
O_WQ, O_WK, O_WV, O_TSC = SH, SH + 1024, SH + 2048, SH + 3072
O_TJ = SH + 11264
O_Q, O_K, O_V = SB, SB + 4096, SB + 8192
O_PB, O_VT, O_XT = SB + 12416, SB + 14464, SB + 14976
O_PT = SC
O_WPJ = SH + 11264
O_LD, O_XQ = SB + 8192, SB + 12288
O_KEY, O_VAL, O_HT = SB, SB + 4096, SB + 8192
O_GES, O_G, O_SQ = SB + 10240, SB + 10752, SB + 12800
O_OH = SB + 12800
ST_M, ST_V, ST_R, ST_T, ST_B = (SA + 0, SA + 2048, SA + 4096, SA + 6144,
                                SA + 8192)


class Rec:
    ENGS = ("sync", "tensor", "vector", "scalar", "gpsimd")

    def __init__(self, nc, ctx):
        self.nc = nc
        self.ops = {e: [] for e in self.ENGS}
        self.sems = {}
        self.cnt = {}
        self.lastwait = {}
        for name in ("LD", "PF", "TT", "VV", "SS", "GG", "SO",
                     "S0", "S1", "S2", "S3", "S4", "S5"):
            self.sems[name] = ctx.enter_context(nc.semaphore(name))
            self.cnt[name] = 0

    def emit(self, eng, fn, inc=None, by=1):
        if inc is None:
            self.ops[eng].append(fn)
            return None
        self.cnt[inc] += by
        sem = self.sems[inc]
        self.ops[eng].append(lambda e, f=fn, s=sem, b=by: f(e).then_inc(s, b))
        return self.cnt[inc]

    def wait(self, eng, sname, val):
        if not val:
            return
        key = (eng, sname)
        if self.lastwait.get(key, -1) >= val:
            return
        self.lastwait[key] = val
        sem = self.sems[sname]
        self.ops[eng].append(lambda e, s=sem, v=val: e.wait_ge(s, v))

    def barrier(self):
        for sname in ("LD", "TT", "VV", "SS", "GG", "SO"):
            v = self.cnt[sname]
            for eng in self.ENGS:
                self.wait(eng, sname, v)


def build_nc(debug=False):
    nc = bass.Bass(trn_type="TRN2", target_bir_lowering=False)

    d_xT = nc.dram_tensor("xT", [C, TQ], F32, kind="ExternalInput").ap()
    d_xq = nc.dram_tensor("xq", [C, OWN], F32, kind="ExternalInput").ap()
    d_wqkv = nc.dram_tensor("wqkv", [C, 1536], F32, kind="ExternalInput").ap()
    d_wproj = nc.dram_tensor("wprojmy", [512, C], F32, kind="ExternalInput").ap()
    d_bcols = nc.dram_tensor("bcols", [128, 20], F32, kind="ExternalInput").ap()
    d_nslope = nc.dram_tensor("negslope", [128, 8], F32, kind="ExternalInput").ap()
    d_bbase = nc.dram_tensor("biasbase", [128, 8], F32, kind="ExternalInput").ap()
    d_selT = nc.dram_tensor("selT", [C, 16], F32, kind="ExternalInput").ap()
    d_keys = nc.dram_tensor("keysR", [16 * C, 128], F32, kind="ExternalInput").ap()
    d_vals = nc.dram_tensor("valuesR", [2048, C], F32, kind="ExternalInput").ap()
    d_tj = nc.dram_tensor("tj", [128, TQ], F32, kind="ExternalInput").ap()
    d_ident = nc.dram_tensor("identc", [128, 128], F32, kind="ExternalInput").ap()
    d_ones = nc.dram_tensor("onesc", [128, 128], F32, kind="ExternalInput").ap()
    d_bones = nc.dram_tensor("blockonesc", [128, 128], F32, kind="ExternalInput").ap()
    d_oh = nc.dram_tensor("oh16", [16, 2048], F32, kind="ExternalInput").ap()
    d_out = nc.dram_tensor("out", [OWN, C], F32, kind="ExternalOutput").ap()
    if debug:
        g_attn = nc.dram_tensor("dbg_attn", [512, TQ], F32, kind="ExternalOutput").ap()
        g_h1 = nc.dram_tensor("dbg_h1", [C, OWN], F32, kind="ExternalOutput").ap()
        g_x2 = nc.dram_tensor("dbg_x2", [C, OWN], F32, kind="ExternalOutput").ap()
        g_f = nc.dram_tensor("dbg_f", [C, OWN], F32, kind="ExternalOutput").ap()
        g_qk = nc.dram_tensor("dbg_qk", [128, 8192], F32, kind="ExternalOutput").ap()
        g_v = nc.dram_tensor("dbg_v", [128, 4224], F32, kind="ExternalOutput").ap()
    h_cin = nc.dram_tensor("cc_in", [2048, OWN], F32)
    h_cout = nc.dram_tensor("cc_out", [C, OWN], F32)

    with ExitStack() as ctx:
        A = ctx.enter_context(nc.sbuf_tensor("arena", [128, FARENA], F32))
        t_ident = ctx.enter_context(nc.sbuf_tensor("identS", [128, 128], F32))
        t_ones = ctx.enter_context(nc.sbuf_tensor("onesS", [128, 128], F32))
        t_bones = ctx.enter_context(nc.sbuf_tensor("bonesS", [128, 128], F32))
        t_nsl = ctx.enter_context(nc.sbuf_tensor("nslS", [128, 8], F32))
        t_bb = ctx.enter_context(nc.sbuf_tensor("bbS", [128, 8], F32))
        t_bc = ctx.enter_context(nc.sbuf_tensor("bcS", [128, 20], F32))
        t_selT = ctx.enter_context(nc.sbuf_tensor("selTS", [128, 128], F32))
        t_sm = ctx.enter_context(nc.sbuf_tensor("smallS", [128, 512], F32))
        t_av = ctx.enter_context(nc.sbuf_tensor("avstgS", [128, 1024], F32))
        t_st = ctx.enter_context(nc.sbuf_tensor("statS", [128, 16], F32))
        t_wf = ctx.enter_context(nc.sbuf_tensor("wfoldS", [128, 8], F32))
        PS = ctx.enter_context(nc.psum_tensor("ps", [128, 4096], F32))
        r = Rec(nc, ctx)

        def BK(i, rows=128, w=512, off=0):
            return PS[0:rows, i * 512 + off:i * 512 + off + w]

        ld = None
        for dst, src in ((t_ident[:, :], d_ident), (t_ones[:, :], d_ones),
                         (t_bones[:, :], d_bones), (t_nsl[:, :], d_nslope),
                         (t_bb[:, :], d_bbase), (t_bc[:, :], d_bcols),
                         (A[:, O_TJ:O_TJ + TQ], d_tj)):
            ld = r.emit("sync", lambda e, d=dst, s=src: e.dma_start(out=d, in_=s[:, :]),
                        inc="LD", by=16)
        for cc in range(8):
            ld = r.emit("sync", lambda e, cc=cc: e.dma_start(
                out=t_selT[:, cc * 16:cc * 16 + 16],
                in_=d_selT[cc * 128:cc * 128 + 128, :]), inc="LD", by=16)
        ld_consts = ld
        vv_ones = None
        for base in (64, 130):
            ap_ones = bass.AP(A, O_V + base, [[FARENA, 128], [132, 32]])
            vv_ones = r.emit("vector", lambda e, a=ap_ones: e.memset(a, 1.0),
                             inc="VV")
        vv_m0 = r.emit("vector", lambda e: e.memset(A[:, SC:SC + 128], 0.0),
                       inc="VV")
        r.wait("gpsimd", "VV", vv_m0)
        msk_gg0 = r.emit("gpsimd", lambda e: e.affine_select(
            A[:, SC:SC + 128], A[:, SC:SC + 128],
            [[1, 128]], AT.is_ge, -1e9, channel_multiplier=-1), inc="GG")

        attn_done_vv = vv_ones
        last_exp_ss = 0
        trp_vv = {6: 0, 7: 0}
        bc_last = {4: 0, 5: 0, 6: 0, 7: 0}
        prod_done_tt = 0
        for p in range(NP):
            r.wait("sync", "TT", prod_done_tt)
            ld_w = 0
            for off, cb in ((O_WQ, p * 128), (O_WK, 512 + p * 128),
                            (O_WV, 1024 + p * 128)):
                for cc in range(8):
                    ld_w = r.emit("sync", lambda e, cc=cc, cb=cb, off=off:
                                  e.dma_start(
                        out=A[:, off + cc * 128:off + cc * 128 + 128],
                        in_=d_wqkv[cc * 128:cc * 128 + 128, cb:cb + 128]),
                        inc="S2", by=16)
            r.wait("scalar", "LD", ld_consts)
            r.wait("scalar", "VV", attn_done_vv)
            ss_tsc = None
            for hh in range(2):
                ss_tsc = r.emit("scalar", lambda e, hh=hh, h=2 * p + hh: e.mul(
                    A[:, O_TSC + hh * TQ:O_TSC + (hh + 1) * TQ],
                    A[:, O_TJ:O_TJ + TQ], t_nsl[:, h:h + 1]), inc="SS")

            tile_tt = {}
            seg_vv = {}
            first_prod = True
            trq = []

            def emit_tr():
                tseg0, ii, vvt0 = trq.pop(0)
                ti = tseg0 * 4 + ii
                trb = 6 + ii % 2
                r.wait("tensor", "VV", vvt0)
                r.wait("tensor", "VV", trp_vv[trb])
                tt2 = r.emit("tensor", lambda e, ii=ii, trb=trb: e.transpose(
                    BK(trb, 128, 128),
                    A[:, O_VT + ii * 128:O_VT + ii * 128 + 128],
                    t_ident[:, :]), inc="TT")
                r.wait("vector", "TT", tt2)
                r.emit("vector", lambda e, ti=ti, trb=trb: e.tensor_copy(
                    A[:, O_V + ti * 132:O_V + ti * 132 + 64],
                    BK(trb, 128, 64)), inc="VV")
                vv2 = r.emit("vector", lambda e, ti=ti, trb=trb: e.tensor_copy(
                    A[:, O_V + ti * 132 + 66:O_V + ti * 132 + 130],
                    BK(trb, 128, 64, off=64)), inc="VV")
                trp_vv[trb] = vv2
                return vv2

            attn_guard_tt = r.cnt["TT"]
            for i in range(64):
                tseg, cc = divmod(i, 8)
                xtb32 = A[:, O_XT + (i % 2) * 512:O_XT + (i % 2) * 512 + 512]
                if i == 0:
                    r.wait("sync", "TT", attn_guard_tt)
                if i >= 2:
                    r.wait("sync", "TT", tile_tt[i - 2])
                ldx = r.emit("sync", lambda e, xtb32=xtb32, cc=cc, t=tseg:
                             e.dma_start(
                    out=xtb32,
                    in_=d_xT[cc * 128:cc * 128 + 128, t * 512:t * 512 + 512]),
                    inc="S%d" % (i % 2), by=16)
                r.wait("tensor", "S%d" % (i % 2), ldx)
                if cc == 0:
                    r.wait("tensor", "S2", ld_w)
                    if first_prod:
                        r.wait("tensor", "VV", attn_done_vv)
                        first_prod = False
                    if tseg >= 2:
                        r.wait("tensor", "VV", seg_vv[tseg - 2])
                kb, vb, qb = tseg % 2, 2 + tseg % 2, 4 + tseg % 2
                tt = None
                for off, bk in ((O_WK, kb), (O_WV, vb), (O_WQ, qb)):
                    tt = r.emit("tensor", lambda e, off=off, cc=cc, bk=bk,
                                xtb32=xtb32: e.matmul(
                        BK(bk), A[:, off + cc * 128:off + cc * 128 + 128],
                        xtb32, start=(cc == 0), stop=(cc == 7)), inc="TT")
                tile_tt[i] = tt
                if trq and cc < 4:
                    emit_tr()
                if cc == 7:
                    r.wait("vector", "TT", tt)
                    r.emit("vector", lambda e, t=tseg, kb=kb, p=p: e.tensor_scalar(
                        A[:, O_K + t * 512:O_K + t * 512 + 512], BK(kb),
                        t_bc[:, 4 + p:5 + p], None, op0=AT.add), inc="VV")
                    r.emit("vector", lambda e, t=tseg, qb=qb, p=p: e.tensor_scalar(
                        A[:, O_Q + t * 512:O_Q + t * 512 + 512], BK(qb),
                        t_bc[:, p:p + 1], 0.125, op0=AT.add, op1=AT.mult), inc="VV")
                    vvt = r.emit("vector", lambda e, vb=vb: e.tensor_copy(
                        A[:, O_VT:O_VT + 512], BK(vb)), inc="VV")
                    for ii in range(4):
                        trq.append((tseg, ii, vvt))
                    seg_vv[tseg] = vvt
            while trq:
                seg_vv[7] = emit_tr()
            prod_done_tt = r.cnt["TT"]
            if p == 3:
                r.wait("sync", "SS", ss_tsc)
                for jc in range(4):
                    ldwpj = r.emit("sync", lambda e, jc=jc: e.dma_start(
                        out=A[:, O_WPJ + jc * 1024:O_WPJ + jc * 1024 + 1024],
                        in_=d_wproj[jc * 128:jc * 128 + 128, :]),
                        inc="S3", by=16)

            if debug and p == 3:
                r.wait("gpsimd", "VV", seg_vv[7])
                r.emit("gpsimd", lambda e: e.dma_start(
                    out=g_qk[:, 0:4096], in_=A[:, O_Q:O_Q + 4096]), inc="SO", by=16)
                r.emit("gpsimd", lambda e: e.dma_start(
                    out=g_qk[:, 4096:8192], in_=A[:, O_K:O_K + 4096]), inc="SO", by=16)
                sfd = r.emit("gpsimd", lambda e: e.dma_start(
                    out=g_v[:, :], in_=A[:, O_V:O_V + 4224]), inc="SO", by=16)
                r.wait("gpsimd", "SO", sfd)

            r.wait("vector", "SS", last_exp_ss if p > 0 else 0)
            bsum_hist = {}
            bsum_tt = 0
            red_vv = 0
            for which, src_off in ((0, O_Q), (1, O_K)):
                for t in range(NSEG):
                    it = which * NSEG + t
                    sqb = A[:, O_PB + (t % 2) * 512:O_PB + (t % 2) * 512 + 512]
                    r.wait("scalar", "VV", seg_vv[7])
                    if it >= 2:
                        r.wait("scalar", "TT", bsum_hist[it - 2])
                    ss = r.emit("scalar", lambda e, t=t, so=src_off, sqb=sqb:
                                e.activation(sqb,
                                             A[:, so + t * 512:so + t * 512 + 512],
                                             AF.Square), inc="SS")
                    r.wait("tensor", "SS", ss)
                    if which == 0 and t == 0:
                        r.wait("tensor", "VV", seg_vv[7])
                    r.wait("tensor", "VV", red_vv)
                    bsum_tt = r.emit("tensor", lambda e, sqb=sqb: e.matmul(
                        BK(0, 128, 512), t_bones[:, :], sqb, start=True, stop=True),
                        inc="TT")
                    bsum_hist[it] = bsum_tt
                    r.wait("vector", "TT", bsum_tt)
                    if t == 0:
                        red_vv = r.emit("vector", lambda e, dc=which: e.reduce_max(
                            t_st[0:2, dc:dc + 1], BK(0, 2, 512), axis=AX.X),
                            inc="VV")
                    else:
                        red_vv = r.emit("vector", lambda e: e.reduce_max(
                            t_st[0:2, 8:9], BK(0, 2, 512), axis=AX.X), inc="VV")
                        r.emit("vector", lambda e, dc=which: e.tensor_tensor(
                            t_st[0:2, dc:dc + 1], t_st[0:2, dc:dc + 1],
                            t_st[0:2, 8:9], op=AT.max), inc="VV")
            vvB = r.emit("vector", lambda e: e.tensor_tensor(
                t_st[0:2, 2:3], t_st[0:2, 0:1], t_st[0:2, 1:2], op=AT.mult),
                inc="VV")
            r.wait("scalar", "VV", vvB)
            ssB = r.emit("scalar", lambda e: e.activation(
                t_st[0:2, 3:4], t_st[0:2, 2:3], AF.Sqrt), inc="SS")
            r.wait("vector", "SS", ssB)
            vvB2 = r.emit("vector", lambda e: e.tensor_scalar(
                t_st[0:2, 4:5], t_st[0:2, 3:4], 1.005, 1.0,
                op0=AT.mult, op1=AT.add), inc="VV")
            r.wait("tensor", "VV", seg_vv[7])
            r.wait("tensor", "VV", vvB2)
            tr_tt = r.emit("tensor", lambda e: e.matmul(
                PS[0:1, 3072:3074], t_st[0:2, 4:5], t_ident[0:2, 0:2],
                start=True, stop=True), inc="TT")
            r.wait("vector", "TT", tr_tt)
            vvc = r.emit("vector", lambda e: e.tensor_copy(
                t_st[0:1, 9:11], PS[0:1, 3072:3074]), inc="VV")
            bc_last[6] = vvc
            r.wait("tensor", "VV", vvc)
            ttb = 0
            for hh in range(2):
                ttb = r.emit("tensor", lambda e, hh=hh: e.matmul(
                    PS[0:128, 3584 + hh:3585 + hh], t_ones[0:1, 0:128],
                    t_st[0:1, 9 + hh:10 + hh], start=(hh == 0), stop=(hh == 1)),
                    inc="TT")
            r.wait("vector", "TT", ttb)
            bias_vv = r.emit("vector", lambda e, p=p: e.tensor_tensor(
                t_st[0:128, 6:8], t_bb[:, 2 * p:2 * p + 2],
                PS[0:128, 3584:3586], op=AT.subtract), inc="VV")

            r.wait("tensor", "VV", bias_vv)
            r.wait("tensor", "SS", ss_tsc)
            r.wait("scalar", "VV", bias_vv)
            r.wait("vector", "GG", msk_gg0)
            av_hist = {0: {}, 1: {}}
            stt_vv = {}
            exp_ss = {}
            avfree_vv = 0
            evac_done = {0: 0, 1: 0}
            pending_norm = [None]
            pending_evac = [None]

            def emit_evac(part=None):
                s0, KT0 = pending_evac[0]
                avb = 4 + 2 * (s0 % 2)
                hs = range(2) if part is None else [part]
                for hh in hs:
                    r.wait("vector", "TT", av_hist[hh][KT0 - 1])
                    r.emit("vector", lambda e, hh=hh, avb=avb: e.tensor_copy(
                        t_av[64 * hh:64 * hh + 64, 0:512],
                        PS[0:64, (avb + hh) * 512:(avb + hh) * 512 + 512]),
                        inc="VV")
                    evac_done[s0 % 2] = r.emit(
                        "vector", lambda e, hh=hh, avb=avb: e.tensor_copy(
                            t_av[32 * hh:32 * hh + 1, 512:1024],
                            PS[64:65, (avb + hh) * 512:(avb + hh) * 512 + 512]),
                        inc="VV")
                if part != 0:
                    pending_evac[0] = None

            def emit_rec(half=None):
                s0, todo, rec = pending_norm[0]
                cols = [(512, 1024)] if half is None else \
                    [(512 + half * 256, 768 + half * 256)]
                if rec is None or half == 1:
                    for c0, c1 in cols:
                        rec = r.emit("vector", lambda e, c0=c0, c1=c1:
                                     e.reciprocal(
                            t_av[0:33, c0:c1], t_av[0:33, c0:c1]), inc="VV")
                    if half != 0:
                        pending_norm[0] = (s0, todo, rec)

            def flush_norm(one=False):
                if pending_norm[0] is None:
                    return None
                emit_rec()
                s0, todo, rec = pending_norm[0]
                hs = todo[:1] if one else todo
                nv = None
                for hh in hs:
                    jc, rb = divmod(2 * p + hh, 2)
                    rb *= 64
                    rrow = 32 * hh
                    bcb = 4 + 2 * (s0 % 2) + hh
                    r.wait("tensor", "VV", rec)
                    r.wait("tensor", "VV", bc_last[bcb])
                    ttn = r.emit("tensor", lambda e, bcb=bcb, rrow=rrow: e.matmul(
                        BK(bcb, 64, 512), t_ones[rrow:rrow + 1, 0:64],
                        t_av[rrow:rrow + 1, 512:1024],
                        start=True, stop=True), inc="TT")
                    r.wait("vector", "TT", ttn)
                    bc_last[bcb] = r.emit("vector", lambda e, bcb=bcb, rb=rb:
                                          e.tensor_copy(
                        t_sm[rb:rb + 64, 0:512], BK(bcb, 64, 512)), inc="VV")
                    nv = r.emit("vector", lambda e, hh=hh, jc=jc, rb=rb,
                                s0=s0: e.tensor_tensor(
                        A[rb:rb + 64,
                          SA + jc * TQ + s0 * 512:SA + jc * TQ + s0 * 512 + 512],
                        t_av[64 * hh:64 * hh + 64, 0:512],
                        t_sm[rb:rb + 64, 0:512], op=AT.mult), inc="VV")
                rest = todo[len(hs):]
                pending_norm[0] = (s0, rest, rec) if rest else None
                return nv

            def do_av(s, q0, sk, hh):
                sp0p = max(q0, sk * 128)
                wp = q0 + 512 - sk * 128 if sk >= 4 * s else 512
                pb_i = 3 * hh + sk % 3
                avb = 4 + 2 * (s % 2)
                av_hist[hh][sk] = r.emit(
                    "tensor", lambda e, hh=hh, sk=sk, sp0p=sp0p, wp=wp, q0=q0,
                    pb_i=pb_i, avb=avb: e.matmul(
                        PS[0:65, (avb + hh) * 512 + sp0p - q0:
                           (avb + hh) * 512 + sp0p - q0 + wp],
                        A[0:128, O_V + sk * 132 + hh * 66:
                          O_V + sk * 132 + hh * 66 + 65],
                        A[:, O_PB + pb_i * 512:O_PB + pb_i * 512 + wp],
                        start=(sk == 0), stop=(sk == 4 * s + 3)), inc="TT")

            for s in range(NSEG):
                q0 = s * 512
                KT = 4 * (s + 1)
                for kt in range(KT):
                    ktb = kt * 128
                    diag = kt >= 4 * s
                    span0 = max(q0, ktb)
                    w = q0 + 512 - ktb if diag else 512
                    j0 = span0 - ktb
                    if kt in (4, 5):
                        flush_norm(one=True)
                    r.wait("tensor", "VV", max(stt_vv.get((0, kt % 2), 0),
                                               stt_vv.get((1, kt % 2), 0)))
                    if kt >= 2:
                        r.wait("tensor", "SS", max(exp_ss.get((0, kt - 2), 0),
                                                   exp_ss.get((1, kt - 2), 0)))
                        if kt == 2:
                            avb = 4 + 2 * (s % 2)
                            r.wait("tensor", "VV", max(evac_done[s % 2],
                                                       bc_last[avb],
                                                       bc_last[avb + 1]))
                    qk_tt = None
                    for hh in range(2):
                        qk_bk = (kt % 2) + 2 * hh
                        rlo = 64 * hh
                        qk_tt = r.emit("tensor", lambda e, qk_bk=qk_bk, w=w,
                                       rlo=rlo, ktb=ktb, span0=span0: e.matmul(
                            BK(qk_bk, 128, w),
                            A[rlo:rlo + 64, O_K + ktb:O_K + ktb + 128],
                            A[rlo:rlo + 64, O_Q + span0:O_Q + span0 + w],
                            start=True, stop=True), inc="TT")
                    if kt >= 2:
                        for hh in range(2):
                            do_av(s, q0, kt - 2, hh)
                    for hh in range(2):
                        qk_bk = (kt % 2) + 2 * hh
                        pb_i = 3 * hh + kt % 3
                        r.wait("vector", "TT", qk_tt)
                        if kt >= 3:
                            r.wait("vector", "TT", av_hist[hh].get(kt - 3, 0))
                        stt_vv[(hh, kt % 2)] = r.emit(
                            "vector", lambda e, hh=hh, j0=j0, w=w, qk_bk=qk_bk,
                            pb_i=pb_i: e.tensor_tensor(
                                A[:, O_PB + pb_i * 512:O_PB + pb_i * 512 + w],
                                A[:, O_TSC + hh * TQ + j0:O_TSC + hh * TQ + j0 + w],
                                BK(qk_bk, 128, w), op=AT.add), inc="VV")
                        if diag:
                            stt_vv[(hh, kt % 2)] = r.emit(
                                "vector", lambda e, pb_i=pb_i: e.tensor_tensor(
                                    A[:, O_PB + pb_i * 512:
                                      O_PB + pb_i * 512 + 128],
                                    A[:, O_PB + pb_i * 512:
                                      O_PB + pb_i * 512 + 128],
                                    A[:, SC:SC + 128], op=AT.add), inc="VV")
                        r.wait("scalar", "VV", stt_vv[(hh, kt % 2)])
                        exp_ss[(hh, kt)] = r.emit(
                            "scalar", lambda e, w=w, pb_i=pb_i, hh=hh: e.activation(
                                A[:, O_PB + pb_i * 512:O_PB + pb_i * 512 + w],
                                A[:, O_PB + pb_i * 512:O_PB + pb_i * 512 + w],
                                AF.Exp, bias=t_st[:, 6 + hh:7 + hh]), inc="SS")
                    if pending_evac[0] is not None and kt in (0, 1):
                        emit_evac(part=kt)
                    if pending_norm[0] is not None and kt in (2, 3) \
                            and pending_evac[0] is None:
                        emit_rec(half=kt - 2)
                r.wait("tensor", "SS", max(exp_ss.get((0, KT - 2), 0),
                                           exp_ss.get((1, KT - 2), 0)))
                for hh in range(2):
                    do_av(s, q0, KT - 2, hh)
                r.wait("tensor", "SS", max(exp_ss.get((0, KT - 1), 0),
                                           exp_ss.get((1, KT - 1), 0)))
                for hh in range(2):
                    do_av(s, q0, KT - 1, hh)
                pending_evac[0] = (s, KT)
                pending_norm[0] = (s, [0, 1], None)
            emit_evac()
            attn_done_vv = flush_norm()
            last_exp_ss = r.cnt["SS"]

        r.barrier()

        def dump(dst, slot, nchunk, width):
            sf = 0
            for ch in range(nchunk):
                sf = r.emit("gpsimd", lambda e, ch=ch, dst=dst, slot=slot,
                            width=width: e.dma_start(
                    out=dst[ch * 128:ch * 128 + 128, :],
                    in_=A[:, slot + ch * width:slot + ch * width + width]),
                    inc="SO", by=16)
            r.wait("gpsimd", "SO", sf)

        if debug:
            dump(g_attn, SA, 4, TQ)
        r.wait("tensor", "S3", ldwpj)
        ttf = None
        for cs in range(8):
            for jc in range(4):
                ttf = r.emit("tensor", lambda e, cs=cs, jc=jc: e.matmul(
                    PS[0:128, 1024 + cs:1025 + cs],
                    A[:, O_WPJ + jc * 1024 + cs * 128:O_WPJ + jc * 1024 + cs * 128 + 128],
                    t_bc[:, 8 + jc:9 + jc],
                    start=(cs == 0 and jc == 0), stop=(cs == 7 and jc == 3)),
                    inc="TT")
        r.wait("vector", "TT", ttf)
        vvf = r.emit("vector", lambda e: e.tensor_copy(
            t_wf[:, 0:8], PS[0:128, 1024:1032]), inc="VV")
        last_pt_vv = {0: 0, 1: 0}
        st_hist = {}
        xt_use = {}
        st_so = 0
        for idx in range(64):
            cs, s8 = divmod(idx, NSEG)
            ppb = idx % 2
            xsl = A[:, O_XT + ppb * 512:O_XT + ppb * 512 + 512]
            if idx >= 2:
                r.wait("sync", "VV", xt_use[idx - 2])
            ldxt = r.emit("sync", lambda e, cs=cs, s8=s8, xsl=xsl: e.dma_start(
                out=xsl, in_=d_xT[cs * 128:cs * 128 + 128,
                                  s8 * 512:s8 * 512 + 512]),
                inc="S%d" % (4 + ppb), by=16)
            r.wait("tensor", "VV", vvf)
            r.wait("tensor", "VV", last_pt_vv[ppb])
            ttp = None
            for jc in range(4):
                ttp = r.emit("tensor", lambda e, cs=cs, s8=s8, jc=jc, ppb=ppb:
                             e.matmul(
                    BK(ppb),
                    A[:, O_WPJ + jc * 1024 + cs * 128:O_WPJ + jc * 1024 + cs * 128 + 128],
                    A[:, SA + jc * TQ + s8 * 512:SA + jc * TQ + s8 * 512 + 512],
                    start=(jc == 0), stop=(jc == 3)), inc="TT")
            r.wait("vector", "TT", ttp)
            r.wait("vector", "S%d" % (4 + ppb), ldxt)
            if idx >= 2:
                r.wait("vector", "S%d" % (idx % 2), st_hist.get(idx - 2, 0))
            pt = A[:, O_PT + ppb * 512:O_PT + ppb * 512 + 512]
            xt_use[idx] = r.emit("vector", lambda e, ppb=ppb, pt=pt, xsl=xsl:
                                 e.scalar_tensor_tensor(
                pt, xsl, 0.5, BK(ppb), op0=AT.mult, op1=AT.add), inc="VV")
            last_pt_vv[ppb] = r.emit("vector", lambda e, cs=cs, pt=pt:
                                     e.tensor_scalar(
                pt, pt, t_wf[:, cs:cs + 1], None, op0=AT.add), inc="VV")
            r.wait("gpsimd", "VV", last_pt_vv[ppb])
            sh, s4 = divmod(s8, 4)
            st_so = r.emit("gpsimd", lambda e, sh=sh, cs=cs, s4=s4, pt=pt:
                           e.dma_start(
                out=h_cin[sh * 1024 + cs * 128:sh * 1024 + cs * 128 + 128,
                          s4 * 512:s4 * 512 + 512], in_=pt),
                inc="S%d" % (idx % 2), by=16)
            st_hist[idx] = st_so
        r.wait("gpsimd", "S0", st_hist[62])
        r.wait("gpsimd", "S1", st_hist[63])
        gg_cc = r.emit("gpsimd", lambda e: e.collective_compute(
            "ReduceScatter", AT.add, replica_groups=[[0, 1], [2, 3], [4, 5], [6, 7]],
            ins=[h_cin.ap().opt()], outs=[h_cout.ap().opt()]), inc="GG")

        ttp_last = r.cnt["TT"]
        r.barrier()

        r.wait("sync", "GG", gg_cc)
        evv_hist = {}
        sq_hist_e = {}
        sq_seq = []

        def e_chunk(cc):
            lb = A[:, O_LD + (cc % 2) * 2048:O_LD + (cc % 2) * 2048 + 2048]
            if cc >= 2:
                r.wait("sync", "VV", evv_hist[cc - 2])
            ldc = r.emit("sync", lambda e, cc=cc, lb=lb: e.dma_start(
                out=lb, in_=h_cout[cc * 128:cc * 128 + 128, :]),
                inc="S%d" % (cc % 2), by=16)
            r.wait("vector", "S%d" % (cc % 2), ldc)
            evv_hist[cc] = r.emit("vector", lambda e, cc=cc, lb=lb:
                                  e.tensor_scalar(
                A[:, SH + cc * 2048:SH + cc * 2048 + 2048], lb,
                t_bc[:, 12 + cc:13 + cc], None, op0=AT.add), inc="VV")
            for s4 in range(4):
                sqb = A[:, SA + 10240 + (len(sq_seq) % 2) * 512:
                        SA + 10240 + (len(sq_seq) % 2) * 512 + 512]
                if len(sq_seq) >= 2:
                    r.wait("scalar", "TT", sq_seq[-2])
                r.wait("scalar", "VV", evv_hist[cc])
                ss = r.emit("scalar", lambda e, s4=s4, cc=cc, sqb=sqb:
                            e.activation(
                                sqb,
                                A[:, SH + cc * 2048 + s4 * 512:
                                  SH + cc * 2048 + s4 * 512 + 512],
                                AF.Square), inc="SS")
                r.wait("tensor", "SS", ss)
                r.emit("tensor", lambda e, s4=s4, cc=cc: e.matmul(
                    PS[0:1, s4 * 512:s4 * 512 + 512], t_ones[:, 0:1],
                    A[:, SH + cc * 2048 + s4 * 512:
                      SH + cc * 2048 + s4 * 512 + 512],
                    start=(cc == 0), stop=(cc == 7)), inc="TT")
                v = r.emit("tensor", lambda e, s4=s4, sqb=sqb: e.matmul(
                    PS[0:1, 2048 + s4 * 512:2048 + s4 * 512 + 512],
                    t_ones[:, 0:1], sqb,
                    start=(cc == 0), stop=(cc == 7)), inc="TT")
                sq_seq.append(v)
                sq_hist_e[s4 * 8 + cc] = v

        r.wait("gpsimd", "TT", ttp_last)
        ld_pref = None
        for e4 in range(4):
            for cc in range(8):
                ld_pref = r.emit("gpsimd", lambda e, e4=e4, cc=cc: e.dma_start(
                    out=A[:, O_KEY + (e4 * 8 + cc) * 128:
                          O_KEY + (e4 * 8 + cc) * 128 + 128],
                    in_=d_keys[e4 * 1024 + cc * 128:e4 * 1024 + cc * 128 + 128,
                               :]), inc="PF", by=16)
        for e4 in range(4):
            ld_pref = r.emit("gpsimd", lambda e, e4=e4: e.dma_start(
                out=A[:, O_VAL + e4 * 1024:O_VAL + e4 * 1024 + 1024],
                in_=d_vals[e4 * 128:e4 * 128 + 128, :]), inc="PF", by=16)
        for cc in range(8):
            e_chunk(cc)

        r.barrier()

        def layernorm(src, dst, after_apply=None, pre_sq=None):
            sq_hist = {} if pre_sq is None else pre_sq
            out_vv = None

            def stats(s4):
                for cc in range(8):
                    it = s4 * 8 + cc
                    sqb = A[:, O_SQ + (cc % 2) * 512:O_SQ + (cc % 2) * 512 + 512]
                    if it >= 2:
                        r.wait("scalar", "TT", sq_hist[it - 2])
                    ss = r.emit("scalar", lambda e, s4=s4, cc=cc, sqb=sqb,
                                src=src: e.activation(
                                    sqb,
                                    A[:, src + cc * 2048 + s4 * 512:
                                      src + cc * 2048 + s4 * 512 + 512],
                                    AF.Square), inc="SS")
                    r.wait("tensor", "SS", ss)
                    r.emit("tensor", lambda e, s4=s4, cc=cc, src=src: e.matmul(
                        PS[0:1, s4 * 512:s4 * 512 + 512], t_ones[:, 0:1],
                        A[:, src + cc * 2048 + s4 * 512:
                          src + cc * 2048 + s4 * 512 + 512],
                        start=(cc == 0), stop=(cc == 7)), inc="TT")
                    sq_hist[it] = r.emit("tensor", lambda e, s4=s4, cc=cc,
                                         sqb=sqb: e.matmul(
                        PS[0:1, 2048 + s4 * 512:2048 + s4 * 512 + 512],
                        t_ones[:, 0:1], sqb,
                        start=(cc == 0), stop=(cc == 7)), inc="TT")

            def chain_apply(s4):
                nonlocal out_vv
                R0 = lambda o: A[0:1, o + s4 * 512:o + s4 * 512 + 512]
                r.wait("vector", "TT", sq_hist[s4 * 8 + 7])
                r.emit("vector", lambda e, R0=R0, s4=s4: e.tensor_scalar(
                    R0(ST_M), PS[0:1, s4 * 512:s4 * 512 + 512], 1.0 / C, None,
                    op0=AT.mult), inc="VV")
                r.emit("vector", lambda e, R0=R0, s4=s4: e.tensor_scalar(
                    R0(ST_V), PS[0:1, 2048 + s4 * 512:2048 + s4 * 512 + 512],
                    1.0 / C, None, op0=AT.mult), inc="VV")
                r.emit("vector", lambda e, R0=R0: e.tensor_tensor(
                    R0(ST_T), R0(ST_M), R0(ST_M), op=AT.mult), inc="VV")
                r.emit("vector", lambda e, R0=R0: e.tensor_tensor(
                    R0(ST_V), R0(ST_V), R0(ST_T), op=AT.subtract), inc="VV")
                vvv = r.emit("vector", lambda e, R0=R0: e.tensor_scalar(
                    R0(ST_V), R0(ST_V), EPS, None, op0=AT.add), inc="VV")
                r.wait("scalar", "VV", vvv)
                ssv = r.emit("scalar", lambda e, R0=R0: e.activation(
                    R0(ST_T), R0(ST_V), AF.Sqrt), inc="SS")
                r.wait("vector", "SS", ssv)
                r.emit("vector", lambda e, R0=R0: e.reciprocal(
                    R0(ST_R), R0(ST_T)), inc="VV")
                for _ in range(2):
                    r.emit("vector", lambda e, R0=R0: e.tensor_tensor(
                        R0(ST_T), R0(ST_R), R0(ST_R), op=AT.mult), inc="VV")
                    r.emit("vector", lambda e, R0=R0: e.tensor_tensor(
                        R0(ST_T), R0(ST_T), R0(ST_V), op=AT.mult), inc="VV")
                    r.emit("vector", lambda e, R0=R0: e.tensor_scalar(
                        R0(ST_T), R0(ST_T), -0.5, 1.5, op0=AT.mult, op1=AT.add),
                        inc="VV")
                    r.emit("vector", lambda e, R0=R0: e.tensor_tensor(
                        R0(ST_R), R0(ST_R), R0(ST_T), op=AT.mult), inc="VV")
                r.emit("vector", lambda e, R0=R0: e.tensor_tensor(
                    R0(ST_B), R0(ST_M), R0(ST_R), op=AT.mult), inc="VV")
                vvb = r.emit("vector", lambda e, R0=R0: e.tensor_scalar(
                    R0(ST_B), R0(ST_B), -1.0, None, op0=AT.mult), inc="VV")
                r.wait("tensor", "VV", vvb)
                tta = r.emit("tensor", lambda e, s4=s4: e.matmul(
                    PS[0:128, s4 * 512:s4 * 512 + 512], t_ones[0:1, 0:128],
                    A[0:1, ST_R + s4 * 512:ST_R + s4 * 512 + 512],
                    start=True, stop=True), inc="TT")
                ttb2 = r.emit("tensor", lambda e, s4=s4: e.matmul(
                    PS[0:128, 2048 + s4 * 512:2048 + s4 * 512 + 512],
                    t_ones[0:1, 0:128],
                    A[0:1, ST_B + s4 * 512:ST_B + s4 * 512 + 512],
                    start=True, stop=True), inc="TT")
                r.wait("vector", "TT", tta)
                vva = r.emit("vector", lambda e, s4=s4: e.tensor_copy(
                    A[:, SC + s4 * 512:SC + s4 * 512 + 512],
                    PS[0:128, s4 * 512:s4 * 512 + 512]), inc="VV")
                r.wait("vector", "TT", ttb2)
                gp_m = {}
                r.wait("gpsimd", "VV", vva)
                for cc in range(4, 8):
                    gp_m[cc] = r.emit(
                        "gpsimd", lambda e, cc=cc, s4=s4, src=src, dst=dst:
                        e.tensor_tensor(
                            A[:, dst + cc * 2048 + s4 * 512:
                              dst + cc * 2048 + s4 * 512 + 512],
                            A[:, src + cc * 2048 + s4 * 512:
                              src + cc * 2048 + s4 * 512 + 512],
                            A[:, SC + s4 * 512:SC + s4 * 512 + 512],
                            op=AT.mult), inc="GG")
                for cc in range(8):
                    if cc < 4:
                        r.emit("vector", lambda e, cc=cc, s4=s4, src=src,
                               dst=dst: e.tensor_tensor(
                            A[:, dst + cc * 2048 + s4 * 512:
                              dst + cc * 2048 + s4 * 512 + 512],
                            A[:, src + cc * 2048 + s4 * 512:
                              src + cc * 2048 + s4 * 512 + 512],
                            A[:, SC + s4 * 512:SC + s4 * 512 + 512],
                            op=AT.mult), inc="VV")
                    else:
                        r.wait("vector", "GG", gp_m[cc])
                    out_vv = r.emit("vector", lambda e, cc=cc, s4=s4, dst=dst:
                                    e.tensor_tensor(
                        A[:, dst + cc * 2048 + s4 * 512:
                          dst + cc * 2048 + s4 * 512 + 512],
                        A[:, dst + cc * 2048 + s4 * 512:
                          dst + cc * 2048 + s4 * 512 + 512],
                        PS[0:128, 2048 + s4 * 512:2048 + s4 * 512 + 512],
                        op=AT.add), inc="VV")

            if pre_sq is None:
                stats(0)
            for s4 in range(4):
                if pre_sq is None and s4 < 3:
                    stats(s4 + 1)
                chain_apply(s4)
                if after_apply is not None:
                    after_apply(s4, out_vv)
            return out_vv

        if debug:
            dump(g_h1, SH, 8, OWN)
        g_tt = [0]

        def ln1_after(s4, vv):
            r.wait("tensor", "VV", vv)
            for cc in range(8):
                g_tt[0] = r.emit("tensor", lambda e, s4=s4, cc=cc: e.matmul(
                    PS[0:16, s4 * 512:s4 * 512 + 512],
                    t_selT[:, cc * 16:cc * 16 + 16],
                    A[:, SA + cc * 2048 + s4 * 512:SA + cc * 2048 + s4 * 512 + 512],
                    start=(cc == 0), stop=(cc == 7)), inc="TT")

        layernorm(SH, SA, after_apply=ln1_after,
                  pre_sq=sq_hist_e)
        r.barrier()
        if debug:
            dump(g_x2, SA, 8, OWN)

        r.wait("scalar", "TT", g_tt[0])
        ss_g = r.emit("scalar", lambda e: e.activation(
            A[0:16, O_G:O_G + 2048], PS[0:16, 0:2048], AF.Sigmoid), inc="SS")
        r.wait("vector", "SS", ss_g)
        r.wait("sync", "SS", ss_g)
        ld_oh = r.emit("sync", lambda e: e.dma_start(
            out=A[0:16, O_OH:O_OH + 2048], in_=d_oh[:, :]), inc="LD", by=16)

        hb_last = {0: 0, 1: 0}
        fb_last = {4: 0, 5: 0}
        g2_last = 0
        q_tt = 0
        r.wait("tensor", "SS", ss_g)
        hid_done_tt = 0
        ldk = 0
        for q4 in range(4):
            if q4 == 0:
                r.wait("tensor", "PF", ld_pref)
            if q4 > 0:
                r.wait("sync", "TT", q_tt)
                for e4 in range(4):
                    eg = 4 * q4 + e4
                    ldk = r.emit("sync", lambda e, eg=eg, e4=e4: e.dma_start(
                        out=A[:, O_VAL + e4 * 1024:O_VAL + e4 * 1024 + 1024],
                        in_=d_vals[eg * 128:eg * 128 + 128, :]), inc="LD", by=16)
            for s4 in range(4):
                for e4 in range(4):
                    eg = 4 * q4 + e4
                    hb = e4 % 2
                    r.wait("tensor", "LD", ldk)
                    r.wait("tensor", "VV", hb_last[hb])
                    tth = None
                    for cc in range(8):
                        tth = r.emit("tensor", lambda e, e4=e4, cc=cc, s4=s4, hb=hb:
                                     e.matmul(
                            BK(hb), A[:, O_KEY + (e4 * 8 + cc) * 128:
                                      O_KEY + (e4 * 8 + cc) * 128 + 128],
                            A[:, SA + cc * 2048 + s4 * 512:
                              SA + cc * 2048 + s4 * 512 + 512],
                            start=(cc == 0), stop=(cc == 7)), inc="TT")
                    hid_done_tt = tth
                    if s4 == 3 and q4 < 3:
                        eg2 = 4 * (q4 + 1) + e4
                        r.wait("sync", "TT", tth)
                        for cc in range(8):
                            ldk = r.emit("sync", lambda e, eg2=eg2, e4=e4,
                                         cc=cc: e.dma_start(
                                out=A[:, O_KEY + (e4 * 8 + cc) * 128:
                                      O_KEY + (e4 * 8 + cc) * 128 + 128],
                                in_=d_keys[eg2 * 1024 + cc * 128:
                                           eg2 * 1024 + cc * 128 + 128,
                                           :]), inc="LD", by=16)
                    r.wait("tensor", "LD", ld_oh)
                    r.wait("tensor", "VV", g2_last)
                    ttg = r.emit("tensor", lambda e, eg=eg, s4=s4: e.matmul(
                        BK(2), A[0:16, O_OH + eg * 128:O_OH + eg * 128 + 128],
                        A[0:16, O_G + s4 * 512:O_G + s4 * 512 + 512],
                        start=True, stop=True), inc="TT")
                    r.wait("vector", "TT", ttg)
                    g2_last = r.emit("vector", lambda e: e.tensor_copy(
                        A[:, O_GES:O_GES + 512], BK(2)), inc="VV")
                    r.wait("vector", "TT", tth)
                    ht_vv = r.emit("vector", lambda e, e4=e4, hb=hb:
                                   e.scalar_tensor_tensor(
                        A[:, O_HT + e4 * 512:O_HT + e4 * 512 + 512], BK(hb),
                        0.0, A[:, O_GES:O_GES + 512],
                        op0=AT.max, op1=AT.mult), inc="VV")
                    hb_last[hb] = ht_vv
                for cs in range(8):
                    fb = 4 + cs % 2
                    r.wait("tensor", "VV", ht_vv)
                    r.wait("tensor", "VV", fb_last[fb])
                    ttv = None
                    for e4 in range(4):
                        ttv = r.emit("tensor", lambda e, e4=e4, cs=cs, fb=fb:
                                     e.matmul(
                            BK(fb), A[:, O_VAL + e4 * 1024 + cs * 128:
                                      O_VAL + e4 * 1024 + cs * 128 + 128],
                            A[:, O_HT + e4 * 512:O_HT + e4 * 512 + 512],
                            start=(e4 == 0), stop=(e4 == 3)), inc="TT")
                    q_tt = ttv
                    r.wait("vector", "TT", ttv)
                    src_slot = SA if q4 == 0 else SH
                    fb_last[fb] = r.emit(
                        "vector", lambda e, cs=cs, s4=s4, fb=fb, ss=src_slot:
                        e.tensor_tensor(
                            A[:, SH + cs * 2048 + s4 * 512:
                              SH + cs * 2048 + s4 * 512 + 512],
                            BK(fb),
                            A[:, ss + cs * 2048 + s4 * 512:
                              ss + cs * 2048 + s4 * 512 + 512],
                            op=AT.add), inc="VV")

        r.barrier()
        if debug:
            dump(g_f, SH, 8, OWN)

        tr_vv = {i: 0 for i in range(8)}
        gst_hist = {}

        def ln2_after(s4, vv):
            r.wait("tensor", "VV", vv)
            for t16 in range(4 * s4, 4 * s4 + 4):
                ob = SB + (t16 % 2) * 1024
                if t16 >= 2:
                    r.wait("vector", "S%d" % (t16 % 2), gst_hist[t16 - 2])
                for cc in range(8):
                    trb = s4 if (t16 * 8 + cc) % 2 == 0 else 4 + s4
                    r.wait("tensor", "VV", tr_vv[trb])
                    ttt = r.emit("tensor", lambda e, t16=t16, cc=cc, trb=trb:
                                 e.transpose(
                        BK(trb, 128, 128),
                        A[:, SA + cc * 2048 + t16 * 128:
                          SA + cc * 2048 + t16 * 128 + 128],
                        t_ident[:, :]), inc="TT")
                    r.wait("vector", "TT", ttt)
                    tr_vv[trb] = r.emit("vector", lambda e, ob=ob, cc=cc,
                                        trb=trb: e.tensor_copy(
                        A[:, ob + cc * 128:ob + cc * 128 + 128],
                        BK(trb, 128, 128)), inc="VV")
                r.wait("gpsimd", "VV", tr_vv[s4])
                r.wait("gpsimd", "VV", tr_vv[4 + s4])
                gst_hist[t16] = r.emit("gpsimd", lambda e, t16=t16, ob=ob:
                                       e.dma_start(
                    out=d_out[t16 * 128:t16 * 128 + 128, :],
                    in_=A[:, ob:ob + 1024]),
                    inc="S%d" % (t16 % 2), by=16)

        layernorm(SH, SA, after_apply=ln2_after)
        r.wait("gpsimd", "S0", gst_hist[14])
        r.wait("gpsimd", "S1", gst_hist[15])

        with nc.Block() as block:
            block.sync(lambda e: [f(e) for f in r.ops["sync"]])
            block.tensor(lambda e: [f(e) for f in r.ops["tensor"]])
            block.vector(lambda e: [f(e) for f in r.ops["vector"]])
            block.scalar(lambda e: [f(e) for f in r.ops["scalar"]])
            block.gpsimd(lambda e: [f(e) for f in r.ops["gpsimd"]])
    return nc


_NC_CACHE = None


def kernel(L0_attn_x, L0_attn_W_attn, L0_attn_b_attn, L0_attn_W_proj,
           L0_attn_b_proj, L0_ffn_expert_sel, L0_ffn_keys, L0_ffn_values):
    global _NC_CACHE
    x = np.asarray(L0_attn_x, np.float32)
    W_attn = np.asarray(L0_attn_W_attn, np.float32)
    b_attn = np.asarray(L0_attn_b_attn, np.float32)
    W_proj = np.asarray(L0_attn_W_proj, np.float32)
    b_proj = np.asarray(L0_attn_b_proj, np.float32)
    sel = np.asarray(L0_ffn_expert_sel, np.float32)
    keys = np.asarray(L0_ffn_keys, np.float32)
    values = np.asarray(L0_ffn_values, np.float32)

    ident = np.eye(128, dtype=np.float32)
    ones = np.ones((128, 128), np.float32)
    bones = np.zeros((128, 128), np.float32)
    for j in range(128):
        if j % 2 == 0:
            bones[0:64, j] = 1.0
        else:
            bones[64:128, j] = 1.0
    tj = np.broadcast_to(np.arange(TQ, dtype=np.float32), (128, TQ)).copy()
    selT = np.ascontiguousarray(sel.T)
    keysR = np.ascontiguousarray(keys.reshape(16 * C, 128))
    valuesR = np.ascontiguousarray(values.reshape(16 * 128, C))
    parts = np.arange(128, dtype=np.float32)
    oh16 = np.kron(np.eye(16, dtype=np.float32), np.ones((1, 128), np.float32))

    in_maps = []
    for c in range(8):
        b, hg = divmod(c, 2)
        heads = np.arange(hg * 8, hg * 8 + 8)
        slopes = (0.5 ** (heads + 1)).astype(np.float32)
        xT = np.ascontiguousarray(x[b].T)
        xq = np.ascontiguousarray(xT[:, hg * OWN:(hg + 1) * OWN])
        wqkv = np.ascontiguousarray(np.concatenate(
            [W_attn[:, 512 * hg:512 * hg + 512],
             W_attn[:, 1024 + 512 * hg:1024 + 512 * hg + 512],
             W_attn[:, 2048 + 512 * hg:2048 + 512 * hg + 512]], axis=1))
        wprojmy = np.ascontiguousarray(W_proj[512 * hg:512 * hg + 512, :])
        bcols = np.zeros((128, 20), np.float32)
        for p in range(4):
            bcols[:, p] = b_attn[512 * hg + 128 * p:512 * hg + 128 * p + 128]
            bcols[:, 4 + p] = b_attn[1024 + 512 * hg + 128 * p:
                                     1024 + 512 * hg + 128 * p + 128]
        bv_my = b_attn[2048 + 512 * hg:2048 + 512 * hg + 512]
        for jc in range(4):
            bcols[:, 8 + jc] = bv_my[jc * 128:jc * 128 + 128]
        for cc in range(8):
            bcols[:, 12 + cc] = b_proj[cc * 128:cc * 128 + 128]
        negslope = np.broadcast_to(-slopes[None, :], (128, 8)).copy()
        biasbase = (parts[:, None] * slopes[None, :]).astype(np.float32)
        in_maps.append(dict(
            xT=xT, xq=xq, wqkv=wqkv, wprojmy=wprojmy, bcols=bcols,
            negslope=negslope, biasbase=biasbase, selT=selT, keysR=keysR,
            valuesR=valuesR, tj=tj, identc=ident, onesc=ones, blockonesc=bones,
            oh16=oh16))

    if _NC_CACHE is None:
        _NC_CACHE = build_nc()
    res = run_bass_kernel_spmd(_NC_CACHE, in_maps, core_ids=list(range(8)))
    y = np.empty((4, TQ, C), np.float32)
    for c in range(8):
        b, hg = divmod(c, 2)
        y[b, hg * OWN:(hg + 1) * OWN, :] = res.results[c]["out"]
    return y


# version 35 (final)
# speedup vs baseline: 1.1808x; 1.0300x over previous
import numpy as np
import concourse.bass as bass
import concourse.mybir as mybir
from concourse.bass_utils import run_bass_kernel_spmd
from contextlib import ExitStack

F32 = mybir.dt.float32
BF16 = mybir.dt.bfloat16
AT = mybir.AluOpType
AF = mybir.ActivationFunctionType
AX = mybir.AxisListType

C = 1024
TQ = 4096
HL = 8
NP = 4
SEG = 512
NSEG = 8
OWN = 2048
EPS = 1e-5

SA = 0
SH = 16384
SB = 32768
SC = 48768
FARENA = 50816
O_WQ, O_WK, O_WV, O_TSC = SH, SH + 1024, SH + 2048, SH + 3072
O_TJ = SH + 11264
O_Q, O_K, O_V = SB, SB + 4096, SB + 8192
O_PB, O_VT, O_XT = SB + 12416, SB + 14464, SB + 14976
O_PT = SC
O_WPJ = SH + 11264
O_LD, O_XQ = SB + 8192, SB + 12288
O_KEY, O_VAL, O_HT = SB, SB + 4096, SB + 8192
O_GES, O_G, O_SQ = SB + 10240, SB + 10752, SB + 12800
O_OH = SB + 12800
ST_M, ST_V, ST_R, ST_T, ST_B = (SA + 0, SA + 2048, SA + 4096, SA + 6144,
                                SA + 8192)


class Rec:
    ENGS = ("sync", "tensor", "vector", "scalar", "gpsimd")

    def __init__(self, nc, ctx):
        self.nc = nc
        self.ops = {e: [] for e in self.ENGS}
        self.sems = {}
        self.cnt = {}
        self.lastwait = {}
        for name in ("LD", "PF", "TT", "VV", "SS", "GG", "SO",
                     "S0", "S1", "S2", "S3", "S4", "S5"):
            self.sems[name] = ctx.enter_context(nc.semaphore(name))
            self.cnt[name] = 0

    def emit(self, eng, fn, inc=None, by=1):
        if inc is None:
            self.ops[eng].append(fn)
            return None
        self.cnt[inc] += by
        sem = self.sems[inc]
        self.ops[eng].append(lambda e, f=fn, s=sem, b=by: f(e).then_inc(s, b))
        return self.cnt[inc]

    def wait(self, eng, sname, val):
        if not val:
            return
        key = (eng, sname)
        if self.lastwait.get(key, -1) >= val:
            return
        self.lastwait[key] = val
        sem = self.sems[sname]
        self.ops[eng].append(lambda e, s=sem, v=val: e.wait_ge(s, v))

    def barrier(self):
        for sname in ("LD", "TT", "VV", "SS", "GG", "SO"):
            v = self.cnt[sname]
            for eng in self.ENGS:
                self.wait(eng, sname, v)


def build_nc(debug=False):
    nc = bass.Bass(trn_type="TRN2", target_bir_lowering=False)

    d_xT = nc.dram_tensor("xT", [C, TQ], F32, kind="ExternalInput").ap()
    d_xq = nc.dram_tensor("xq", [C, OWN], F32, kind="ExternalInput").ap()
    d_wqkv = nc.dram_tensor("wqkv", [C, 1536], F32, kind="ExternalInput").ap()
    d_wproj = nc.dram_tensor("wprojmy", [512, C], F32, kind="ExternalInput").ap()
    d_bcols = nc.dram_tensor("bcols", [128, 20], F32, kind="ExternalInput").ap()
    d_nslope = nc.dram_tensor("negslope", [128, 8], F32, kind="ExternalInput").ap()
    d_bbase = nc.dram_tensor("biasbase", [128, 8], F32, kind="ExternalInput").ap()
    d_selT = nc.dram_tensor("selT", [C, 16], F32, kind="ExternalInput").ap()
    d_keys = nc.dram_tensor("keysR", [16 * C, 128], F32, kind="ExternalInput").ap()
    d_vals = nc.dram_tensor("valuesR", [2048, C], F32, kind="ExternalInput").ap()
    d_tj = nc.dram_tensor("tj", [128, TQ], F32, kind="ExternalInput").ap()
    d_ident = nc.dram_tensor("identc", [128, 128], F32, kind="ExternalInput").ap()
    d_ones = nc.dram_tensor("onesc", [128, 128], F32, kind="ExternalInput").ap()
    d_bones = nc.dram_tensor("blockonesc", [128, 128], F32, kind="ExternalInput").ap()
    d_oh = nc.dram_tensor("oh16", [16, 2048], F32, kind="ExternalInput").ap()
    d_out = nc.dram_tensor("out", [OWN, C], F32, kind="ExternalOutput").ap()
    if debug:
        g_attn = nc.dram_tensor("dbg_attn", [512, TQ], F32, kind="ExternalOutput").ap()
        g_h1 = nc.dram_tensor("dbg_h1", [C, OWN], F32, kind="ExternalOutput").ap()
        g_x2 = nc.dram_tensor("dbg_x2", [C, OWN], F32, kind="ExternalOutput").ap()
        g_f = nc.dram_tensor("dbg_f", [C, OWN], F32, kind="ExternalOutput").ap()
        g_qk = nc.dram_tensor("dbg_qk", [128, 8192], F32, kind="ExternalOutput").ap()
        g_v = nc.dram_tensor("dbg_v", [128, 4224], F32, kind="ExternalOutput").ap()
    h_cin = nc.dram_tensor("cc_in", [2048, OWN], F32)
    h_cout = nc.dram_tensor("cc_out", [C, OWN], F32)

    with ExitStack() as ctx:
        A = ctx.enter_context(nc.sbuf_tensor("arena", [128, FARENA], F32))
        t_ident = ctx.enter_context(nc.sbuf_tensor("identS", [128, 128], F32))
        t_ones = ctx.enter_context(nc.sbuf_tensor("onesS", [128, 128], F32))
        t_bones = ctx.enter_context(nc.sbuf_tensor("bonesS", [128, 128], F32))
        t_nsl = ctx.enter_context(nc.sbuf_tensor("nslS", [128, 8], F32))
        t_bb = ctx.enter_context(nc.sbuf_tensor("bbS", [128, 8], F32))
        t_bc = ctx.enter_context(nc.sbuf_tensor("bcS", [128, 20], F32))
        t_selT = ctx.enter_context(nc.sbuf_tensor("selTS", [128, 128], F32))
        t_sm = ctx.enter_context(nc.sbuf_tensor("smallS", [128, 512], F32))
        t_av = ctx.enter_context(nc.sbuf_tensor("avstgS", [128, 1024], F32))
        t_st = ctx.enter_context(nc.sbuf_tensor("statS", [128, 16], F32))
        t_wf = ctx.enter_context(nc.sbuf_tensor("wfoldS", [128, 8], F32))
        PS = ctx.enter_context(nc.psum_tensor("ps", [128, 4096], F32))
        r = Rec(nc, ctx)

        def BK(i, rows=128, w=512, off=0):
            return PS[0:rows, i * 512 + off:i * 512 + off + w]

        ld = None
        for dst, src in ((t_ident[:, :], d_ident), (t_ones[:, :], d_ones),
                         (t_bones[:, :], d_bones), (t_nsl[:, :], d_nslope),
                         (t_bb[:, :], d_bbase), (t_bc[:, :], d_bcols),
                         (A[:, O_TJ:O_TJ + TQ], d_tj)):
            ld = r.emit("sync", lambda e, d=dst, s=src: e.dma_start(out=d, in_=s[:, :]),
                        inc="LD", by=16)
        for cc in range(8):
            ld = r.emit("sync", lambda e, cc=cc: e.dma_start(
                out=t_selT[:, cc * 16:cc * 16 + 16],
                in_=d_selT[cc * 128:cc * 128 + 128, :]), inc="LD", by=16)
        ld_consts = ld
        vv_ones = None
        for base in (64, 130):
            ap_ones = bass.AP(A, O_V + base, [[FARENA, 128], [132, 32]])
            vv_ones = r.emit("vector", lambda e, a=ap_ones: e.memset(a, 1.0),
                             inc="VV")
        vv_m0 = r.emit("vector", lambda e: e.memset(A[:, SC:SC + 128], 0.0),
                       inc="VV")
        r.wait("gpsimd", "VV", vv_m0)
        msk_gg0 = r.emit("gpsimd", lambda e: e.affine_select(
            A[:, SC:SC + 128], A[:, SC:SC + 128],
            [[1, 128]], AT.is_ge, -1e9, channel_multiplier=-1), inc="GG")

        attn_done_vv = vv_ones
        last_exp_ss = 0
        trp_vv = {6: 0, 7: 0}
        bc_last = {4: 0, 5: 0, 6: 0, 7: 0}
        prod_done_tt = 0
        for p in range(NP):
            r.wait("sync", "TT", prod_done_tt)
            ld_w = 0
            for off, cb in ((O_WQ, p * 128), (O_WK, 512 + p * 128),
                            (O_WV, 1024 + p * 128)):
                for cc in range(8):
                    ld_w = r.emit("sync", lambda e, cc=cc, cb=cb, off=off:
                                  e.dma_start(
                        out=A[:, off + cc * 128:off + cc * 128 + 128],
                        in_=d_wqkv[cc * 128:cc * 128 + 128, cb:cb + 128]),
                        inc="S2", by=16)
            r.wait("scalar", "LD", ld_consts)
            r.wait("scalar", "VV", attn_done_vv)
            ss_tsc = None
            for hh in range(2):
                ss_tsc = r.emit("scalar", lambda e, hh=hh, h=2 * p + hh: e.mul(
                    A[:, O_TSC + hh * TQ:O_TSC + (hh + 1) * TQ],
                    A[:, O_TJ:O_TJ + TQ], t_nsl[:, h:h + 1]), inc="SS")

            tile_tt = {}
            seg_vv = {}
            first_prod = True
            trq = []

            def emit_tr():
                tseg0, ii, vvt0 = trq.pop(0)
                ti = tseg0 * 4 + ii
                trb = 6 + ii % 2
                r.wait("tensor", "VV", vvt0)
                r.wait("tensor", "VV", trp_vv[trb])
                tt2 = r.emit("tensor", lambda e, ii=ii, trb=trb: e.transpose(
                    BK(trb, 128, 128),
                    A[:, O_VT + ii * 128:O_VT + ii * 128 + 128],
                    t_ident[:, :]), inc="TT")
                r.wait("vector", "TT", tt2)
                r.emit("vector", lambda e, ti=ti, trb=trb: e.tensor_copy(
                    A[:, O_V + ti * 132:O_V + ti * 132 + 64],
                    BK(trb, 128, 64)), inc="VV")
                vv2 = r.emit("vector", lambda e, ti=ti, trb=trb: e.tensor_copy(
                    A[:, O_V + ti * 132 + 66:O_V + ti * 132 + 130],
                    BK(trb, 128, 64, off=64)), inc="VV")
                trp_vv[trb] = vv2
                return vv2

            attn_guard_tt = r.cnt["TT"]
            for i in range(64):
                tseg, cc = divmod(i, 8)
                xtb32 = A[:, O_XT + (i % 2) * 512:O_XT + (i % 2) * 512 + 512]
                if i == 0:
                    r.wait("sync", "TT", attn_guard_tt)
                if i >= 2:
                    r.wait("sync", "TT", tile_tt[i - 2])
                ldx = r.emit("sync", lambda e, xtb32=xtb32, cc=cc, t=tseg:
                             e.dma_start(
                    out=xtb32,
                    in_=d_xT[cc * 128:cc * 128 + 128, t * 512:t * 512 + 512]),
                    inc="S%d" % (i % 2), by=16)
                r.wait("tensor", "S%d" % (i % 2), ldx)
                if cc == 0:
                    r.wait("tensor", "S2", ld_w)
                    if first_prod:
                        r.wait("tensor", "VV", attn_done_vv)
                        first_prod = False
                    if tseg >= 2:
                        r.wait("tensor", "VV", seg_vv[tseg - 2])
                kb, vb, qb = tseg % 2, 2 + tseg % 2, 4 + tseg % 2
                tt = None
                for off, bk in ((O_WK, kb), (O_WV, vb), (O_WQ, qb)):
                    tt = r.emit("tensor", lambda e, off=off, cc=cc, bk=bk,
                                xtb32=xtb32: e.matmul(
                        BK(bk), A[:, off + cc * 128:off + cc * 128 + 128],
                        xtb32, start=(cc == 0), stop=(cc == 7)), inc="TT")
                tile_tt[i] = tt
                if trq and cc < 4:
                    emit_tr()
                if cc == 7:
                    r.wait("vector", "TT", tt)
                    r.emit("vector", lambda e, t=tseg, kb=kb, p=p: e.tensor_scalar(
                        A[:, O_K + t * 512:O_K + t * 512 + 512], BK(kb),
                        t_bc[:, 4 + p:5 + p], None, op0=AT.add), inc="VV")
                    r.emit("vector", lambda e, t=tseg, qb=qb, p=p: e.tensor_scalar(
                        A[:, O_Q + t * 512:O_Q + t * 512 + 512], BK(qb),
                        t_bc[:, p:p + 1], 0.125, op0=AT.add, op1=AT.mult), inc="VV")
                    vvt = r.emit("vector", lambda e, vb=vb: e.tensor_copy(
                        A[:, O_VT:O_VT + 512], BK(vb)), inc="VV")
                    for ii in range(4):
                        trq.append((tseg, ii, vvt))
                    seg_vv[tseg] = vvt
            while trq:
                seg_vv[7] = emit_tr()
            prod_done_tt = r.cnt["TT"]
            if p == 3:
                r.wait("sync", "SS", ss_tsc)
                for jc in range(4):
                    ldwpj = r.emit("sync", lambda e, jc=jc: e.dma_start(
                        out=A[:, O_WPJ + jc * 1024:O_WPJ + jc * 1024 + 1024],
                        in_=d_wproj[jc * 128:jc * 128 + 128, :]),
                        inc="S3", by=16)

            if debug and p == 3:
                r.wait("gpsimd", "VV", seg_vv[7])
                r.emit("gpsimd", lambda e: e.dma_start(
                    out=g_qk[:, 0:4096], in_=A[:, O_Q:O_Q + 4096]), inc="SO", by=16)
                r.emit("gpsimd", lambda e: e.dma_start(
                    out=g_qk[:, 4096:8192], in_=A[:, O_K:O_K + 4096]), inc="SO", by=16)
                sfd = r.emit("gpsimd", lambda e: e.dma_start(
                    out=g_v[:, :], in_=A[:, O_V:O_V + 4224]), inc="SO", by=16)
                r.wait("gpsimd", "SO", sfd)

            r.wait("vector", "SS", last_exp_ss if p > 0 else 0)
            bsum_hist = {}
            bsum_tt = 0
            red_vv = 0
            for which, src_off in ((0, O_Q), (1, O_K)):
                for t in range(NSEG):
                    it = which * NSEG + t
                    sqb = A[:, O_PB + (t % 2) * 512:O_PB + (t % 2) * 512 + 512]
                    r.wait("scalar", "VV", seg_vv[7])
                    if it >= 2:
                        r.wait("scalar", "TT", bsum_hist[it - 2])
                    ss = r.emit("scalar", lambda e, t=t, so=src_off, sqb=sqb:
                                e.activation(sqb,
                                             A[:, so + t * 512:so + t * 512 + 512],
                                             AF.Square), inc="SS")
                    r.wait("tensor", "SS", ss)
                    if which == 0 and t == 0:
                        r.wait("tensor", "VV", seg_vv[7])
                    r.wait("tensor", "VV", red_vv)
                    bsum_tt = r.emit("tensor", lambda e, sqb=sqb: e.matmul(
                        BK(0, 128, 512), t_bones[:, :], sqb, start=True, stop=True),
                        inc="TT")
                    bsum_hist[it] = bsum_tt
                    r.wait("vector", "TT", bsum_tt)
                    if t == 0:
                        red_vv = r.emit("vector", lambda e, dc=which: e.reduce_max(
                            t_st[0:2, dc:dc + 1], BK(0, 2, 512), axis=AX.X),
                            inc="VV")
                    else:
                        red_vv = r.emit("vector", lambda e: e.reduce_max(
                            t_st[0:2, 8:9], BK(0, 2, 512), axis=AX.X), inc="VV")
                        r.emit("vector", lambda e, dc=which: e.tensor_tensor(
                            t_st[0:2, dc:dc + 1], t_st[0:2, dc:dc + 1],
                            t_st[0:2, 8:9], op=AT.max), inc="VV")
            vvB = r.emit("vector", lambda e: e.tensor_tensor(
                t_st[0:2, 2:3], t_st[0:2, 0:1], t_st[0:2, 1:2], op=AT.mult),
                inc="VV")
            r.wait("scalar", "VV", vvB)
            ssB = r.emit("scalar", lambda e: e.activation(
                t_st[0:2, 3:4], t_st[0:2, 2:3], AF.Sqrt), inc="SS")
            r.wait("vector", "SS", ssB)
            vvB2 = r.emit("vector", lambda e: e.tensor_scalar(
                t_st[0:2, 4:5], t_st[0:2, 3:4], 1.005, 1.0,
                op0=AT.mult, op1=AT.add), inc="VV")
            r.wait("tensor", "VV", seg_vv[7])
            r.wait("tensor", "VV", vvB2)
            tr_tt = r.emit("tensor", lambda e: e.matmul(
                PS[0:1, 3072:3074], t_st[0:2, 4:5], t_ident[0:2, 0:2],
                start=True, stop=True), inc="TT")
            r.wait("vector", "TT", tr_tt)
            vvc = r.emit("vector", lambda e: e.tensor_copy(
                t_st[0:1, 9:11], PS[0:1, 3072:3074]), inc="VV")
            bc_last[6] = vvc
            r.wait("tensor", "VV", vvc)
            ttb = 0
            for hh in range(2):
                ttb = r.emit("tensor", lambda e, hh=hh: e.matmul(
                    PS[0:128, 3584 + hh:3585 + hh], t_ones[0:1, 0:128],
                    t_st[0:1, 9 + hh:10 + hh], start=(hh == 0), stop=(hh == 1)),
                    inc="TT")
            r.wait("vector", "TT", ttb)
            bias_vv = r.emit("vector", lambda e, p=p: e.tensor_tensor(
                t_st[0:128, 6:8], t_bb[:, 2 * p:2 * p + 2],
                PS[0:128, 3584:3586], op=AT.subtract), inc="VV")

            r.wait("tensor", "VV", bias_vv)
            r.wait("tensor", "SS", ss_tsc)
            r.wait("scalar", "VV", bias_vv)
            r.wait("vector", "GG", msk_gg0)
            av_hist = {0: {}, 1: {}}
            stt_vv = {}
            exp_ss = {}
            avfree_vv = 0
            evac_done = {0: 0, 1: 0}
            pending_norm = [None]
            pending_evac = [None]

            def emit_evac(part=None):
                s0, KT0 = pending_evac[0]
                avb = 4 + 2 * (s0 % 2)
                hs = range(2) if part is None else [part]
                for hh in hs:
                    r.wait("vector", "TT", av_hist[hh][KT0 - 1])
                    r.emit("vector", lambda e, hh=hh, avb=avb: e.tensor_copy(
                        t_av[64 * hh:64 * hh + 64, 0:512],
                        PS[0:64, (avb + hh) * 512:(avb + hh) * 512 + 512]),
                        inc="VV")
                    evac_done[s0 % 2] = r.emit(
                        "vector", lambda e, hh=hh, avb=avb: e.tensor_copy(
                            t_av[32 * hh:32 * hh + 1, 512:1024],
                            PS[64:65, (avb + hh) * 512:(avb + hh) * 512 + 512]),
                        inc="VV")
                if part != 0:
                    pending_evac[0] = None

            def emit_rec(half=None):
                s0, todo, rec = pending_norm[0]
                cols = [(512, 1024)] if half is None else \
                    [(512 + half * 256, 768 + half * 256)]
                if rec is None or half == 1:
                    for c0, c1 in cols:
                        rec = r.emit("vector", lambda e, c0=c0, c1=c1:
                                     e.reciprocal(
                            t_av[0:33, c0:c1], t_av[0:33, c0:c1]), inc="VV")
                    if half != 0:
                        pending_norm[0] = (s0, todo, rec)

            def flush_norm(one=False):
                if pending_norm[0] is None:
                    return None
                emit_rec()
                s0, todo, rec = pending_norm[0]
                hs = todo[:1] if one else todo
                nv = None
                for hh in hs:
                    jc, rb = divmod(2 * p + hh, 2)
                    rb *= 64
                    rrow = 32 * hh
                    bcb = 4 + 2 * (s0 % 2) + hh
                    r.wait("tensor", "VV", rec)
                    r.wait("tensor", "VV", bc_last[bcb])
                    ttn = r.emit("tensor", lambda e, bcb=bcb, rrow=rrow: e.matmul(
                        BK(bcb, 64, 512), t_ones[rrow:rrow + 1, 0:64],
                        t_av[rrow:rrow + 1, 512:1024],
                        start=True, stop=True), inc="TT")
                    r.wait("vector", "TT", ttn)
                    bc_last[bcb] = r.emit("vector", lambda e, bcb=bcb, rb=rb:
                                          e.tensor_copy(
                        t_sm[rb:rb + 64, 0:512], BK(bcb, 64, 512)), inc="VV")
                    nv = r.emit("vector", lambda e, hh=hh, jc=jc, rb=rb,
                                s0=s0: e.tensor_tensor(
                        A[rb:rb + 64,
                          SA + jc * TQ + s0 * 512:SA + jc * TQ + s0 * 512 + 512],
                        t_av[64 * hh:64 * hh + 64, 0:512],
                        t_sm[rb:rb + 64, 0:512], op=AT.mult), inc="VV")
                rest = todo[len(hs):]
                pending_norm[0] = (s0, rest, rec) if rest else None
                return nv

            def do_av(s, q0, sk, hh):
                sp0p = max(q0, sk * 128)
                wp = q0 + 512 - sk * 128 if sk >= 4 * s else 512
                pb_i = 3 * hh + sk % 3
                avb = 4 + 2 * (s % 2)
                av_hist[hh][sk] = r.emit(
                    "tensor", lambda e, hh=hh, sk=sk, sp0p=sp0p, wp=wp, q0=q0,
                    pb_i=pb_i, avb=avb: e.matmul(
                        PS[0:65, (avb + hh) * 512 + sp0p - q0:
                           (avb + hh) * 512 + sp0p - q0 + wp],
                        A[0:128, O_V + sk * 132 + hh * 66:
                          O_V + sk * 132 + hh * 66 + 65],
                        A[:, O_PB + pb_i * 512:O_PB + pb_i * 512 + wp],
                        start=(sk == 0), stop=(sk == 4 * s + 3)), inc="TT")

            for s in range(NSEG):
                q0 = s * 512
                KT = 4 * (s + 1)
                for kt in range(KT):
                    ktb = kt * 128
                    diag = kt >= 4 * s
                    span0 = max(q0, ktb)
                    w = q0 + 512 - ktb if diag else 512
                    j0 = span0 - ktb
                    if kt in (5, 6):
                        flush_norm(one=True)
                    r.wait("tensor", "VV", max(stt_vv.get((0, kt % 2), 0),
                                               stt_vv.get((1, kt % 2), 0)))
                    if kt >= 2:
                        r.wait("tensor", "SS", max(exp_ss.get((0, kt - 2), 0),
                                                   exp_ss.get((1, kt - 2), 0)))
                        if kt == 2:
                            avb = 4 + 2 * (s % 2)
                            r.wait("tensor", "VV", max(evac_done[s % 2],
                                                       bc_last[avb],
                                                       bc_last[avb + 1]))
                    qk_tt = None
                    for hh in range(2):
                        qk_bk = (kt % 2) + 2 * hh
                        rlo = 64 * hh
                        qk_tt = r.emit("tensor", lambda e, qk_bk=qk_bk, w=w,
                                       rlo=rlo, ktb=ktb, span0=span0: e.matmul(
                            BK(qk_bk, 128, w),
                            A[rlo:rlo + 64, O_K + ktb:O_K + ktb + 128],
                            A[rlo:rlo + 64, O_Q + span0:O_Q + span0 + w],
                            start=True, stop=True), inc="TT")
                    if kt >= 2:
                        for hh in range(2):
                            do_av(s, q0, kt - 2, hh)
                    for hh in range(2):
                        qk_bk = (kt % 2) + 2 * hh
                        pb_i = 3 * hh + kt % 3
                        r.wait("vector", "TT", qk_tt)
                        if kt >= 3:
                            r.wait("vector", "TT", av_hist[hh].get(kt - 3, 0))
                        stt_vv[(hh, kt % 2)] = r.emit(
                            "vector", lambda e, hh=hh, j0=j0, w=w, qk_bk=qk_bk,
                            pb_i=pb_i: e.tensor_tensor(
                                A[:, O_PB + pb_i * 512:O_PB + pb_i * 512 + w],
                                A[:, O_TSC + hh * TQ + j0:O_TSC + hh * TQ + j0 + w],
                                BK(qk_bk, 128, w), op=AT.add), inc="VV")
                        if diag:
                            stt_vv[(hh, kt % 2)] = r.emit(
                                "vector", lambda e, pb_i=pb_i: e.tensor_tensor(
                                    A[:, O_PB + pb_i * 512:
                                      O_PB + pb_i * 512 + 128],
                                    A[:, O_PB + pb_i * 512:
                                      O_PB + pb_i * 512 + 128],
                                    A[:, SC:SC + 128], op=AT.add), inc="VV")
                        r.wait("scalar", "VV", stt_vv[(hh, kt % 2)])
                        exp_ss[(hh, kt)] = r.emit(
                            "scalar", lambda e, w=w, pb_i=pb_i, hh=hh: e.activation(
                                A[:, O_PB + pb_i * 512:O_PB + pb_i * 512 + w],
                                A[:, O_PB + pb_i * 512:O_PB + pb_i * 512 + w],
                                AF.Exp, bias=t_st[:, 6 + hh:7 + hh]), inc="SS")
                    if pending_evac[0] is not None and kt in (0, 1):
                        emit_evac(part=kt)
                    if pending_norm[0] is not None and kt in (2, 3) \
                            and pending_evac[0] is None:
                        emit_rec(half=kt - 2)
                r.wait("tensor", "SS", max(exp_ss.get((0, KT - 2), 0),
                                           exp_ss.get((1, KT - 2), 0)))
                for hh in range(2):
                    do_av(s, q0, KT - 2, hh)
                r.wait("tensor", "SS", max(exp_ss.get((0, KT - 1), 0),
                                           exp_ss.get((1, KT - 1), 0)))
                for hh in range(2):
                    do_av(s, q0, KT - 1, hh)
                pending_evac[0] = (s, KT)
                pending_norm[0] = (s, [0, 1], None)
            emit_evac()
            attn_done_vv = flush_norm()
            last_exp_ss = r.cnt["SS"]

        r.barrier()

        def dump(dst, slot, nchunk, width):
            sf = 0
            for ch in range(nchunk):
                sf = r.emit("gpsimd", lambda e, ch=ch, dst=dst, slot=slot,
                            width=width: e.dma_start(
                    out=dst[ch * 128:ch * 128 + 128, :],
                    in_=A[:, slot + ch * width:slot + ch * width + width]),
                    inc="SO", by=16)
            r.wait("gpsimd", "SO", sf)

        if debug:
            dump(g_attn, SA, 4, TQ)
        r.wait("tensor", "S3", ldwpj)
        ttf = None
        for cs in range(8):
            for jc in range(4):
                ttf = r.emit("tensor", lambda e, cs=cs, jc=jc: e.matmul(
                    PS[0:128, 1024 + cs:1025 + cs],
                    A[:, O_WPJ + jc * 1024 + cs * 128:O_WPJ + jc * 1024 + cs * 128 + 128],
                    t_bc[:, 8 + jc:9 + jc],
                    start=(cs == 0 and jc == 0), stop=(cs == 7 and jc == 3)),
                    inc="TT")
        r.wait("vector", "TT", ttf)
        vvf = r.emit("vector", lambda e: e.tensor_copy(
            t_wf[:, 0:8], PS[0:128, 1024:1032]), inc="VV")
        last_pt_vv = {0: 0, 1: 0}
        st_hist = {}
        xt_use = {}
        st_so = 0
        for idx in range(64):
            cs, s8 = divmod(idx, NSEG)
            ppb = idx % 2
            xsl = A[:, O_XT + ppb * 512:O_XT + ppb * 512 + 512]
            if idx >= 2:
                r.wait("sync", "VV", xt_use[idx - 2])
            ldxt = r.emit("sync", lambda e, cs=cs, s8=s8, xsl=xsl: e.dma_start(
                out=xsl, in_=d_xT[cs * 128:cs * 128 + 128,
                                  s8 * 512:s8 * 512 + 512]),
                inc="S%d" % (4 + ppb), by=16)
            r.wait("tensor", "VV", vvf)
            r.wait("tensor", "VV", last_pt_vv[ppb])
            ttp = None
            for jc in range(4):
                ttp = r.emit("tensor", lambda e, cs=cs, s8=s8, jc=jc, ppb=ppb:
                             e.matmul(
                    BK(ppb),
                    A[:, O_WPJ + jc * 1024 + cs * 128:O_WPJ + jc * 1024 + cs * 128 + 128],
                    A[:, SA + jc * TQ + s8 * 512:SA + jc * TQ + s8 * 512 + 512],
                    start=(jc == 0), stop=(jc == 3)), inc="TT")
            r.wait("vector", "TT", ttp)
            r.wait("vector", "S%d" % (4 + ppb), ldxt)
            if idx >= 2:
                r.wait("vector", "S%d" % (idx % 2), st_hist.get(idx - 2, 0))
            pt = A[:, O_PT + ppb * 512:O_PT + ppb * 512 + 512]
            xt_use[idx] = r.emit("vector", lambda e, ppb=ppb, pt=pt, xsl=xsl:
                                 e.scalar_tensor_tensor(
                pt, xsl, 0.5, BK(ppb), op0=AT.mult, op1=AT.add), inc="VV")
            last_pt_vv[ppb] = r.emit("vector", lambda e, cs=cs, pt=pt:
                                     e.tensor_scalar(
                pt, pt, t_wf[:, cs:cs + 1], None, op0=AT.add), inc="VV")
            r.wait("gpsimd", "VV", last_pt_vv[ppb])
            sh, s4 = divmod(s8, 4)
            st_so = r.emit("gpsimd", lambda e, sh=sh, cs=cs, s4=s4, pt=pt:
                           e.dma_start(
                out=h_cin[sh * 1024 + cs * 128:sh * 1024 + cs * 128 + 128,
                          s4 * 512:s4 * 512 + 512], in_=pt),
                inc="S%d" % (idx % 2), by=16)
            st_hist[idx] = st_so
        r.wait("gpsimd", "S0", st_hist[62])
        r.wait("gpsimd", "S1", st_hist[63])
        gg_cc = r.emit("gpsimd", lambda e: e.collective_compute(
            "ReduceScatter", AT.add, replica_groups=[[0, 1], [2, 3], [4, 5], [6, 7]],
            ins=[h_cin.ap().opt()], outs=[h_cout.ap().opt()]), inc="GG")

        ttp_last = r.cnt["TT"]
        r.barrier()

        r.wait("sync", "GG", gg_cc)
        evv_hist = {}
        sq_hist_e = {}
        sq_seq = []

        def e_chunk(cc):
            lb = A[:, O_LD + (cc % 2) * 2048:O_LD + (cc % 2) * 2048 + 2048]
            if cc >= 2:
                r.wait("sync", "VV", evv_hist[cc - 2])
            ldc = r.emit("sync", lambda e, cc=cc, lb=lb: e.dma_start(
                out=lb, in_=h_cout[cc * 128:cc * 128 + 128, :]),
                inc="S%d" % (cc % 2), by=16)
            r.wait("vector", "S%d" % (cc % 2), ldc)
            evv_hist[cc] = r.emit("vector", lambda e, cc=cc, lb=lb:
                                  e.tensor_scalar(
                A[:, SH + cc * 2048:SH + cc * 2048 + 2048], lb,
                t_bc[:, 12 + cc:13 + cc], None, op0=AT.add), inc="VV")
            for s4 in range(4):
                sqb = A[:, SA + 10240 + (len(sq_seq) % 2) * 512:
                        SA + 10240 + (len(sq_seq) % 2) * 512 + 512]
                if len(sq_seq) >= 2:
                    r.wait("scalar", "TT", sq_seq[-2])
                r.wait("scalar", "VV", evv_hist[cc])
                ss = r.emit("scalar", lambda e, s4=s4, cc=cc, sqb=sqb:
                            e.activation(
                                sqb,
                                A[:, SH + cc * 2048 + s4 * 512:
                                  SH + cc * 2048 + s4 * 512 + 512],
                                AF.Square), inc="SS")
                r.wait("tensor", "SS", ss)
                r.emit("tensor", lambda e, s4=s4, cc=cc: e.matmul(
                    PS[0:1, s4 * 512:s4 * 512 + 512], t_ones[:, 0:1],
                    A[:, SH + cc * 2048 + s4 * 512:
                      SH + cc * 2048 + s4 * 512 + 512],
                    start=(cc == 0), stop=(cc == 7)), inc="TT")
                v = r.emit("tensor", lambda e, s4=s4, sqb=sqb: e.matmul(
                    PS[0:1, 2048 + s4 * 512:2048 + s4 * 512 + 512],
                    t_ones[:, 0:1], sqb,
                    start=(cc == 0), stop=(cc == 7)), inc="TT")
                sq_seq.append(v)
                sq_hist_e[s4 * 8 + cc] = v

        r.wait("gpsimd", "TT", ttp_last)
        ld_pref = None
        for e4 in range(4):
            for cc in range(8):
                ld_pref = r.emit("gpsimd", lambda e, e4=e4, cc=cc: e.dma_start(
                    out=A[:, O_KEY + (e4 * 8 + cc) * 128:
                          O_KEY + (e4 * 8 + cc) * 128 + 128],
                    in_=d_keys[e4 * 1024 + cc * 128:e4 * 1024 + cc * 128 + 128,
                               :]), inc="PF", by=16)
        for e4 in range(4):
            ld_pref = r.emit("gpsimd", lambda e, e4=e4: e.dma_start(
                out=A[:, O_VAL + e4 * 1024:O_VAL + e4 * 1024 + 1024],
                in_=d_vals[e4 * 128:e4 * 128 + 128, :]), inc="PF", by=16)
        for cc in range(8):
            e_chunk(cc)

        r.barrier()

        def layernorm(src, dst, after_apply=None, pre_sq=None):
            sq_hist = {} if pre_sq is None else pre_sq
            out_vv = None

            def stats(s4):
                for cc in range(8):
                    it = s4 * 8 + cc
                    sqb = A[:, O_SQ + (cc % 2) * 512:O_SQ + (cc % 2) * 512 + 512]
                    if it >= 2:
                        r.wait("scalar", "TT", sq_hist[it - 2])
                    ss = r.emit("scalar", lambda e, s4=s4, cc=cc, sqb=sqb,
                                src=src: e.activation(
                                    sqb,
                                    A[:, src + cc * 2048 + s4 * 512:
                                      src + cc * 2048 + s4 * 512 + 512],
                                    AF.Square), inc="SS")
                    r.wait("tensor", "SS", ss)
                    r.emit("tensor", lambda e, s4=s4, cc=cc, src=src: e.matmul(
                        PS[0:1, s4 * 512:s4 * 512 + 512], t_ones[:, 0:1],
                        A[:, src + cc * 2048 + s4 * 512:
                          src + cc * 2048 + s4 * 512 + 512],
                        start=(cc == 0), stop=(cc == 7)), inc="TT")
                    sq_hist[it] = r.emit("tensor", lambda e, s4=s4, cc=cc,
                                         sqb=sqb: e.matmul(
                        PS[0:1, 2048 + s4 * 512:2048 + s4 * 512 + 512],
                        t_ones[:, 0:1], sqb,
                        start=(cc == 0), stop=(cc == 7)), inc="TT")

            def chain_apply(s4):
                nonlocal out_vv
                R0 = lambda o: A[0:1, o + s4 * 512:o + s4 * 512 + 512]
                r.wait("vector", "TT", sq_hist[s4 * 8 + 7])
                r.emit("vector", lambda e, R0=R0, s4=s4: e.tensor_scalar(
                    R0(ST_M), PS[0:1, s4 * 512:s4 * 512 + 512], 1.0 / C, None,
                    op0=AT.mult), inc="VV")
                r.emit("vector", lambda e, R0=R0, s4=s4: e.tensor_scalar(
                    R0(ST_V), PS[0:1, 2048 + s4 * 512:2048 + s4 * 512 + 512],
                    1.0 / C, None, op0=AT.mult), inc="VV")
                r.emit("vector", lambda e, R0=R0: e.tensor_tensor(
                    R0(ST_T), R0(ST_M), R0(ST_M), op=AT.mult), inc="VV")
                r.emit("vector", lambda e, R0=R0: e.tensor_tensor(
                    R0(ST_V), R0(ST_V), R0(ST_T), op=AT.subtract), inc="VV")
                vvv = r.emit("vector", lambda e, R0=R0: e.tensor_scalar(
                    R0(ST_V), R0(ST_V), EPS, None, op0=AT.add), inc="VV")
                r.wait("scalar", "VV", vvv)
                ssv = r.emit("scalar", lambda e, R0=R0: e.activation(
                    R0(ST_T), R0(ST_V), AF.Sqrt), inc="SS")
                r.wait("vector", "SS", ssv)
                r.emit("vector", lambda e, R0=R0: e.reciprocal(
                    R0(ST_R), R0(ST_T)), inc="VV")
                for _ in range(2):
                    r.emit("vector", lambda e, R0=R0: e.tensor_tensor(
                        R0(ST_T), R0(ST_R), R0(ST_R), op=AT.mult), inc="VV")
                    r.emit("vector", lambda e, R0=R0: e.tensor_tensor(
                        R0(ST_T), R0(ST_T), R0(ST_V), op=AT.mult), inc="VV")
                    r.emit("vector", lambda e, R0=R0: e.tensor_scalar(
                        R0(ST_T), R0(ST_T), -0.5, 1.5, op0=AT.mult, op1=AT.add),
                        inc="VV")
                    r.emit("vector", lambda e, R0=R0: e.tensor_tensor(
                        R0(ST_R), R0(ST_R), R0(ST_T), op=AT.mult), inc="VV")
                r.emit("vector", lambda e, R0=R0: e.tensor_tensor(
                    R0(ST_B), R0(ST_M), R0(ST_R), op=AT.mult), inc="VV")
                vvb = r.emit("vector", lambda e, R0=R0: e.tensor_scalar(
                    R0(ST_B), R0(ST_B), -1.0, None, op0=AT.mult), inc="VV")
                r.wait("tensor", "VV", vvb)
                tta = r.emit("tensor", lambda e, s4=s4: e.matmul(
                    PS[0:128, s4 * 512:s4 * 512 + 512], t_ones[0:1, 0:128],
                    A[0:1, ST_R + s4 * 512:ST_R + s4 * 512 + 512],
                    start=True, stop=True), inc="TT")
                ttb2 = r.emit("tensor", lambda e, s4=s4: e.matmul(
                    PS[0:128, 2048 + s4 * 512:2048 + s4 * 512 + 512],
                    t_ones[0:1, 0:128],
                    A[0:1, ST_B + s4 * 512:ST_B + s4 * 512 + 512],
                    start=True, stop=True), inc="TT")
                r.wait("vector", "TT", tta)
                vva = r.emit("vector", lambda e, s4=s4: e.tensor_copy(
                    A[:, SC + s4 * 512:SC + s4 * 512 + 512],
                    PS[0:128, s4 * 512:s4 * 512 + 512]), inc="VV")
                r.wait("vector", "TT", ttb2)
                gp_m = {}
                r.wait("gpsimd", "VV", vva)
                for cc in range(4, 8):
                    gp_m[cc] = r.emit(
                        "gpsimd", lambda e, cc=cc, s4=s4, src=src, dst=dst:
                        e.tensor_tensor(
                            A[:, dst + cc * 2048 + s4 * 512:
                              dst + cc * 2048 + s4 * 512 + 512],
                            A[:, src + cc * 2048 + s4 * 512:
                              src + cc * 2048 + s4 * 512 + 512],
                            A[:, SC + s4 * 512:SC + s4 * 512 + 512],
                            op=AT.mult), inc="GG")
                for cc in range(8):
                    if cc < 4:
                        r.emit("vector", lambda e, cc=cc, s4=s4, src=src,
                               dst=dst: e.tensor_tensor(
                            A[:, dst + cc * 2048 + s4 * 512:
                              dst + cc * 2048 + s4 * 512 + 512],
                            A[:, src + cc * 2048 + s4 * 512:
                              src + cc * 2048 + s4 * 512 + 512],
                            A[:, SC + s4 * 512:SC + s4 * 512 + 512],
                            op=AT.mult), inc="VV")
                    else:
                        r.wait("vector", "GG", gp_m[cc])
                    out_vv = r.emit("vector", lambda e, cc=cc, s4=s4, dst=dst:
                                    e.tensor_tensor(
                        A[:, dst + cc * 2048 + s4 * 512:
                          dst + cc * 2048 + s4 * 512 + 512],
                        A[:, dst + cc * 2048 + s4 * 512:
                          dst + cc * 2048 + s4 * 512 + 512],
                        PS[0:128, 2048 + s4 * 512:2048 + s4 * 512 + 512],
                        op=AT.add), inc="VV")

            if pre_sq is None:
                stats(0)
            for s4 in range(4):
                if pre_sq is None and s4 < 3:
                    stats(s4 + 1)
                chain_apply(s4)
                if after_apply is not None:
                    after_apply(s4, out_vv)
            return out_vv

        if debug:
            dump(g_h1, SH, 8, OWN)
        g_tt = [0]

        def ln1_after(s4, vv):
            r.wait("tensor", "VV", vv)
            for cc in range(8):
                g_tt[0] = r.emit("tensor", lambda e, s4=s4, cc=cc: e.matmul(
                    PS[0:16, s4 * 512:s4 * 512 + 512],
                    t_selT[:, cc * 16:cc * 16 + 16],
                    A[:, SA + cc * 2048 + s4 * 512:SA + cc * 2048 + s4 * 512 + 512],
                    start=(cc == 0), stop=(cc == 7)), inc="TT")

        layernorm(SH, SA, after_apply=ln1_after,
                  pre_sq=sq_hist_e)
        r.barrier()
        if debug:
            dump(g_x2, SA, 8, OWN)

        r.wait("scalar", "TT", g_tt[0])
        ss_g = r.emit("scalar", lambda e: e.activation(
            A[0:16, O_G:O_G + 2048], PS[0:16, 0:2048], AF.Sigmoid), inc="SS")
        r.wait("vector", "SS", ss_g)
        r.wait("sync", "SS", ss_g)
        ld_oh = r.emit("sync", lambda e: e.dma_start(
            out=A[0:16, O_OH:O_OH + 2048], in_=d_oh[:, :]), inc="LD", by=16)

        hb_last = {0: 0, 1: 0}
        fb_last = {4: 0, 5: 0}
        g2_last = 0
        q_tt = 0
        r.wait("tensor", "SS", ss_g)
        hid_done_tt = 0
        ldk = 0
        for q4 in range(4):
            if q4 == 0:
                r.wait("tensor", "PF", ld_pref)
            if q4 > 0:
                r.wait("sync", "TT", q_tt)
                for e4 in range(4):
                    eg = 4 * q4 + e4
                    ldk = r.emit("sync", lambda e, eg=eg, e4=e4: e.dma_start(
                        out=A[:, O_VAL + e4 * 1024:O_VAL + e4 * 1024 + 1024],
                        in_=d_vals[eg * 128:eg * 128 + 128, :]), inc="LD", by=16)
            for s4 in range(4):
                for e4 in range(4):
                    eg = 4 * q4 + e4
                    hb = e4 % 2
                    r.wait("tensor", "LD", ldk)
                    r.wait("tensor", "VV", hb_last[hb])
                    tth = None
                    for cc in range(8):
                        tth = r.emit("tensor", lambda e, e4=e4, cc=cc, s4=s4, hb=hb:
                                     e.matmul(
                            BK(hb), A[:, O_KEY + (e4 * 8 + cc) * 128:
                                      O_KEY + (e4 * 8 + cc) * 128 + 128],
                            A[:, SA + cc * 2048 + s4 * 512:
                              SA + cc * 2048 + s4 * 512 + 512],
                            start=(cc == 0), stop=(cc == 7)), inc="TT")
                    hid_done_tt = tth
                    if s4 == 3 and q4 < 3:
                        eg2 = 4 * (q4 + 1) + e4
                        r.wait("sync", "TT", tth)
                        for cc in range(8):
                            ldk = r.emit("sync", lambda e, eg2=eg2, e4=e4,
                                         cc=cc: e.dma_start(
                                out=A[:, O_KEY + (e4 * 8 + cc) * 128:
                                      O_KEY + (e4 * 8 + cc) * 128 + 128],
                                in_=d_keys[eg2 * 1024 + cc * 128:
                                           eg2 * 1024 + cc * 128 + 128,
                                           :]), inc="LD", by=16)
                    r.wait("tensor", "LD", ld_oh)
                    r.wait("tensor", "VV", g2_last)
                    ttg = r.emit("tensor", lambda e, eg=eg, s4=s4: e.matmul(
                        BK(2), A[0:16, O_OH + eg * 128:O_OH + eg * 128 + 128],
                        A[0:16, O_G + s4 * 512:O_G + s4 * 512 + 512],
                        start=True, stop=True), inc="TT")
                    r.wait("vector", "TT", ttg)
                    g2_last = r.emit("vector", lambda e: e.tensor_copy(
                        A[:, O_GES:O_GES + 512], BK(2)), inc="VV")
                    r.wait("vector", "TT", tth)
                    ht_vv = r.emit("vector", lambda e, e4=e4, hb=hb:
                                   e.scalar_tensor_tensor(
                        A[:, O_HT + e4 * 512:O_HT + e4 * 512 + 512], BK(hb),
                        0.0, A[:, O_GES:O_GES + 512],
                        op0=AT.max, op1=AT.mult), inc="VV")
                    hb_last[hb] = ht_vv
                for cs in range(8):
                    fb = 4 + cs % 2
                    r.wait("tensor", "VV", ht_vv)
                    r.wait("tensor", "VV", fb_last[fb])
                    ttv = None
                    for e4 in range(4):
                        ttv = r.emit("tensor", lambda e, e4=e4, cs=cs, fb=fb:
                                     e.matmul(
                            BK(fb), A[:, O_VAL + e4 * 1024 + cs * 128:
                                      O_VAL + e4 * 1024 + cs * 128 + 128],
                            A[:, O_HT + e4 * 512:O_HT + e4 * 512 + 512],
                            start=(e4 == 0), stop=(e4 == 3)), inc="TT")
                    q_tt = ttv
                    r.wait("vector", "TT", ttv)
                    src_slot = SA if q4 == 0 else SH
                    fb_last[fb] = r.emit(
                        "vector", lambda e, cs=cs, s4=s4, fb=fb, ss=src_slot:
                        e.tensor_tensor(
                            A[:, SH + cs * 2048 + s4 * 512:
                              SH + cs * 2048 + s4 * 512 + 512],
                            BK(fb),
                            A[:, ss + cs * 2048 + s4 * 512:
                              ss + cs * 2048 + s4 * 512 + 512],
                            op=AT.add), inc="VV")

        r.barrier()
        if debug:
            dump(g_f, SH, 8, OWN)

        tr_vv = {i: 0 for i in range(8)}
        gst_hist = {}

        def ln2_after(s4, vv):
            r.wait("tensor", "VV", vv)
            for t16 in range(4 * s4, 4 * s4 + 4):
                ob = SB + (t16 % 2) * 1024
                if t16 >= 2:
                    r.wait("vector", "S%d" % (t16 % 2), gst_hist[t16 - 2])
                for cc in range(8):
                    trb = s4 if (t16 * 8 + cc) % 2 == 0 else 4 + s4
                    r.wait("tensor", "VV", tr_vv[trb])
                    ttt = r.emit("tensor", lambda e, t16=t16, cc=cc, trb=trb:
                                 e.transpose(
                        BK(trb, 128, 128),
                        A[:, SA + cc * 2048 + t16 * 128:
                          SA + cc * 2048 + t16 * 128 + 128],
                        t_ident[:, :]), inc="TT")
                    r.wait("vector", "TT", ttt)
                    tr_vv[trb] = r.emit("vector", lambda e, ob=ob, cc=cc,
                                        trb=trb: e.tensor_copy(
                        A[:, ob + cc * 128:ob + cc * 128 + 128],
                        BK(trb, 128, 128)), inc="VV")
                r.wait("gpsimd", "VV", tr_vv[s4])
                r.wait("gpsimd", "VV", tr_vv[4 + s4])
                gst_hist[t16] = r.emit("gpsimd", lambda e, t16=t16, ob=ob:
                                       e.dma_start(
                    out=d_out[t16 * 128:t16 * 128 + 128, :],
                    in_=A[:, ob:ob + 1024]),
                    inc="S%d" % (t16 % 2), by=16)

        layernorm(SH, SA, after_apply=ln2_after)
        r.wait("gpsimd", "S0", gst_hist[14])
        r.wait("gpsimd", "S1", gst_hist[15])

        with nc.Block() as block:
            block.sync(lambda e: [f(e) for f in r.ops["sync"]])
            block.tensor(lambda e: [f(e) for f in r.ops["tensor"]])
            block.vector(lambda e: [f(e) for f in r.ops["vector"]])
            block.scalar(lambda e: [f(e) for f in r.ops["scalar"]])
            block.gpsimd(lambda e: [f(e) for f in r.ops["gpsimd"]])
    return nc


_NC_CACHE = None


def kernel(L0_attn_x, L0_attn_W_attn, L0_attn_b_attn, L0_attn_W_proj,
           L0_attn_b_proj, L0_ffn_expert_sel, L0_ffn_keys, L0_ffn_values):
    global _NC_CACHE
    x = np.asarray(L0_attn_x, np.float32)
    W_attn = np.asarray(L0_attn_W_attn, np.float32)
    b_attn = np.asarray(L0_attn_b_attn, np.float32)
    W_proj = np.asarray(L0_attn_W_proj, np.float32)
    b_proj = np.asarray(L0_attn_b_proj, np.float32)
    sel = np.asarray(L0_ffn_expert_sel, np.float32)
    keys = np.asarray(L0_ffn_keys, np.float32)
    values = np.asarray(L0_ffn_values, np.float32)

    ident = np.eye(128, dtype=np.float32)
    ones = np.ones((128, 128), np.float32)
    bones = np.zeros((128, 128), np.float32)
    for j in range(128):
        if j % 2 == 0:
            bones[0:64, j] = 1.0
        else:
            bones[64:128, j] = 1.0
    tj = np.broadcast_to(np.arange(TQ, dtype=np.float32), (128, TQ)).copy()
    selT = np.ascontiguousarray(sel.T)
    keysR = np.ascontiguousarray(keys.reshape(16 * C, 128))
    valuesR = np.ascontiguousarray(values.reshape(16 * 128, C))
    parts = np.arange(128, dtype=np.float32)
    oh16 = np.kron(np.eye(16, dtype=np.float32), np.ones((1, 128), np.float32))

    in_maps = []
    for c in range(8):
        b, hg = divmod(c, 2)
        heads = np.arange(hg * 8, hg * 8 + 8)
        slopes = (0.5 ** (heads + 1)).astype(np.float32)
        xT = np.ascontiguousarray(x[b].T)
        xq = np.ascontiguousarray(xT[:, hg * OWN:(hg + 1) * OWN])
        wqkv = np.ascontiguousarray(np.concatenate(
            [W_attn[:, 512 * hg:512 * hg + 512],
             W_attn[:, 1024 + 512 * hg:1024 + 512 * hg + 512],
             W_attn[:, 2048 + 512 * hg:2048 + 512 * hg + 512]], axis=1))
        wprojmy = np.ascontiguousarray(W_proj[512 * hg:512 * hg + 512, :])
        bcols = np.zeros((128, 20), np.float32)
        for p in range(4):
            bcols[:, p] = b_attn[512 * hg + 128 * p:512 * hg + 128 * p + 128]
            bcols[:, 4 + p] = b_attn[1024 + 512 * hg + 128 * p:
                                     1024 + 512 * hg + 128 * p + 128]
        bv_my = b_attn[2048 + 512 * hg:2048 + 512 * hg + 512]
        for jc in range(4):
            bcols[:, 8 + jc] = bv_my[jc * 128:jc * 128 + 128]
        for cc in range(8):
            bcols[:, 12 + cc] = b_proj[cc * 128:cc * 128 + 128]
        negslope = np.broadcast_to(-slopes[None, :], (128, 8)).copy()
        biasbase = (parts[:, None] * slopes[None, :]).astype(np.float32)
        in_maps.append(dict(
            xT=xT, xq=xq, wqkv=wqkv, wprojmy=wprojmy, bcols=bcols,
            negslope=negslope, biasbase=biasbase, selT=selT, keysR=keysR,
            valuesR=valuesR, tj=tj, identc=ident, onesc=ones, blockonesc=bones,
            oh16=oh16))

    if _NC_CACHE is None:
        _NC_CACHE = build_nc()
    res = run_bass_kernel_spmd(_NC_CACHE, in_maps, core_ids=list(range(8)))
    y = np.empty((4, TQ, C), np.float32)
    for c in range(8):
        b, hg = divmod(c, 2)
        y[b, hg * OWN:(hg + 1) * OWN, :] = res.results[c]["out"]
    return y
